# version 02
# baseline (speedup 1.0000x reference)
import numpy as np
import ml_dtypes
import concourse.bass as bass
import concourse.mybir as mybir
from concourse.ap import AP
from concourse.bass_utils import run_bass_kernel_spmd

F32 = mybir.dt.float32
BF16 = mybir.dt.bfloat16
AF = mybir.ActivationFunctionType
ALU = mybir.AluOpType
AX = mybir.AxisListType

NCORES = 8
NB = 4
T = 2048
D = 1024
NT = 16
NDC = 8
EPS = 1e-5
BF = ml_dtypes.bfloat16


class Sched:
    ENGINES = ("T", "V", "S", "G", "SY")
    QUEUES = ("SY", "G")
    DMA_DEPTH = 8

    def __init__(self):
        self.count = {e: 0 for e in self.ENGINES}
        self.ops = {e: [] for e in self.ENGINES}
        self.last_write = {}
        self.readers = {}
        self.sem_names = list(self.ENGINES) + [
            f"{q}{i}" for q in self.QUEUES for i in range(self.DMA_DEPTH)]
        self.waited = {e: {} for e in self.ENGINES}
        self.dma_idx = {q: 0 for q in self.QUEUES}
        self.dma_total = {s: 0 for s in self.sem_names}
        self._sems = None

    def _deps(self, eng, reads, writes):
        need = {}
        for k in reads:
            w = self.last_write.get(k)
            if w is not None:
                need[w[0]] = max(need.get(w[0], 0), w[1])
        for k in writes:
            w = self.last_write.get(k)
            if w is not None:
                need[w[0]] = max(need.get(w[0], 0), w[1])
            for (re, rv) in self.readers.get(k, ()):
                need[re] = max(need.get(re, 0), rv)
        waits = []
        for pe, val in sorted(need.items()):
            if pe == eng and pe == "T":
                continue
            if self.waited[eng].get(pe, 0) >= val:
                continue
            waits.append((pe, val))
            self.waited[eng][pe] = val
        return waits

    def op(self, eng, reads, writes, fn, n_inc=1):
        waits = self._deps(eng, reads, writes)
        after = self.count[eng] + n_inc
        self.ops[eng].append((waits, fn, n_inc, eng))
        self.count[eng] = after
        for k in writes:
            self.last_write[k] = (eng, after)
            self.readers[k] = []
        for k in reads:
            self.readers.setdefault(k, []).append((eng, after))

    def dma_op(self, q, reads, writes, fn, ndma):
        waits = self._deps(q, reads, writes)
        s = f"{q}{self.dma_idx[q] % self.DMA_DEPTH}"
        self.dma_idx[q] += 1
        if self.dma_total[s] > 0 and self.waited[q].get(s, 0) < self.dma_total[s]:
            waits.append((s, self.dma_total[s]))
            self.waited[q][s] = self.dma_total[s]
        self.dma_total[s] += 16 * ndma
        tok = (s, self.dma_total[s])
        self.ops[q].append((waits, fn, s, q))
        for k in writes:
            self.last_write[k] = tok
            self.readers[k] = []
        for k in reads:
            self.readers.setdefault(k, []).append(tok)

    def alias(self, new_keys, old_keys):
        deps = []
        for k in old_keys:
            w = self.last_write.pop(k, None)
            if w is not None:
                deps.append(w)
            deps.extend(self.readers.pop(k, ()))
        byeng = {}
        for (e, v) in deps:
            byeng[e] = max(byeng.get(e, 0), v)
        for k in new_keys:
            self.readers[k] = [(e, v) for e, v in byeng.items()]
            self.last_write.pop(k, None)

    def emit(self, nc, block):
        sems = {s: nc.alloc_semaphore("sem_" + s) for s in self.sem_names}
        self._sems = sems

        def run_stream(e):
            def body(engine):
                for (waits, fn, extra, eng) in self.ops[e]:
                    for (pe, val) in waits:
                        engine.wait_ge(sems[pe], val)
                    if isinstance(extra, str):
                        fn(engine, sems[extra])
                    else:
                        last = fn(engine)
                        if last is not None:
                            last.then_inc(sems[e], extra)
            return body

        block.tensor(run_stream("T"))
        block.vector(run_stream("V"))
        block.scalar(run_stream("S"))
        block.gpsimd(run_stream("G"))
        block.sync(run_stream("SY"))
        return sems


class Builder:
    def __init__(self, taps=(), nbatches=NB, stages=(1, 1, 1, 1), substop=None):
        self.substop = substop
        self.nbatches = nbatches
        self.stages = stages
        self.taps = set(taps)
        self.nc = bass.Bass("TRN2", target_bir_lowering=False)
        self.sc = Sched()
        self.tap_names = []
        self.war_keys = [("WARinit",)]
        self.ars_keys = [("ARSinit",)]

    def op(self, eng, reads, writes, fn, n_inc=1):
        self.sc.op(eng, reads, writes, fn, n_inc)

    def dma(self, q, reads, writes, pairs, slow=False):
        def fn(e, sem):
            for (o, i) in pairs:
                if slow:
                    e.dma_start(out=o, in_=i,
                                allow_slow_non_contiguous=True).then_inc(sem, 16)
                else:
                    e.dma_start(out=o, in_=i).then_inc(sem, 16)
        self.sc.dma_op(q, reads, writes, fn, len(pairs))

    def mm(self, reads, writes, groups, accum=False, stop=True):
        def fn(e):
            last = None
            for (ps, mms) in groups:
                n = len(mms)
                for i, (l, r) in enumerate(mms):
                    last = e.matmul(ps, l, r,
                                    start=(i == 0 and not accum),
                                    stop=(i == n - 1 and stop))
            return last
        self.sc.op("T", reads, writes, fn)

    def use_war(self, new_keys):
        self.sc.alias(new_keys, self.war_keys)
        self.war_keys = list(new_keys)

    def use_ars(self, new_keys):
        self.sc.alias(new_keys, self.ars_keys)
        self.ars_keys = list(new_keys)

    def tap(self, label, ap, reads, shape, dtype=F32):
        if label not in self.taps:
            return
        out = self.nc.dram_tensor("tap_" + label, list(shape), dtype,
                                  kind="ExternalOutput").ap()
        self.tap_names.append("tap_" + label)
        self.dma("SY", reads, [], [(out, ap)])

    def build(self):
        nc = self.nc
        d = {}

        def din(name, shape, dt_=F32):
            d[name] = nc.dram_tensor(name, list(shape), dt_,
                                     kind="ExternalInput").ap()
        din("x", (NB, T, D))
        for nm, sh in [
            ("WQ", (D, D)), ("WK", (D, D)), ("WV", (D, D)),
            ("WO", (D, D)), ("Wg", (D, D)),
            ("bQT", (128, 8)), ("bKT", (128, 8)),
            ("bV", (1, D)), ("bO", (1, D)), ("bg", (1, D)),
            ("A", (64, 1)), ("Bw", (D, 64)), ("Cw", (64, D)),
            ("W1", (D, 4 * D)), ("b1", (1, 4 * D)),
            ("W2", (4 * D, D)), ("b2", (1, D)),
            ("W_qkv", (D, 3 * D)), ("W_out", (D, D)),
            ("agate_w", (D, 8)), ("q_w", (8, D, 128)), ("qbT", (128, 8)),
            ("k_w", (D, 128)), ("k_b", (128, 1)),
            ("v_w", (D, 128)), ("v_b", (1, 128)),
            ("o_w", (8, 128, D)), ("o_b", (8, D)),
            ("relT", (128, 129)),
            ("fgate_w", (D, 16)), ("ff_w1", (16, D, 512)), ("ff_b1", (16, 512)),
            ("ff_w2", (16, 512, D)), ("ff_b2", (16, D)),
            ("LNPa", (128, 64)),
        ]:
            din(nm, sh)
        for nm, sh, dt_ in [
            ("IDN", (128, 128), BF16), ("ONESR", (1, 128), BF16),
            ("MASKA", (128, 256), F32), ("MASKB", (128, 512), F32),
            ("IOT8", (128, 128), F32), ("IOT16", (128, 256), F32),
            ("SEGC", (128, 512), BF16), ("LTX", (32, 32), BF16),
            ("ONH", (1, 256), BF16),
        ]:
            din(nm, sh, dt_)

        self.y = nc.dram_tensor("y", [NB, T, D], F32, kind="ExternalOutput").ap()
        self.rext = nc.dram_tensor("rext", [NT * 2, 128, 384], F32,
                                   kind="Internal")
        self.cvd = nc.dram_tensor("cvd", [1, 4096], BF16, kind="Internal")
        self.crsd = nc.dram_tensor("crsd", [1, 4096], BF16, kind="Internal")
        self.d = d

        sb = lambda nm, shape, dt_=F32: nc.alloc_sbuf_tensor(nm, list(shape), dt_)
        self.X = sb("X", (128, NT, D))
        self.XT = sb("XT", (128, NDC, T), BF16)
        self.Bb = sb("Bb", (128, NT, D), BF16)
        self.WARt = sb("WAR", (128, 16384), BF16)
        self.WSL = sb("WSL", (128, 2, 4096), BF16)
        self.SCR2 = sb("SCR2", (128, 1024))
        self.SCB = sb("SCB", (128, 1024), BF16)
        self.ST = sb("ST", (128, 64))
        self.MSQ = sb("MSQ", (128, 16))
        self.SUMS = sb("SUMS", (128, 32))
        self.LNP = sb("LNP", (128, 64))
        self.BQS = sb("BQS", (128, 32))
        self.ACOL = sb("ACOL", (64, 1))
        self.BROW = sb("BROW", (1, 1024), BF16)
        self.IDN = sb("IDNs", (128, 128), BF16)
        self.ONESR = sb("ONESRs", (1, 128), BF16)
        self.ARS = sb("ARS", (128, 5760))

        pp = lambda nm, shape, dt_=F32: nc.alloc_psum_tensor(nm, list(shape), dt_)
        self.pf = [pp(f"pf{i}", (128, 512)) for i in range(6)]
        self.pb = [pp(f"pb{i}", (128, 1024), BF16) for i in range(2)]

        A = self.ARS
        self.U = A[0:64, 0:2048]
        self.USB = A[0:64, 2048:3072].bitcast(BF16)
        self.QTg = A[:, 3072:3136].bitcast(BF16)
        self.KTr = A[:, 3136:3264].bitcast(BF16)
        self.Vr = A[:, 3264:3392].bitcast(BF16).rearrange("p (s n) -> p s n", s=2)
        self.Pa = A[:, 3392:3520].bitcast(BF16)
        self.PTa = A[:, 3520:3648].bitcast(BF16)
        self.SML = A[:, 3648:4160]
        self.SMM = A[:, 4160:4672]
        self.SCB2 = A[:, 3648:4160].bitcast(BF16)
        self.CW = A[0:64, 4672:5184].bitcast(BF16)
        self.BWw = A[:, 5184:5440].bitcast(BF16).rearrange("p (c n) -> p c n", c=8)
        self.MASKA = A[:, 5440:5696]
        self.SKW = A[:, 0:256]
        self.RELS = A[:, 256:388]
        self.REXT = A[:, 388:772]
        self.BCL = [A[:, 388:515], A[:, 772:899]]
        self.BCR = [A[:, 516:644], A[:, 900:1028]]
        self.GL = A[:, 772:1028]
        self.GT = A[:, 1028:1156]
        self.G2T = A[:, 1156:1412]
        self.GTB = A[0:8, 1412:2436].bitcast(BF16)
        self.G2TB = A[0:16, 2436:3460].bitcast(BF16)
        self.RTT = A[:, 3460:3526].bitcast(BF16)
        self.OB = A[0:8, 3526:4038].bitcast(BF16)
        self.FB2 = A[0:16, 4038:4550].bitcast(BF16)
        self.IOT8 = A[:, 4550:4678]
        self.IOT16 = A[:, 4678:4934]
        self.AGW = A[:, 4934:4966].bitcast(BF16).rearrange("p (c n) -> p c n", c=8)
        self.CTXE = A[:, 5094:5158].bitcast(BF16)
        self.CTXE2 = A[:, 5158:5222].bitcast(BF16)
        self.SKW2 = A[:, 5222:5478]
        self.RELS2 = A[:, 5478:5610]
        self.FGW = A[:, 4966:5030].bitcast(BF16).rearrange("p (c n) -> p c n", c=8)
        self.SLb = A[:, 0:2048]
        self.MASKB = A[:, 2048:2560]
        self.SEGC = A[:, 2560:2816].bitcast(BF16)
        self.LTX = A[0:32, 2816:2832].bitcast(BF16)
        self.CSKT = A[:, 2832:2864]
        self.CUMK = A[:, 2864:2900]
        self.CUMKB = A[:, 2900:2918].bitcast(BF16)
        self.CSVB = A[0:32, 2918:2982].bitcast(BF16)
        self.CUMVB = A[0:32, 2982:3046].bitcast(BF16)
        self.CV2 = A[0:2, 3046:4070].bitcast(BF16)
        self.CRS2 = A[0:2, 4070:5094].bitcast(BF16)
        self.SMS = A[:, 5100:5132]
        self.PTT = A[:, 5132:5196].bitcast(BF16)
        self.PTT2 = A[:, 5610:5674].bitcast(BF16)

        W = self.WARt[:]
        self.CTX = W.rearrange("p (h t) -> p h t", h=8)
        self.MIDT = W[:, 0:2048].rearrange("p (j q t) -> p j q t", j=4, q=4)
        self.MIDTf = W[:, 0:2048]
        self.QP = W[:, 0:4096].rearrange("p (s t) -> p s t", s=2)
        self.KT1 = W[:, 4096:6144]
        self.VTok = W[:, 6144:8192].rearrange("p (i n) -> p i n", i=16)
        self.PB = W[:, 8192:10240]
        self.PT = W[:, 10240:12288]
        self.OWp = W[:, 12288:14336].rearrange("p (s n) -> p s n", s=2)
        self.MIDT2 = W[:, 14336:14848].rearrange("p (q t) -> p q t", q=4)
        self.PB2 = W[:, 14336:16384]
        self.MIDT2f = W[:, 14336:14848]
        self.QTb = W[:, 0:2048]
        self.KTb = W[:, 2048:4096]
        self.VGb = W[:, 4096:6144].rearrange("p (i n) -> p i n", i=16)
        self.PLb = W[:, 6144:8192]
        self.OUTBg = W[:, 8192:10240]
        self.PTb = W[:, 10240:11264]
        self.CRSB = self.WARt[0:1, 8192:12288]
        self.CRB = W[:, 15360:16384].bitcast(F32)

        self.prologue()
        for b in range(self.nbatches):
            self.batch(b)

        from contextlib import ExitStack
        with ExitStack() as stack:
            block = stack.enter_context(nc.Block())
            self.sc.emit(nc, block)
        return nc

    def prologue(self):
        d = self.d
        self.dma("SY", [], [("IDN",), ("ONESR",), ("LNP",), ("BQSr",),
                            ("ACOL",)], [
            (self.IDN[:], d["IDN"]), (self.ONESR[:], d["ONESR"]),
            (self.LNP[:], d["LNPa"]),
            (self.BQS[:, 0:8], d["bQT"]), (self.BQS[:, 8:16], d["qbT"]),
            (self.BQS[:, 16:24], d["bKT"]), (self.BQS[:, 24:25], d["k_b"]),
            (self.ACOL[:], d["A"]),
        ])
        self.op("V", [("BQSr",)], [("BQS",)],
                lambda e: e.tensor_scalar_mul(self.BQS[:, 0:8],
                                              self.BQS[:, 0:8], 0.125))
        self.op("V", [("BQSr",)], [("BQS2",)],
                lambda e: e.tensor_scalar_mul(self.BQS[:, 8:16],
                                              self.BQS[:, 8:16],
                                              float(128.0 ** -0.5)))
        self.op("V", [], [("ONEC",), ("SUMS",)],
                lambda e: e.memset(self.SUMS[:, 30:31], 1.0))
        self.op("V", [], [("CONSTC",), ("SUMS",)],
                lambda e: e.memset(self.SUMS[:, 8:9], float(EPS)))
        self.op("V", [], [("CONSTC",), ("SUMS",)],
                lambda e: e.memset(self.SUMS[:, 9:10], 0.125))
        self.op("V", [], [("CONSTC",), ("SUMS",)],
                lambda e: e.memset(self.SUMS[:, 10:11], float(128.0 ** -0.5)))
        self.op("V", [], [("CONSTC",), ("SUMS",)],
                lambda e: e.memset(self.SUMS[:, 11:12], 1.0 / 64.0))

    def batch(self, b):
        d = self.d
        X = self.X
        self.dma("SY", [], [("X", ti) for ti in range(NT)],
                 [(X[:], d["x"][b].rearrange("(t p) d -> p t d", p=128))])
        if self.substop == "load":
            pass
        elif self.stages[0]:
            self.attn_a(b)
        self.tap(f"y1_{b}", X[:], [("X", ti) for ti in range(NT)], (128, NT, D))
        if self.stages[1]:
            self.ffn(b, 0)
        self.tap(f"y2_{b}", X[:], [("X", ti) for ti in range(NT)], (128, NT, D))
        if self.stages[2]:
            self.attn_b(b)
        self.tap(f"y3_{b}", X[:], [("X", ti) for ti in range(NT)], (128, NT, D))
        if self.stages[3]:
            self.ffn(b, 1)
        self.dma("SY", [("X", ti) for ti in range(NT)], [("Yout", b)],
                 [(self.y[b].rearrange("(t p) d -> p t d", p=128), X[:])])

    def ln_stats(self, src_key, src_ap_fn, c0, tag):
        stv = self.ST
        for ti in range(NT):
            ap = src_ap_fn(ti)
            self.op("S", [(src_key, ti)], [("SCR2",), (tag, "s2", ti),
                                           ("STm", c0 + 16 + ti)],
                    lambda e, ap=ap, ti=ti: e.activation(
                        self.SCR2[:], ap, AF.Square,
                        accum_out=stv[:, c0 + 16 + ti:c0 + 17 + ti]))
            self.op("V", [(src_key, ti)], [(tag, "s1", ti), ("STm", c0 + ti)],
                    lambda e, ap=ap, ti=ti: e.reduce_sum(
                        stv[:, c0 + ti:c0 + 1 + ti], ap, axis=AX.X))
        rs1 = [(tag, "s1", ti) for ti in range(NT)]
        rs2 = [(tag, "s2", ti) for ti in range(NT)]
        stm = [("STm", c0 + i) for i in range(32)]
        self.op("V", rs1, [(tag, "mu")] + stm[:16],
                lambda e: e.tensor_scalar_mul(stv[:, c0:c0 + 16],
                                              stv[:, c0:c0 + 16], 1.0 / D))
        self.op("V", [(tag, "mu")], [(tag, "msq"), ("MSQ",)],
                lambda e: e.tensor_tensor(self.MSQ[:], stv[:, c0:c0 + 16],
                                          stv[:, c0:c0 + 16], ALU.mult))
        self.op("V", rs2 + [(tag, "msq")], [(tag, "var")] + stm[16:],
                lambda e: e.tensor_scalar(stv[:, c0 + 16:c0 + 32],
                                          stv[:, c0 + 16:c0 + 32],
                                          1.0 / D, None, ALU.mult))
        self.op("V", [(tag, "var"), (tag, "msq")], [(tag, "var2")] + stm[16:],
                lambda e: e.tensor_tensor(stv[:, c0 + 16:c0 + 32],
                                          stv[:, c0 + 16:c0 + 32],
                                          self.MSQ[:], ALU.subtract))
        self.op("S", [(tag, "var2")], [(tag, "sd")] + stm[16:],
                lambda e: e.activation(stv[:, c0 + 16:c0 + 32],
                                       stv[:, c0 + 16:c0 + 32], AF.Sqrt,
                                       bias=self.SUMS[:, 8:9]))
        self.op("V", [(tag, "sd")], [(tag, "rstd")] + stm[16:],
                lambda e: e.reciprocal(stv[:, c0 + 16:c0 + 32],
                                       stv[:, c0 + 16:c0 + 32]))
        return (stv[:, c0:c0 + 16], stv[:, c0 + 16:c0 + 32], tag)

    def normalize_tile(self, src_ap, stats, ti, out_ap, src_keys, out_keys):
        mu, rstd, tag = stats
        self.op("V", src_keys + [(tag, "mu"), (tag, "rstd")], out_keys,
                lambda e: e.tensor_scalar(out_ap, src_ap,
                                          mu[:, ti:ti + 1], rstd[:, ti:ti + 1],
                                          ALU.subtract, ALU.mult))

    def transpose_to_xt(self, ti, gcol, bcol, src_key, buf=None):
        sbuf = self.SCB if buf is None else buf
        for dc in range(NDC):
            self.op("T", [src_key, ("IDN",)], [("pb", 0)],
                    lambda e, dc=dc: e.transpose(
                        self.pb[0][:, dc * 128:dc * 128 + 128],
                        sbuf[:, dc * 128:dc * 128 + 128], self.IDN[:]))
        for dc in range(NDC):
            if gcol is not None:
                self.op("V", [("pb", 0), ("LNP",)],
                        [("XT", ti, dc)],
                        lambda e, dc=dc, ti=ti: e.tensor_scalar(
                            self.XT[:, dc, ti * 128:ti * 128 + 128],
                            self.pb[0][:, dc * 128:dc * 128 + 128],
                            self.LNP[:, gcol * 8 + dc:gcol * 8 + dc + 1],
                            self.LNP[:, bcol * 8 + dc:bcol * 8 + dc + 1],
                            ALU.mult, ALU.add))
            else:
                self.op("V", [("pb", 0)], [("XT", ti, dc)],
                        lambda e, dc=dc, ti=ti: e.tensor_copy(
                            self.XT[:, dc, ti * 128:ti * 128 + 128],
                            self.pb[0][:, dc * 128:dc * 128 + 128]))

    def ln_to_xt(self, src_key, src_ap_fn, c0, tag, gcol, bcol, rot=False):
        st = self.ln_stats(src_key, src_ap_fn, c0, tag)
        for ti in range(NT):
            use2 = rot and ti % 2 == 1
            buf = self.SCB2 if use2 else self.SCB
            bk = ("SCB2",) if use2 else ("SCB",)
            self.normalize_tile(src_ap_fn(ti), st, ti, buf[:],
                                [(src_key, ti)], [bk])
            self.transpose_to_xt(ti, gcol, bcol, bk, buf)
        return st

    def attn_a(self, b):
        d = self.d
        X, XT, Bb = self.X, self.XT, self.Bb
        pf, pbp = self.pf, self.pb
        WS = self.WSL
        CTX = self.CTX

        ars_keys = [("U", tc) for tc in range(4)] + [
            ("USB",), ("QTg",), ("KTr", 0), ("KTr", 1), ("Vr", 0), ("Vr", 1),
            ("Pa",), ("PTa",), ("gch",), ("smch",), ("CW",), ("BWw",),
            ("MASKA",), ("SCB2",)]
        self.use_ars(ars_keys)
        self.dma("SY", [], [("MASKA",)], [(self.MASKA, d["MASKA"])])
        self.dma("G", [], [("CW",), ("BWw",)], [
            (self.CW, d["Cw"]),
            (self.BWw, d["Bw"].rearrange("(c p) n -> p c n", p=128)),
        ])
        self.dma("G", [], [("BROW",)], [(self.BROW[:], d["bV"])])

        st0 = self.ln_stats("X", lambda ti: X[:, ti, :], 0, ("aL0", b))
        if self.substop == "stats":
            return
        for ti in range(NT):
            self.normalize_tile(X[:, ti, :], st0, ti, Bb[:, ti, :],
                                [("X", ti)], [("B", ti)])
        if self.substop == "norm":
            return
        st1 = self.ln_stats("B", lambda ti: Bb[:, ti, :], 32, ("aL1", b))
        for ti in range(NT):
            use2 = ti % 2 == 1
            buf = self.SCB2 if use2 else self.SCB
            bk = ("SCB2",) if use2 else ("SCB",)
            self.normalize_tile(Bb[:, ti, :], st1, ti, buf[:],
                                [("B", ti)], [bk])
            self.transpose_to_xt(ti, 0, 1, bk, buf)

        if self.substop == "ln":
            return
        ctx_keys = [("CTX", hg) for hg in range(8)]
        self.use_war(ctx_keys)

        for hg in range(8):
            slot = hg % 2
            wk = ("WSL", slot)
            self.dma("G", [], [wk], [
                (WS[:, slot, 0:1024].rearrange("p (c n) -> p c n", n=128),
                 d["WQ"].rearrange("(c p) n -> p c n", p=128)[:, :, hg * 128:hg * 128 + 128]),
                (WS[:, slot, 1024:2048].rearrange("p (c n) -> p c n", n=128),
                 d["WK"].rearrange("(c p) n -> p c n", p=128)[:, :, hg * 128:hg * 128 + 128]),
                (WS[:, slot, 2048:3072].rearrange("p (c n) -> p c n", n=128),
                 d["WV"].rearrange("(c p) n -> p c n", p=128)[:, :, hg * 128:hg * 128 + 128]),
            ])
            wq = WS[:, slot, 0:1024].rearrange("p (c n) -> p c n", n=128)
            wkv = WS[:, slot, 1024:2048].rearrange("p (c n) -> p c n", n=128)
            wv = WS[:, slot, 2048:3072].rearrange("p (c n) -> p c n", n=128)
            for ti in range(NT):
                xtr = [("XT", ti, dc) for dc in range(NDC)]
                xts = lambda dc, ti=ti: XT[:, dc, ti * 128:ti * 128 + 128]
                self.mm(xtr + [wk], [("pf", 0)],
                        [(pf[0][:, 0:128],
                          [(wq[:, dc, :], xts(dc)) for dc in range(NDC)])])
                self.op("V", [("pf", 0), ("BQS",)], [("QTg",)],
                        lambda e, hg=hg: e.tensor_scalar(
                            self.QTg, pf[0][:, 0:128], 0.125,
                            self.BQS[:, hg:hg + 1], ALU.mult, ALU.add))
                self.mm(xtr + [wk], [("pf", 1)],
                        [(pf[1][:, 0:128],
                          [(wkv[:, dc, :], xts(dc)) for dc in range(NDC)])])
                self.op("V", [("pf", 1), ("BQSr",)], [("KTr", ti % 2)],
                        lambda e, ti=ti, hg=hg: e.tensor_scalar_add(
                            self.KTr[:, (ti % 2) * 128:(ti % 2) * 128 + 128],
                            pf[1][:, 0:128],
                            self.BQS[:, 16 + hg:17 + hg]))
                self.mm(xtr + [wk, ("BROW",)], [("pf", 2)],
                        [(pf[2][:, 0:128],
                          [(xts(dc), wv[:, dc, :]) for dc in range(NDC)] +
                          [(self.ONESR[:], self.BROW[:, hg * 128:hg * 128 + 128])])])
                self.op("V", [("pf", 2)], [("Vr", ti % 2)],
                        lambda e, ti=ti: e.tensor_copy(
                            self.Vr[:, ti % 2, :], pf[2][:, 0:128]))
                for h2 in range(2):
                    r0 = 64 * h2
                    grp = []
                    if ti > 0:
                        grp.append((pf[3][:, 0:128],
                                    [(self.QTg[r0:r0 + 64, :],
                                      self.KTr[r0:r0 + 64,
                                               ((ti - 1) % 2) * 128:((ti - 1) % 2) * 128 + 128])]))
                    grp.append((pf[3][:, 128:256],
                                [(self.QTg[r0:r0 + 64, :],
                                  self.KTr[r0:r0 + 64,
                                           (ti % 2) * 128:(ti % 2) * 128 + 128])]))
                    self.mm([("QTg",), ("KTr", ti % 2), ("KTr", (ti - 1) % 2)],
                            [("pf", 3)], grp)
                    if ti == 0:
                        self.op("V", [], [("SA0",), ("SCR2",)],
                                lambda e: e.memset(self.SCR2[:, 0:128], -1e9))
                        self.op("V", [("pf", 3), ("MASKA",), ("SA0",)], [("SA",), ("SCR2",)],
                                lambda e: e.tensor_tensor(
                                    self.SCR2[:, 128:256], pf[3][:, 128:256],
                                    self.MASKA[:, 128:256], ALU.add))
                    else:
                        self.op("V", [("pf", 3), ("MASKA",)], [("SA",), ("SA0",), ("SCR2",)],
                                lambda e: e.tensor_tensor(
                                    self.SCR2[:, 0:256], pf[3][:, 0:256],
                                    self.MASKA, ALU.add))
                    self.op("S", [("SA",), ("SA0",), ("SCR2",)], [("Pa",), ("sumA",), ("SUMS",)],
                            lambda e: e.activation(self.Pa, self.SCR2[:, 0:256],
                                                   AF.Exp,
                                                   accum_out=self.SUMS[:, 0:1]))
                    self.op("V", [("sumA",)], [("rA",), ("SUMS",)],
                            lambda e: e.reciprocal(self.SUMS[:, 1:2],
                                                   self.SUMS[:, 0:1]))
                    self.op("V", [("rA",), ("Pa",)], [("Pa",)],
                            lambda e: e.tensor_scalar_mul(self.Pa, self.Pa,
                                                          self.SUMS[:, 1:2]))
                    for k in ([1] if ti == 0 else [0, 1]):
                        self.op("T", [("Pa",), ("IDN",)], [("pb", 1)],
                                lambda e, k=k: e.transpose(
                                    pbp[1][:, k * 128:k * 128 + 128],
                                    self.Pa[:, k * 128:k * 128 + 128],
                                    self.IDN[:]))
                    if ti == 0:
                        self.op("V", [("pb", 1)], [("PTa",)],
                                lambda e: e.tensor_copy(self.PTa[:, 128:256],
                                                        pbp[1][:, 128:256]))
                    else:
                        self.op("V", [("pb", 1)], [("PTa",)],
                                lambda e: e.tensor_copy(self.PTa,
                                                        pbp[1][:, 0:256]))
                    mmc = []
                    if ti > 0:
                        mmc.append((self.Vr[:, (ti - 1) % 2, r0:r0 + 64],
                                    self.PTa[:, 0:128]))
                    mmc.append((self.Vr[:, ti % 2, r0:r0 + 64],
                                self.PTa[:, 128:256]))
                    self.mm([("PTa",), ("Vr", ti % 2), ("Vr", (ti - 1) % 2)],
                            [("pf", 4)],
                            [(pf[4][r0:r0 + 64, 0:128], mmc)])
                self.op("V", [("pf", 4)], [("CTX", hg)],
                        lambda e, hg=hg, ti=ti: e.tensor_copy(
                            CTX[:, hg, ti * 128:ti * 128 + 128],
                            pf[4][:, 0:128]))

        if self.substop == "qkv":
            return
        for tc in range(4):
            xtr = [("XT", 4 * tc + i, dc) for i in range(4) for dc in range(NDC)]
            self.mm(xtr + [("BWw",)], [("pf", 5)],
                    [(pf[5][0:64, :],
                      [(self.BWw[:, dc, :], XT[:, dc, tc * 512:tc * 512 + 512])
                       for dc in range(NDC)])])
            self.op("V", [("pf", 5)], [("U", tc)],
                    lambda e, tc=tc: e.tensor_copy(
                        self.U[:, tc * 512:tc * 512 + 512], pf[5][0:64, :]))
        self.op("V", [("U", 0), ("U", 1), ("U", 2), ("U", 3), ("ACOL",)],
                [("USB",)],
                lambda e: e.tensor_tensor_scan(
                    self.USB, self.ACOL[:].broadcast_to((64, T)), self.U,
                    0.0, ALU.mult, ALU.add))
        if self.substop == "pass1":
            return

        for dch in range(2):
            cs = slice(dch * 512, dch * 512 + 512)
            self.dma("G", [], [("WSL", 0), ("WSL", 1)], [
                (WS[:, 0, :].rearrange("p (c n) -> p c n", n=512),
                 d["WO"].rearrange("(c p) n -> p c n", p=128)[:, :, cs]),
                (WS[:, 1, :].rearrange("p (c n) -> p c n", n=512),
                 d["Wg"].rearrange("(c p) n -> p c n", p=128)[:, :, cs]),
            ])
            self.dma("G", [], [("BROW",)], [
                (self.BROW[:, 0:512], d["bO"][:, cs]),
                (self.BROW[:, 512:1024], d["bg"][:, cs]),
            ])
            wo = WS[:, 0, :].rearrange("p (c n) -> p c n", n=512)
            wg = WS[:, 1, :].rearrange("p (c n) -> p c n", n=512)
            for ti in range(NT):
                self.mm(ctx_keys + [("WSL", 0), ("BROW",)], [("pf", 0)],
                        [(pf[0][:, :],
                          [(CTX[:, hgx, ti * 128:ti * 128 + 128], wo[:, hgx, :])
                           for hgx in range(8)] +
                          [(self.ONESR[:], self.BROW[:, 0:512])])])
                if self.substop == "pass2" and dch == 0 and ti == 0:
                    dbg = self.nc.dram_tensor("dbg_ao", [128, 512], F32,
                                              kind="ExternalOutput").ap()
                    self.op("V", [("pf", 0)], [("dbgao",), ("SCR2",)],
                            lambda e: e.tensor_copy(self.SCR2[:, 0:512],
                                                    pf[0][:, :]))
                    self.dma("SY", [("dbgao",), ("SCR2",)], [], [(dbg, self.SCR2[:, 0:512])])
                self.mm([("XT", ti, dc) for dc in range(NDC)] +
                        [("WSL", 1), ("BROW",)], [("pf", 2)],
                        [(pf[2][:, :],
                          [(XT[:, dc, ti * 128:ti * 128 + 128], wg[:, dc, :])
                           for dc in range(NDC)] +
                          [(self.ONESR[:], self.BROW[:, 512:1024])])])
                self.op("S", [("pf", 2)], [("gch",), ("SCB2",)],
                        lambda e: e.activation(self.SML, pf[2][:, :], AF.Sigmoid))
                self.mm([("USB",), ("CW",)], [("pf", 3)],
                        [(pf[3][:, :],
                          [(self.USB[:, ti * 128:ti * 128 + 128],
                            self.CW[:, dch * 512:dch * 512 + 512])])])
                self.op("V", [("pf", 3)], [("smch",), ("SCB2",)],
                        lambda e: e.tensor_copy(self.SMM, pf[3][:, :]))
                self.op("V", [("pf", 0), ("smch",)], [("d1",), ("SCR2",)],
                        lambda e: e.tensor_tensor(self.SCR2[:, 0:512],
                                                  pf[0][:, :], self.SMM,
                                                  ALU.subtract))
                self.op("V", [("d1",), ("gch",)], [("d2",), ("SCR2",)],
                        lambda e: e.tensor_tensor(self.SCR2[:, 0:512],
                                                  self.SCR2[:, 0:512],
                                                  self.SML, ALU.mult))
                self.op("V", [("d2",), ("smch",)], [("d3",), ("SCR2",)],
                        lambda e: e.tensor_tensor(self.SCR2[:, 0:512],
                                                  self.SCR2[:, 0:512],
                                                  self.SMM, ALU.add))
                self.op("V", [("d3",), ("B", ti)], [("x1f",), ("SCR2",)],
                        lambda e, ti=ti, cs=cs: e.tensor_tensor(
                            self.SCR2[:, 0:512], self.SCR2[:, 0:512],
                            Bb[:, ti, cs], ALU.add))
                self.op("V", [("x1f",), ("SCR2",), ("X", ti)], [("X", ti)],
                        lambda e, ti=ti, cs=cs: e.tensor_tensor(
                            X[:, ti, cs], X[:, ti, cs], self.SCR2[:, 0:512],
                            ALU.add))
                self.op("V", [("x1f",), ("SCR2",)], [("B", ti)],
                        lambda e, ti=ti, cs=cs: e.tensor_copy(
                            Bb[:, ti, cs], self.SCR2[:, 0:512]))

        if self.substop == "pass2":
            return
        self.ln_to_xt("B", lambda ti: Bb[:, ti, :], 0, ("aL2", b), 2, 3, rot=True)

        self.use_war([("MIDT", j) for j in range(4)])
        MIDT = self.MIDT
        for tb in range(4):
            tis = list(range(4 * tb, 4 * tb + 4))
            for nc_ in range(NDC):
                self.dma("G", [], [("WSL", 0), ("WSL", 1)], [
                    (WS[:, 0, :].rearrange("p (c n) -> p c n", n=512),
                     d["W1"].rearrange("(c p) n -> p c n", p=128)[:, :, nc_ * 512:nc_ * 512 + 512]),
                    (WS[:, 1, :].rearrange("p (c n) -> p c n", n=1024),
                     d["W2"][nc_ * 512:nc_ * 512 + 512, :]
                     .rearrange("(c p) n -> p c n", p=128)),
                ])
                self.dma("G", [], [("BROW",)], [
                    (self.BROW[:, 0:512], d["b1"][:, nc_ * 512:nc_ * 512 + 512]),
                ])
                w1 = WS[:, 0, :].rearrange("p (c n) -> p c n", n=512)
                w2 = WS[:, 1, :].rearrange("p (c n) -> p c n", n=1024)
                for j, ti in enumerate(tis):
                    buf = self.SCB if j % 2 == 0 else self.SCB2
                    bk = ("SCB",) if j % 2 == 0 else ("SCB2",)
                    mmg = [(XT[:, dc, ti * 128:ti * 128 + 128], w1[:, dc, :])
                           for dc in range(NDC)]
                    mmg.append((self.ONESR[:], self.BROW[:, 0:512]))
                    self.mm([("XT", ti, dc) for dc in range(NDC)] +
                            [("WSL", 0), ("BROW",)], [("pf", 4)],
                            [(pf[4][:, :], mmg)])
                    self.op("S", [("pf", 4)], [bk],
                            lambda e, buf=buf: e.activation(buf[:, 0:512],
                                                            pf[4][:, :],
                                                            AF.Gelu))
                    for q in range(4):
                        self.op("T", [bk, ("IDN",)], [("pb", 0)],
                                lambda e, q=q, buf=buf: e.transpose(
                                    pbp[0][:, q * 128:q * 128 + 128],
                                    buf[:, q * 128:q * 128 + 128],
                                    self.IDN[:]))
                    self.op("V", [("pb", 0)],
                            [("MIDT", j)],
                            lambda e, j=j: e.tensor_copy(
                                self.MIDTf[:, j * 512:j * 512 + 512],
                                pbp[0][:, 0:512]))
                for dch in range(2):
                    if nc_ == 0:
                        self.dma("G", [], [("BROW",)],
                                 [(self.BROW[:, 512:1024],
                                   d["b2"][:, dch * 512:dch * 512 + 512])])
                    for j, ti in enumerate(tis):
                        mmg = [(MIDT[:, j, q, :],
                                w2[:, q, dch * 512:dch * 512 + 512])
                               for q in range(4)]
                        if nc_ == 0:
                            mmg.append((self.ONESR[:], self.BROW[:, 512:1024]))
                        self.mm([("MIDT", j), ("WSL", 1)] +
                                ([("BROW",)] if nc_ == 0 else []),
                                [("pf", j % 2)],
                                [(pf[j % 2][:, :], mmg)])
                        cs = slice(dch * 512, dch * 512 + 512)
                        self.op("V", [("pf", j % 2), ("X", ti)], [("X", ti)],
                                lambda e, ti=ti, cs=cs, j=j: e.tensor_tensor(
                                    X[:, ti, cs], X[:, ti, cs],
                                    pf[j % 2][:, :], ALU.add))

    def ffn(self, b, app):
        d = self.d
        X, XT, Bb = self.X, self.XT, self.Bb
        pf, pbp = self.pf, self.pb
        WS = self.WSL

        ars_keys = ([("SKW", 0), ("SKW", 1), ("RELS", 0), ("RELS", 1),
                     ("GT",), ("G2T",),
                     ("GTB",), ("G2TB",), ("RTT",), ("OBs",), ("FB2",),
                     ("IOT8",), ("IOT16",), ("AGW",), ("FGW",),
                     ("CTXE", 0), ("CTXE", 1),
                     ("BCL", 0), ("BCL", 1), ("BCR", 0), ("BCR", 1),
                     ("SCB2",)] +
                    [("GL", ti) for ti in range(NT)])
        self.use_ars(ars_keys)
        self.dma("SY", [], [("IOT8",), ("IOT16",)],
                 [(self.IOT8, d["IOT8"]), (self.IOT16, d["IOT16"])])
        self.dma("G", [], [("RTT",), ("OBs",), ("FB2",), ("AGW",), ("FGW",)], [
            (self.RTT[:, 0:129], d["relT"]),
            (self.OB, d["o_b"]),
            (self.FB2, d["ff_b2"]),
            (self.AGW, d["agate_w"].rearrange("(c p) n -> p c n", p=128)),
            (self.FGW, d["fgate_w"].rearrange("(c p) n -> p c n", p=128)),
        ])

        st0 = self.ln_stats("X", lambda ti: X[:, ti, :], 0, ("fL0", b, app))
        for ti in range(NT):
            self.normalize_tile(X[:, ti, :], st0, ti, Bb[:, ti, :],
                                [("X", ti)], [("B", ti)])
        self.ln_to_xt("B", lambda ti: Bb[:, ti, :], 32, ("fL1", b, app), 4, 5)

        war_keys = ([("QP", s) for s in range(2)] + [("KT1", tc) for tc in range(4)] +
                    [("VTok", ti) for ti in range(NT)] +
                    [("PB", r_, c4) for r_ in range(2) for c4 in range(4)] +
                    [("PBs", 0), ("PBs", 1)] +
                    [("PT", 0), ("PT", 1), ("OWp", 0), ("OWp", 1)])
        self.use_war(war_keys)

        for ti in range(NT):
            self.mm([("XT", ti, dc) for dc in range(NDC)] + [("AGW",)],
                    [("pf", 5)],
                    [(pf[5][:, 0:8],
                      [(XT[:, dc, ti * 128:ti * 128 + 128], self.AGW[:, dc, :])
                       for dc in range(NDC)])])
            self.op("V", [("pf", 5)], [("GL", ti)],
                    lambda e, ti=ti: e.tensor_copy(self.GL[:, ti * 8:ti * 8 + 8],
                                                   pf[5][:, 0:8]))
        self.gates_topk(self.GL[:, 0:128], 8, self.IOT8, self.GT, ("GT",),
                        [("GL", ti) for ti in range(NT)], b, app)
        self.op("V", [("GT",)], [("GTc",), ("SCB",)],
                lambda e: e.tensor_copy(self.SCB[:, 0:128], self.GT))
        for half in range(2):
            for j in range(8):
                ti = half * 8 + j
                self.op("T", [("GTc",), ("SCB",), ("IDN",)], [("pb", 0)],
                        lambda e, ti=ti, j=j: e.transpose(
                            pbp[0][0:8, j * 128:j * 128 + 128],
                            self.SCB[:, ti * 8:ti * 8 + 8], self.IDN[:]))
            self.op("V", [("pb", 0)], [("GTB",)],
                    lambda e, half=half: e.tensor_copy(
                        self.GTB[:, half * 1024:half * 1024 + 1024],
                        pbp[0][0:8, :]))

        self.sc.alias([("BCL", 1), ("BCR", 1)],
                      [("GL", ti) for ti in range(NT)])
        self.dma("G", [], [("WSL", 0)], [
            (WS[:, 0, 0:1024].rearrange("p (c n) -> p c n", n=128),
             d["k_w"].rearrange("(c p) n -> p c n", p=128)),
            (WS[:, 0, 1024:2048].rearrange("p (c n) -> p c n", n=128),
             d["v_w"].rearrange("(c p) n -> p c n", p=128)),
        ])
        self.dma("G", [], [("BROW",)], [(self.BROW[:, 0:128], d["v_b"])])
        kw = WS[:, 0, 0:1024].rearrange("p (c n) -> p c n", n=128)
        vw = WS[:, 0, 1024:2048].rearrange("p (c n) -> p c n", n=128)
        for tc in range(4):
            xtr = [("XT", 4 * tc + i, dc) for i in range(4) for dc in range(NDC)]
            self.mm(xtr + [("WSL", 0)], [("pf", 0)],
                    [(pf[0][:, :],
                      [(kw[:, dc, :], XT[:, dc, tc * 512:tc * 512 + 512])
                       for dc in range(NDC)])])
            self.op("V", [("pf", 0), ("BQSr",)], [("KT1", tc)],
                    lambda e, tc=tc: e.tensor_scalar_add(
                        self.KT1[:, tc * 512:tc * 512 + 512], pf[0][:, :],
                        self.BQS[:, 24:25]))
        for ti in range(NT):
            self.mm([("XT", ti, dc) for dc in range(NDC)] +
                    [("WSL", 0), ("BROW",)], [("pf", 1)],
                    [(pf[1][:, 0:128],
                      [(XT[:, dc, ti * 128:ti * 128 + 128], vw[:, dc, :])
                       for dc in range(NDC)] +
                      [(self.ONESR[:], self.BROW[:, 0:128])])])
            self.op("V", [("pf", 1)], [("VTok", ti)],
                    lambda e, ti=ti: e.tensor_copy(self.VTok[:, ti, :],
                                                   pf[1][:, 0:128]))

        for ep in range(4):
            self.dma("G", [], [("WSL", 1)], [
                (WS[:, 1, e2 * 1024:e2 * 1024 + 1024]
                 .rearrange("p (c n) -> p c n", n=128),
                 d["q_w"][2 * ep + e2].rearrange("(c p) n -> p c n", p=128))
                for e2 in range(2)])
            self.dma("G", [], [("OWp", 0), ("OWp", 1)], [
                (self.OWp[:, e2, :], d["o_w"][2 * ep + e2]) for e2 in range(2)])
            for e2 in range(2):
                e_ = 2 * ep + e2
                qw = WS[:, 1, e2 * 1024:e2 * 1024 + 1024].rearrange(
                    "p (c n) -> p c n", n=128)
                for tc in range(4):
                    xtr = [("XT", 4 * tc + i, dc) for i in range(4)
                           for dc in range(NDC)]
                    self.mm(xtr + [("WSL", 1)], [("pf", 0)],
                            [(pf[0][:, :],
                              [(qw[:, dc, :], XT[:, dc, tc * 512:tc * 512 + 512])
                               for dc in range(NDC)])])
                    self.op("V", [("pf", 0), ("BQS2",)], [("QP", e2)],
                            lambda e, e2=e2, tc=tc, e_=e_: e.tensor_scalar(
                                self.QP[:, e2, tc * 512:tc * 512 + 512],
                                pf[0][:, :], float(128.0 ** -0.5),
                                self.BQS[:, 8 + e_:9 + e_],
                                ALU.mult, ALU.add))
            for ti in range(NT):
                lo = max(0, 128 * ti - 64)
                hi = min(T, 128 * ti + 192)
                jjoff = lo - (128 * ti - 64)
                for e2 in range(2):
                    e_ = 2 * ep + e2
                    rr = (ti * 2 + e2) % 2
                    RELS = self.RELS if rr == 0 else self.RELS2
                    SKW = self.SKW if rr == 0 else self.SKW2
                    PB = self.PB if rr == 0 else self.PB2
                    CTXE = self.CTXE if rr == 0 else self.CTXE2
                    sb_ = 16 + 8 * rr
                    st_ = 12 + 2 * rr
                    qtile = self.QP[:, e2, ti * 128:ti * 128 + 128]
                    self.mm([("QP", e2), ("RTT",)], [("pf", 5)],
                            [(pf[5][:, 0:129], [(qtile, self.RTT[:, 0:129])])])
                    self.op("V", [("pf", 5)], [("RELS", rr)],
                            lambda e, RELS=RELS: e.tensor_copy(
                                RELS[:, 0:129], pf[5][:, 0:129]))
                    self.op("V", [("RELS", rr), ("IOT16",)], [("BCL", rr)],
                            lambda e, RELS=RELS, rr=rr: e.tensor_scalar(
                                self.BCL[rr], self.IOT16[:, 0:127],
                                0.0, RELS[:, 0:1], ALU.mult, ALU.add))
                    self.op("V", [("RELS", rr), ("IOT16",)], [("BCR", rr)],
                            lambda e, RELS=RELS, rr=rr: e.tensor_scalar(
                                self.BCR[rr], self.IOT16[:, 0:128],
                                0.0, RELS[:, 128:129], ALU.mult, ALU.add))
                    slot = self.rext.ap()[ti * 2 + e2]
                    self.dma("SY", [("RELS", rr), ("BCL", rr), ("BCR", rr)],
                             [("rexd", ti, e2)],
                             [(slot[:, 0:127], self.BCL[rr]),
                              (slot[:, 127:256], RELS[:, 0:129]),
                              (slot[:, 256:384], self.BCR[rr])])
                    skw_src = AP(self.rext, (ti * 2 + e2) * 49152 + 127 + jjoff,
                                 [[383, 128], [1, hi - lo]])
                    self.dma("SY", [("rexd", ti, e2)], [("SKW", rr)],
                             [(SKW[:, 0:hi - lo], skw_src)])
                    nsum = 0
                    sum_keys = []
                    for c4 in range(4):
                        s0, s1 = c4 * 512, c4 * 512 + 512
                        bank = c4 % 2
                        self.mm([("QP", e2), ("KT1", c4)], [("pf", bank)],
                                [(pf[bank][:, :],
                                  [(qtile, self.KT1[:, s0:s1])])])
                        segs = []
                        a0 = s0
                        for (rs, re_, kind) in [(s0, min(s1, lo), "L"),
                                                (max(s0, lo), min(s1, hi), "B"),
                                                (max(s0, hi), s1, "R")]:
                            if re_ > rs:
                                segs.append((rs, re_, kind))
                        for (rs, re_, kind) in segs:
                            pslice = pf[bank][:, rs - s0:re_ - s0]
                            if kind == "B":
                                self.op("V", [("pf", bank), ("SKW", rr)],
                                        [("pf", bank)],
                                        lambda e, pslice=pslice, rs=rs, re_=re_,
                                        lo=lo, SKW=SKW:
                                        e.tensor_tensor(
                                            pslice, pslice,
                                            SKW[:, rs - lo:re_ - lo],
                                            ALU.add))
                                bias = 0.0
                            elif kind == "L":
                                bias = RELS[:, 0:1]
                            else:
                                bias = RELS[:, 128:129]
                            sk = ("psum", rr, nsum)
                            self.op("S", [("pf", bank), ("RELS", rr)],
                                    [("PB", rr, c4), sk, ("SUMS", rr)],
                                    lambda e, pslice=pslice, rs=rs, re_=re_,
                                    bias=bias, nsum=nsum, PB=PB, sb_=sb_:
                                    e.activation(PB[:, rs:re_], pslice,
                                                 AF.Exp, bias=bias,
                                                 accum_out=self.SUMS[:, sb_ + nsum:sb_ + nsum + 1]))
                            sum_keys.append(sk)
                            nsum += 1
                    self.op("V", sum_keys, [("tsum", rr), ("SUMS", rr)],
                            lambda e, nsum=nsum, sb_=sb_, st_=st_: e.reduce_sum(
                                self.SUMS[:, st_:st_ + 1],
                                self.SUMS[:, sb_:sb_ + nsum],
                                axis=AX.X))
                    self.op("V", [("tsum", rr)], [("rsum", rr), ("SUMS", rr)],
                            lambda e, st_=st_: e.reciprocal(
                                self.SUMS[:, st_ + 1:st_ + 2],
                                self.SUMS[:, st_:st_ + 1]))
                    self.op("V", [("rsum", rr), ("GT",)],
                            [("pscale", rr), ("SUMS", rr)],
                            lambda e, ti=ti, e_=e_, st_=st_: e.tensor_tensor(
                                self.SUMS[:, st_ + 1:st_ + 2],
                                self.SUMS[:, st_ + 1:st_ + 2],
                                self.GT[:, ti * 8 + e_:ti * 8 + e_ + 1],
                                ALU.mult))
                    self.op("V", [("pscale", rr)] +
                            [("PB", rr, c4) for c4 in range(4)],
                            [("PBs", rr)],
                            lambda e, PB=PB, st_=st_: e.tensor_scalar_mul(
                                PB, PB, self.SUMS[:, st_ + 1:st_ + 2]))
                    for half in range(2):
                        for k in range(8):
                            kt = half * 8 + k
                            self.op("T", [("PBs", rr), ("IDN",)], [("pb", 1)],
                                    lambda e, kt=kt, k=k, PB=PB: e.transpose(
                                        pbp[1][:, k * 128:k * 128 + 128],
                                        PB[:, kt * 128:kt * 128 + 128],
                                        self.IDN[:]))
                        self.op("V", [("pb", 1)],
                                [("PT", half)],
                                lambda e, half=half: e.tensor_copy(
                                    self.PT[:, half * 1024:half * 1024 + 1024],
                                    pbp[1][:, :]))
                    self.mm([("PT", 0), ("PT", 1)] +
                            [("VTok", kt) for kt in range(NT)], [("pf", 2)],
                            [(pf[2][:, 0:128],
                              [(self.VTok[:, kt, :],
                                self.PT[:, kt * 128:kt * 128 + 128])
                               for kt in range(NT)])])
                    self.op("V", [("pf", 2)], [("CTXE", rr)],
                            lambda e, CTXE=CTXE: e.tensor_copy(
                                CTXE, pf[2][:, 0:128]))
                    mmo = [[(CTXE,
                             self.OWp[:, e2, dch * 512:dch * 512 + 512])]
                           for dch in range(2)]
                    if ep == 0 and e2 == 0:
                        for dch in range(2):
                            mmo[dch].append(
                                (self.GTB[:, ti * 128:ti * 128 + 128],
                                 self.OB[:, dch * 512:dch * 512 + 512]))
                    self.mm([("CTXE", rr), ("OWp", e2), ("GTB",), ("OBs",)],
                            [("pf", 3), ("pf", 4)],
                            [(pf[3][:, :], mmo[0]), (pf[4][:, :], mmo[1])],
                            accum=(e2 == 1), stop=(e2 == 1))
                for dch in range(2):
                    cs = slice(dch * 512, dch * 512 + 512)
                    self.op("V", [("pf", 3 + dch), ("B", ti)], [("B", ti)],
                            lambda e, ti=ti, cs=cs, dch=dch: e.tensor_tensor(
                                Bb[:, ti, cs], Bb[:, ti, cs],
                                pf[3 + dch][:, :], ALU.add))

        for ti in range(NT):
            self.op("V", [("B", ti), ("X", ti)], [("X", ti)],
                    lambda e, ti=ti: e.tensor_tensor(
                        X[:, ti, :], X[:, ti, :], Bb[:, ti, :], ALU.add))

        self.ln_to_xt("B", lambda ti: Bb[:, ti, :], 0, ("fL2", b, app), 6, 7)

        for ti in range(NT):
            self.mm([("XT", ti, dc) for dc in range(NDC)] + [("FGW",)],
                    [("pf", 5)],
                    [(pf[5][:, 0:16],
                      [(XT[:, dc, ti * 128:ti * 128 + 128], self.FGW[:, dc, :])
                       for dc in range(NDC)])])
            self.op("V", [("pf", 5)], [("GL", ti)],
                    lambda e, ti=ti: e.tensor_copy(
                        self.GL[:, ti * 16:ti * 16 + 16], pf[5][:, 0:16]))
        self.gates_topk(self.GL[:, 0:256], 16, self.IOT16, self.G2T, ("G2T",),
                        [("GL", ti) for ti in range(NT)], b, app,
                        iot_key=("IOT16",))
        self.op("V", [("G2T",)], [("G2c",), ("SCB",)],
                lambda e: e.tensor_copy(self.SCB[:, 0:256], self.G2T))
        for half in range(2):
            for j in range(8):
                ti = half * 8 + j
                self.op("T", [("G2c",), ("SCB",), ("IDN",)], [("pb", 0)],
                        lambda e, ti=ti, j=j: e.transpose(
                            pbp[0][0:16, j * 128:j * 128 + 128],
                            self.SCB[:, ti * 16:ti * 16 + 16], self.IDN[:]))
            self.op("V", [("pb", 0)], [("G2TB",)],
                    lambda e, half=half: e.tensor_copy(
                        self.G2TB[:, half * 1024:half * 1024 + 1024],
                        pbp[0][0:16, :]))

        self.use_war([("MIDT2",)])
        for e_ in range(16):
            self.dma("G", [], [("WSL", 0), ("WSL", 1)], [
                (WS[:, 0, :].rearrange("p (c n) -> p c n", n=512),
                 d["ff_w1"][e_].rearrange("(c p) n -> p c n", p=128)),
                (WS[:, 1, :].rearrange("p (c n) -> p c n", n=1024),
                 d["ff_w2"][e_].rearrange("(c p) n -> p c n", p=128)),
            ])
            self.dma("G", [], [("BROW",)],
                     [(self.BROW[:, 0:512], d["ff_b1"][e_:e_ + 1, :])])
            w1 = WS[:, 0, :].rearrange("p (c n) -> p c n", n=512)
            w2 = WS[:, 1, :].rearrange("p (c n) -> p c n", n=1024)
            for ti in range(NT):
                mmg = [(XT[:, dc, ti * 128:ti * 128 + 128], w1[:, dc, :])
                       for dc in range(NDC)]
                mmg.append((self.ONESR[:], self.BROW[:, 0:512]))
                self.mm([("XT", ti, dc) for dc in range(NDC)] +
                        [("WSL", 0), ("BROW",)], [("pf", 4)],
                        [(pf[4][:, :], mmg)])
                self.op("S", [("pf", 4), ("G2T",)], [("SCB",)],
                        lambda e, ti=ti, e_=e_: e.activation(
                            self.SCB[:, 0:512], pf[4][:, :], AF.Relu,
                            scale=self.G2T[:, ti * 16 + e_:ti * 16 + e_ + 1]))
                for q in range(4):
                    self.op("T", [("SCB",), ("IDN",)], [("pb", 0)],
                            lambda e, q=q: e.transpose(
                                pbp[0][:, q * 128:q * 128 + 128],
                                self.SCB[:, q * 128:q * 128 + 128],
                                self.IDN[:]))
                self.op("V", [("pb", 0)],
                        [("MIDT2",)],
                        lambda e: e.tensor_copy(
                            self.MIDT2f[:], pbp[0][:, 0:512]))
                for dch in range(2):
                    mmg = [(self.MIDT2[:, q, :],
                            w2[:, q, dch * 512:dch * 512 + 512])
                           for q in range(4)]
                    if e_ == 0:
                        mmg.append((self.G2TB[:, ti * 128:ti * 128 + 128],
                                    self.FB2[:, dch * 512:dch * 512 + 512]))
                    self.mm([("MIDT2",), ("WSL", 1), ("G2TB",), ("FB2",)],
                            [("pf", dch)],
                            [(pf[dch][:, :], mmg)])
                    cs = slice(dch * 512, dch * 512 + 512)
                    self.op("V", [("pf", dch), ("X", ti)], [("X", ti)],
                            lambda e, ti=ti, cs=cs, dch=dch: e.tensor_tensor(
                                X[:, ti, cs], X[:, ti, cs], pf[dch][:, :],
                                ALU.add))

    def gates_topk(self, logits_ap, ne, iot_ap, out_ap, out_key, logit_keys,
                   b, app, iot_key=("IOT8",)):
        S2 = self.SCR2
        nt = NT
        w = nt * ne
        l3 = logits_ap.rearrange("p (t e) -> p t e", e=ne)
        rk = logit_keys

        def v(eng, reads, writes, fn):
            self.op(eng, reads + [("SCR2",), iot_key],
                    writes + [("SCR2",)], fn)
        v("V", rk, [("g_m1",)],
          lambda e: e.reduce_max(S2[:, 0:16], l3, axis=AX.X))
        v("V", rk + [("g_m1",)], [("g_eq",)],
          lambda e: e.tensor_tensor(
              S2[:, 128:128 + w].rearrange("p (t e) -> p t e", e=ne), l3,
              S2[:, 0:16].unsqueeze(2).broadcast_to((128, nt, ne)),
              ALU.is_equal))
        v("V", [("g_eq",)], [("g_t1",)],
          lambda e: e.tensor_tensor(S2[:, 384:384 + w], S2[:, 128:128 + w],
                                    iot_ap[:, 0:w], ALU.mult))
        v("V", [("g_eq",), ("g_t1",)], [("g_t2",)],
          lambda e: e.scalar_tensor_tensor(S2[:, 384:384 + w],
                                           S2[:, 128:128 + w], -float(ne),
                                           S2[:, 384:384 + w],
                                           ALU.mult, ALU.add))
        v("V", [("g_t2",)], [("g_i1",)],
          lambda e: e.tensor_reduce(S2[:, 16:32],
                                    S2[:, 384:384 + w].rearrange(
                                        "p (t e) -> p t e", e=ne),
                                    axis=AX.X, op=ALU.min))
        v("V", [("g_i1",)], [("g_i1b",)],
          lambda e: e.tensor_scalar_add(S2[:, 16:32], S2[:, 16:32], float(ne)))
        v("V", [("g_i1b",)], [("g_m1k",)],
          lambda e: e.tensor_tensor(
              S2[:, 128:128 + w].rearrange("p (t e) -> p t e", e=ne),
              iot_ap[:, 0:w].rearrange("p (t e) -> p t e", e=ne),
              S2[:, 16:32].unsqueeze(2).broadcast_to((128, nt, ne)),
              ALU.is_equal))
        v("V", rk + [("g_m1k",)], [("g_l2",)],
          lambda e: e.scalar_tensor_tensor(S2[:, 384:384 + w],
                                           S2[:, 128:128 + w], -1e30,
                                           logits_ap, ALU.mult, ALU.add))
        v("V", [("g_l2",)], [("g_m2",)],
          lambda e: e.reduce_max(S2[:, 32:48],
                                 S2[:, 384:384 + w].rearrange(
                                     "p (t e) -> p t e", e=ne), axis=AX.X))
        v("V", [("g_l2",), ("g_m2",)], [("g_eq2",)],
          lambda e: e.tensor_tensor(
              S2[:, 640:640 + w].rearrange("p (t e) -> p t e", e=ne),
              S2[:, 384:384 + w].rearrange("p (t e) -> p t e", e=ne),
              S2[:, 32:48].unsqueeze(2).broadcast_to((128, nt, ne)),
              ALU.is_equal))
        v("V", [("g_eq2",)], [("g_t1b",)],
          lambda e: e.tensor_tensor(S2[:, 384:384 + w], S2[:, 640:640 + w],
                                    iot_ap[:, 0:w], ALU.mult))
        v("V", [("g_eq2",), ("g_t1b",)], [("g_t2b",)],
          lambda e: e.scalar_tensor_tensor(S2[:, 384:384 + w],
                                           S2[:, 640:640 + w], -float(ne),
                                           S2[:, 384:384 + w],
                                           ALU.mult, ALU.add))
        v("V", [("g_t2b",)], [("g_i2",)],
          lambda e: e.tensor_reduce(S2[:, 48:64],
                                    S2[:, 384:384 + w].rearrange(
                                        "p (t e) -> p t e", e=ne),
                                    axis=AX.X, op=ALU.min))
        v("V", [("g_i2",)], [("g_i2b",)],
          lambda e: e.tensor_scalar_add(S2[:, 48:64], S2[:, 48:64], float(ne)))
        v("V", [("g_i2b",)], [("g_m2k",)],
          lambda e: e.tensor_tensor(
              S2[:, 640:640 + w].rearrange("p (t e) -> p t e", e=ne),
              iot_ap[:, 0:w].rearrange("p (t e) -> p t e", e=ne),
              S2[:, 48:64].unsqueeze(2).broadcast_to((128, nt, ne)),
              ALU.is_equal))
        v("V", [("g_m1",), ("g_m2",)], [("g_dm",)],
          lambda e: e.tensor_tensor(S2[:, 64:80], S2[:, 0:16], S2[:, 32:48],
                                    ALU.subtract))
        v("S", [("g_dm",)], [("g_g1",)],
          lambda e: e.activation(S2[:, 64:80], S2[:, 64:80], AF.Sigmoid))
        v("V", [("g_g1",)], [("g_g2",)],
          lambda e: e.tensor_scalar(S2[:, 80:96], S2[:, 64:80], -1.0, 1.0,
                                    ALU.mult, ALU.add))
        v("V", [("g_m1k",), ("g_g1",)], [("g_p1",)],
          lambda e: e.tensor_tensor(
              out_ap.rearrange("p (t e) -> p t e", e=ne),
              S2[:, 128:128 + w].rearrange("p (t e) -> p t e", e=ne),
              S2[:, 64:80].unsqueeze(2).broadcast_to((128, nt, ne)),
              ALU.mult))
        v("V", [("g_m2k",), ("g_g2",)], [("g_p2",)],
          lambda e: e.tensor_tensor(
              S2[:, 640:640 + w].rearrange("p (t e) -> p t e", e=ne),
              S2[:, 640:640 + w].rearrange("p (t e) -> p t e", e=ne),
              S2[:, 80:96].unsqueeze(2).broadcast_to((128, nt, ne)),
              ALU.mult))
        v("V", [("g_p1",), ("g_p2",)], [out_key],
          lambda e: e.tensor_tensor(out_ap, out_ap, S2[:, 640:640 + w],
                                    ALU.add))

    def attn_b(self, b):
        d = self.d
        X, XT = self.X, self.XT
        pf, pbp = self.pf, self.pb
        WS = self.WSL

        ars_keys = [("SLb", cg) for cg in range(4)] + [
            ("MASKB",), ("SEGC",), ("LTX",), ("CSKT",), ("CUMK",), ("CUMKB",),
            ("CSVB",), ("CUMVB",), ("CV2",), ("CRS2",), ("PTT", 0), ("PTT", 1),
            ("PTTa", 0), ("PTTa", 1), ("PTTb", 0), ("PTTb", 1), ("SMS",),
            ("SCB2",)]
        self.use_ars(ars_keys)
        self.dma("SY", [], [("MASKB",)], [(self.MASKB, d["MASKB"])])
        self.dma("G", [], [("SEGC",), ("LTX",)], [
            (self.SEGC, d["SEGC"]), (self.LTX, d["LTX"]),
        ])
        self.op("V", [], [("PTT", 0)], lambda e: e.memset(self.PTT, 0.0))
        self.op("V", [], [("PTT", 1)], lambda e: e.memset(self.PTT2, 0.0))
        self.op("V", [], [("CRS2",), ("SCB2",)],
                lambda e: e.memset(self.CRS2, 0.0))

        st0 = self.ln_stats("X", lambda ti: X[:, ti, :], 0, ("bL0", b))
        for ti in range(NT):
            self.normalize_tile(X[:, ti, :], st0, ti, self.SCB[:],
                                [("X", ti)], [("SCB",)])
            self.transpose_to_xt(ti, None, None, ("SCB",))

        war_keys = ([("QTb", tc) for tc in range(4)] +
                    [("KTb", tc) for tc in range(4)] +
                    [("VGb", ti) for ti in range(NT)] +
                    [("PLb", cg) for cg in range(4)] +
                    [("CRSB", h2_, half_, i_) for h2_ in range(2)
                     for half_ in range(2) for i_ in range(2)])
        self.use_war(war_keys)
        OUTB = self.Bb[:]

        for hg in range(8):
            self.dma("G", [], [("WSL", 0)], [
                (WS[:, 0, i * 1024:i * 1024 + 1024]
                 .rearrange("p (c n) -> p c n", n=128),
                 d["W_qkv"].rearrange("(c p) n -> p c n", p=128)
                 [:, :, i * 1024 + hg * 128: i * 1024 + hg * 128 + 128])
                for i in range(3)])
            wq = WS[:, 0, 0:1024].rearrange("p (c n) -> p c n", n=128)
            wk = WS[:, 0, 1024:2048].rearrange("p (c n) -> p c n", n=128)
            wv = WS[:, 0, 2048:3072].rearrange("p (c n) -> p c n", n=128)
            for tc in range(4):
                xtr = [("XT", 4 * tc + i, dc) for i in range(4)
                       for dc in range(NDC)]
                self.mm(xtr + [("WSL", 0)], [("pf", 0)],
                        [(pf[0][:, :],
                          [(wq[:, dc, :], XT[:, dc, tc * 512:tc * 512 + 512])
                           for dc in range(NDC)])])
                self.op("V", [("pf", 0)], [("QTb", tc)],
                        lambda e, tc=tc: e.tensor_scalar_mul(
                            self.QTb[:, tc * 512:tc * 512 + 512], pf[0][:, :],
                            0.125))
                self.mm(xtr + [("WSL", 0)], [("pf", 1)],
                        [(pf[1][:, :],
                          [(wk[:, dc, :], XT[:, dc, tc * 512:tc * 512 + 512])
                           for dc in range(NDC)])])
                self.op("V", [("pf", 1)], [("KTb", tc)],
                        lambda e, tc=tc: e.tensor_copy(
                            self.KTb[:, tc * 512:tc * 512 + 512], pf[1][:, :]))
            for ti in range(NT):
                self.mm([("XT", ti, dc) for dc in range(NDC)] + [("WSL", 0)],
                        [("pf", 2)],
                        [(pf[2][:, 0:128],
                          [(XT[:, dc, ti * 128:ti * 128 + 128], wv[:, dc, :])
                           for dc in range(NDC)])])
                self.op("V", [("pf", 2)], [("VGb", ti)],
                        lambda e, ti=ti: e.tensor_copy(
                            self.VGb[:, ti, :], pf[2][:, 0:128]))
            self.op("V", [("KTb", tc) for tc in range(4)], [("CSKT",)],
                    lambda e: e.reduce_sum(
                        self.CSKT[:, 0:32],
                        self.KTb.rearrange("p (c k) -> p c k", c=32),
                        axis=AX.X))
            self.mm([("VGb", ti) for ti in range(NT)] + [("SEGC",)],
                    [("pf", 5)],
                    [(pf[5][0:32, 0:128],
                      [(self.SEGC[:, ti * 32:ti * 32 + 32], self.VGb[:, ti, :])
                       for ti in range(NT)])])
            self.op("V", [("pf", 5)], [("CSVB",)],
                    lambda e: e.tensor_copy(self.CSVB, pf[5][0:32, 0:128]))
            self.op("V", [("CUMK",)], [("CUMKz",)],
                    lambda e: e.memset(self.CUMK[:, 0:1], 0.0))
            self.op("V", [("CSKT",), ("CUMKz",), ("ONEC",)], [("CUMK",)],
                    lambda e: e.tensor_tensor_scan(
                        self.CUMK[:, 1:33],
                        self.SUMS[:, 30:31].broadcast_to((128, 32)),
                        self.CSKT[:, 0:32], 0.0, ALU.mult, ALU.add))
            self.op("V", [("CUMK",)], [("CUMKB",)],
                    lambda e: e.tensor_copy(self.CUMKB[:, 0:33],
                                            self.CUMK[:, 0:33]))
            self.mm([("CSVB",), ("LTX",)], [("pf", 5)],
                    [(pf[5][0:32, 0:128], [(self.LTX, self.CSVB)])])
            self.op("V", [("pf", 5)], [("CUMVB",)],
                    lambda e: e.tensor_copy(self.CUMVB, pf[5][0:32, 0:128]))
            self.dma("SY", [("CUMVB",)], [("cvd",)],
                     [(self.cvd.ap().rearrange("o (c f) -> (o c) f", c=32),
                       self.CUMVB)])
            cv2_src = AP(self.cvd, 0, [[128, 2], [256, 16], [1, 128]])
            self.dma("SY", [("cvd",)], [("CV2",), ("SCB2",)],
                     [(self.CV2.rearrange("p (t f) -> p t f", t=16), cv2_src)])
            for h2 in range(2):
                r0 = 64 * h2
                for half in range(2):
                    grp = []
                    for cc in range(8):
                        c = half * 16 + cc
                        grp.append((pf[h2][0:1, cc * 64:cc * 64 + 64],
                                    [(self.CUMKB[r0:r0 + 64, c:c + 1],
                                      self.QTb[r0:r0 + 64, c * 64:c * 64 + 64])]))
                    for cc in range(8):
                        c = half * 16 + 8 + cc
                        grp.append((pf[2 + h2][0:1, cc * 64:cc * 64 + 64],
                                    [(self.CUMKB[r0:r0 + 64, c:c + 1],
                                      self.QTb[r0:r0 + 64, c * 64:c * 64 + 64])]))
                    self.mm([("CUMKB",), ("QTb", half * 2), ("QTb", half * 2 + 1)],
                            [("pf", h2), ("pf", 2 + h2)], grp)
                    self.op("S", [("pf", h2), ("CONSTC",)],
                            [("CRSB", h2, half, 0)],
                            lambda e, h2=h2, half=half: e.activation(
                                self.CRSB[:, h2 * 2048 + half * 1024:
                                          h2 * 2048 + half * 1024 + 512],
                                pf[h2][0:1, :], AF.Sigmoid,
                                scale=self.SUMS[0:1, 11:12]))
                    self.op("S", [("pf", 2 + h2), ("CONSTC",)],
                            [("CRSB", h2, half, 1)],
                            lambda e, h2=h2, half=half: e.activation(
                                self.CRSB[:, h2 * 2048 + half * 1024 + 512:
                                          h2 * 2048 + half * 1024 + 1024],
                                pf[2 + h2][0:1, :], AF.Sigmoid,
                                scale=self.SUMS[0:1, 11:12]))
            for cg in range(4):
                grp = []
                for cc in range(8):
                    c = cg * 8 + cc
                    for h2 in range(2):
                        r0 = 64 * h2
                        grp.append((pf[2][r0:r0 + 64, cc * 64:cc * 64 + 64],
                                    [(self.QTb[r0:r0 + 64, c * 64:c * 64 + 64],
                                      self.KTb[r0:r0 + 64, c * 64:c * 64 + 64])]))
                self.mm([("QTb", cg), ("KTb", cg)], [("pf", 2)], grp)
                self.op("V", [("pf", 2), ("MASKB",)], [("SLb", cg)],
                        lambda e, cg=cg: e.tensor_tensor(
                            self.SLb[:, cg * 512:cg * 512 + 512],
                            pf[2][:, :], self.MASKB, ALU.add))
            self.op("S", [("SLb", cg) for cg in range(4)], [("PLball",)],
                    lambda e: e.activation(self.PLb, self.SLb, AF.Exp))
            self.op("V", [("PLball",)], [("SMS",)],
                    lambda e: e.reduce_sum(
                        self.SMS[:, 0:32],
                        self.PLb.rearrange("p (c k) -> p c k", c=32),
                        axis=AX.X))
            self.op("V", [("SMS",)], [("SMSr",)],
                    lambda e: e.reciprocal(self.SMS[:, 0:32], self.SMS[:, 0:32]))
            self.op("V", [("SMSr",), ("PLball",)], [("PLball",)],
                    lambda e: e.tensor_tensor(
                        self.PLb.rearrange("p (c k) -> p c k", c=32),
                        self.PLb.rearrange("p (c k) -> p c k", c=32),
                        self.SMS[:, 0:32].unsqueeze(2)
                        .broadcast_to((128, 32, 64)), ALU.mult))
            self.dma("SY", [("CRSB", h2_, half_, i_) for h2_ in range(2)
                            for half_ in range(2) for i_ in range(2)],
                     [("crsd",)],
                     [(self.crsd.ap(), self.WARt[0:1, 8192:12288])])
            for h2 in range(2):
                r0 = 64 * h2
                crs_src = AP(self.crsd, h2 * 2048,
                             [[128, 16], [1, 64]])
                crs_src2 = AP(self.crsd, h2 * 2048 + 64,
                              [[128, 16], [1, 64]])
                self.dma("SY", [("crsd",)], [("CRS2",), ("SCB2",)], [
                    (self.CRS2[0:1, :].rearrange("p (t f) -> p t f", t=16)
                     [:, :, 0:64], crs_src),
                    (self.CRS2[1:2, :].rearrange("p (t f) -> p t f", t=16)
                     [:, :, 64:128], crs_src2),
                ])
                for ti in range(NT):
                    ca, cb = 2 * ti, 2 * ti + 1
                    ptt = self.PTT if ti % 2 == 0 else self.PTT2
                    pk = ("PTT", ti % 2)
                    cpo = (ti % 2) * 256
                    self.op("T", [("PLball",), ("IDN",)], [("pb", 1)],
                            lambda e, ca=ca, r0=r0, cpo=cpo: e.transpose(
                                pbp[1][0:64, cpo:cpo + 64],
                                self.PLb[r0:r0 + 64, ca * 64:ca * 64 + 64],
                                self.IDN[r0:r0 + 64, r0:r0 + 64]))
                    self.op("T", [("PLball",), ("IDN",)], [("pb", 1)],
                            lambda e, cb=cb, r0=r0, cpo=cpo: e.transpose(
                                pbp[1][64:128, cpo + 64:cpo + 128],
                                self.PLb[r0:r0 + 64, cb * 64:cb * 64 + 64],
                                self.IDN[r0:r0 + 64, r0:r0 + 64]))
                    self.op("V", [("pb", 1)], [pk, ("PTTa", ti % 2)],
                            lambda e, ptt=ptt, cpo=cpo: e.tensor_copy(
                                ptt[0:64, 0:64], pbp[1][0:64, cpo:cpo + 64]))
                    self.op("V", [("pb", 1)], [pk, ("PTTb", ti % 2)],
                            lambda e, ptt=ptt, cpo=cpo: e.tensor_copy(
                                ptt[64:128, 64:128],
                                pbp[1][64:128, cpo + 64:cpo + 128]))
                    po = (ti % 8) * 64
                    self.mm([pk, ("VGb", ti), ("CRS2",),
                             ("CV2",)],
                            [("pf", 3)],
                            [(pf[3][:, po:po + 64],
                              [(ptt, self.VGb[:, ti, r0:r0 + 64]),
                               (self.CRS2[:, ti * 128:ti * 128 + 128],
                                self.CV2[:, ti * 128 + r0:ti * 128 + r0 + 64]
                                if False else
                                self.CV2.rearrange("p (t f) -> p t f", t=16)
                                [:, ti, r0:r0 + 64])])])
                    if ti % 8 == 7:
                        for tj in range(ti - 7, ti + 1):
                            self.op("V", [("pf", 3)], [("B", tj)],
                                    lambda e, tj=tj, hg=hg, r0=r0: e.tensor_copy(
                                        OUTB[:, tj, hg * 128 + r0:
                                             hg * 128 + r0 + 64],
                                        pf[3][:, (tj % 8) * 64:(tj % 8) * 64 + 64]))

        self.dma("G", [], [("WSL", 0), ("WSL", 1)], [
            (WS[:, 0, :].rearrange("p (c n) -> p c n", n=1024),
             d["W_out"].rearrange("(c p) n -> p c n", p=128)[:, 0:4, :]),
            (WS[:, 1, :].rearrange("p (c n) -> p c n", n=1024),
             d["W_out"].rearrange("(c p) n -> p c n", p=128)[:, 4:8, :]),
        ])
        wouts = [WS[:, 0, :].rearrange("p (c n) -> p c n", n=1024),
                 WS[:, 1, :].rearrange("p (c n) -> p c n", n=1024)]
        for ti in range(NT):
            self.op("V", [("B", ti)], [("SCB",)],
                    lambda e, ti=ti: e.tensor_copy(self.SCB[:],
                                                   OUTB[:, ti, :]))
            for dc in range(NDC):
                self.op("T", [("SCB",), ("IDN",)], [("pb", 0)],
                        lambda e, dc=dc: e.transpose(
                            self.pb[0][:, dc * 128:dc * 128 + 128],
                            self.SCB[:, dc * 128:dc * 128 + 128], self.IDN[:]))
            self.op("V", [("pb", 0)], [("OBT",), ("SCB",)],
                    lambda e: e.tensor_copy(self.SCB[:], self.pb[0][:, :]))
            obt = self.SCB[:].rearrange("p (c n) -> p c n", c=8)
            for dch in range(2):
                self.mm([("OBT",), ("SCB",), ("WSL", 0), ("WSL", 1)],
                        [("pf", dch)],
                        [(pf[dch][:, :],
                          [(obt[:, dc, :],
                            wouts[dc // 4][:, dc % 4, dch * 512:dch * 512 + 512])
                           for dc in range(NDC)])])
                cs = slice(dch * 512, dch * 512 + 512)
                self.op("V", [("pf", dch), ("X", ti)], [("X", ti)],
                        lambda e, ti=ti, cs=cs, dch=dch: e.tensor_tensor(
                            X[:, ti, cs], X[:, ti, cs], pf[dch][:, :],
                            ALU.add))


def build_constants():
    eye = np.eye(128, dtype=np.float32).astype(BF)
    onesr = np.ones((1, 128), np.float32).astype(BF)
    i = np.arange(128)
    m_cur = np.where(i[:, None] >= i[None, :], 0.0, -1e9).astype(np.float32)
    m_prev = np.where(i[None, :] > i[:, None], 0.0, -1e9).astype(np.float32)
    maska = np.concatenate([m_prev, m_cur], axis=1)
    q = np.arange(64)
    mb = np.where(q[None, :] > q[:, None], -65000.0, 0.0).astype(np.float32)
    mb2 = np.concatenate([mb, mb], axis=0)
    maskb = np.tile(mb2, (1, 8))
    iot8 = np.tile(np.arange(8, dtype=np.float32)[None, :], (128, 16))
    iot16 = np.tile(np.arange(16, dtype=np.float32)[None, :], (128, 16))
    segc = np.zeros((128, 512), np.float32)
    for ti in range(16):
        for p in range(128):
            segc[p, ti * 32 + 2 * ti + p // 64] = 1.0 / 64.0
    ltx = np.tril(np.ones((32, 32), np.float32), k=-1).T.copy()
    ltx = np.triu(np.ones((32, 32), np.float32), k=1) * 0.5
    onh = np.concatenate([
        np.concatenate([np.ones((1, 64)), np.zeros((1, 64))], axis=1),
        np.concatenate([np.zeros((1, 64)), np.ones((1, 64))], axis=1),
    ], axis=1).astype(np.float32)
    return {
        "IDN": eye, "ONESR": onesr, "MASKA": maska, "MASKB": maskb,
        "IOT8": iot8, "IOT16": iot16,
        "SEGC": segc.astype(BF), "LTX": ltx.astype(BF), "ONH": onh.astype(BF),
    }


def prep_inputs(inputs):
    g = lambda k: np.asarray(inputs[k])
    p = "L0_"
    cm = {}
    for nm in ["WQ", "WK", "WV", "WO", "Wg", "W1", "W2", "Bw", "Cw"]:
        cm[nm] = g(p + "attn_a_" + nm)
    cm["A"] = g(p + "attn_a_A").reshape(64, 1)
    lnp = np.zeros((128, 64), np.float32)
    for i, nm in enumerate(["attn_a_ln1_g", "attn_a_ln1_b", "attn_a_ln2_g",
                            "attn_a_ln2_b", "ffn_ln1_w", "ffn_ln1_b",
                            "ffn_ln2_w", "ffn_ln2_b"]):
        lnp[:, 8 * i:8 * i + 8] = g(p + nm).reshape(8, 128).T
    cm["LNPa"] = lnp
    cm["bQT"] = g(p + "attn_a_bQ").reshape(8, 128).T.copy()
    cm["bKT"] = g(p + "attn_a_bK").reshape(8, 128).T.copy()
    cm["bV"] = g(p + "attn_a_bV").reshape(1, D)
    cm["bO"] = g(p + "attn_a_bO").reshape(1, D)
    cm["bg"] = g(p + "attn_a_bg").reshape(1, D)
    cm["b1"] = g(p + "attn_a_b1").reshape(1, 4 * D)
    cm["b2"] = g(p + "attn_a_b2").reshape(1, D)
    cm["W_qkv"] = g(p + "attn_b_W_qkv")
    cm["W_out"] = g(p + "attn_b_W_out")
    cm["agate_w"] = g(p + "ffn_attn_gate_w")
    cm["q_w"] = g(p + "ffn_q_w")
    cm["qbT"] = g(p + "ffn_q_b").T.copy()
    cm["k_w"] = g(p + "ffn_k_w")
    cm["k_b"] = g(p + "ffn_k_b").reshape(128, 1)
    cm["v_w"] = g(p + "ffn_v_w")
    cm["v_b"] = g(p + "ffn_v_b").reshape(1, 128)
    cm["o_w"] = g(p + "ffn_o_w")
    cm["o_b"] = g(p + "ffn_o_b")
    cm["relT"] = g(p + "ffn_rel_table").T.copy()
    cm["fgate_w"] = g(p + "ffn_ff_gate_w")
    cm["ff_w1"] = g(p + "ffn_ff_w1")
    cm["ff_b1"] = g(p + "ffn_ff_b1")
    cm["ff_w2"] = g(p + "ffn_ff_w2")
    cm["ff_b2"] = g(p + "ffn_ff_b2")
    cm = {k: np.ascontiguousarray(v, dtype=np.float32) for k, v in cm.items()}
    cm.update(build_constants())
    x = np.asarray(inputs["L0_attn_a_x"])
    in_maps = []
    for c in range(NCORES):
        m = dict(cm)
        m["x"] = np.ascontiguousarray(x[c * NB:(c + 1) * NB])
        in_maps.append(m)
    return in_maps


def kernel(**inputs):
    in_maps = prep_inputs(inputs)
    bld = Builder()
    nc = bld.build()
    res = run_bass_kernel_spmd(nc, in_maps, core_ids=list(range(NCORES)))
    out = np.empty((32, T, D), np.float32)
    for c in range(NCORES):
        out[c * NB:(c + 1) * NB] = res.results[c]["y"]
    return out


# version 03
# speedup vs baseline: 1.0182x; 1.0182x over previous
import numpy as np
import ml_dtypes
import concourse.bass as bass
import concourse.mybir as mybir
from concourse.ap import AP
from concourse.bass_utils import run_bass_kernel_spmd

F32 = mybir.dt.float32
BF16 = mybir.dt.bfloat16
AF = mybir.ActivationFunctionType
ALU = mybir.AluOpType
AX = mybir.AxisListType

NCORES = 8
NB = 4
T = 2048
D = 1024
NT = 16
NDC = 8
EPS = 1e-5
BF = ml_dtypes.bfloat16


class Sched:
    ENGINES = ("T", "V", "S", "G", "SY")
    QUEUES = ("SY", "G")
    DMA_DEPTH = 8

    def __init__(self):
        self.count = {e: 0 for e in self.ENGINES}
        self.ops = {e: [] for e in self.ENGINES}
        self.last_write = {}
        self.readers = {}
        self.sem_names = list(self.ENGINES) + [
            f"{q}{i}" for q in self.QUEUES for i in range(self.DMA_DEPTH)]
        self.waited = {e: {} for e in self.ENGINES}
        self.dma_idx = {q: 0 for q in self.QUEUES}
        self.dma_total = {s: 0 for s in self.sem_names}
        self._sems = None

    def _deps(self, eng, reads, writes):
        need = {}
        for k in reads:
            w = self.last_write.get(k)
            if w is not None:
                need[w[0]] = max(need.get(w[0], 0), w[1])
        for k in writes:
            w = self.last_write.get(k)
            if w is not None:
                need[w[0]] = max(need.get(w[0], 0), w[1])
            for (re, rv) in self.readers.get(k, ()):
                need[re] = max(need.get(re, 0), rv)
        waits = []
        for pe, val in sorted(need.items()):
            if pe == eng and pe == "T":
                continue
            if self.waited[eng].get(pe, 0) >= val:
                continue
            waits.append((pe, val))
            self.waited[eng][pe] = val
        return waits

    def op(self, eng, reads, writes, fn, n_inc=1):
        waits = self._deps(eng, reads, writes)
        after = self.count[eng] + n_inc
        self.ops[eng].append((waits, fn, n_inc, eng))
        self.count[eng] = after
        for k in writes:
            self.last_write[k] = (eng, after)
            self.readers[k] = []
        for k in reads:
            self.readers.setdefault(k, []).append((eng, after))

    def dma_op(self, q, reads, writes, fn, ndma):
        waits = self._deps(q, reads, writes)
        s = f"{q}{self.dma_idx[q] % self.DMA_DEPTH}"
        self.dma_idx[q] += 1
        if self.dma_total[s] > 0 and self.waited[q].get(s, 0) < self.dma_total[s]:
            waits.append((s, self.dma_total[s]))
            self.waited[q][s] = self.dma_total[s]
        self.dma_total[s] += 16 * ndma
        tok = (s, self.dma_total[s])
        self.ops[q].append((waits, fn, s, q))
        for k in writes:
            self.last_write[k] = tok
            self.readers[k] = []
        for k in reads:
            self.readers.setdefault(k, []).append(tok)

    def alias(self, new_keys, old_keys):
        deps = []
        for k in old_keys:
            w = self.last_write.pop(k, None)
            if w is not None:
                deps.append(w)
            deps.extend(self.readers.pop(k, ()))
        byeng = {}
        for (e, v) in deps:
            byeng[e] = max(byeng.get(e, 0), v)
        for k in new_keys:
            self.readers[k] = [(e, v) for e, v in byeng.items()]
            self.last_write.pop(k, None)

    def emit(self, nc, block):
        sems = {s: nc.alloc_semaphore("sem_" + s) for s in self.sem_names}
        self._sems = sems

        def run_stream(e):
            def body(engine):
                for (waits, fn, extra, eng) in self.ops[e]:
                    for (pe, val) in waits:
                        engine.wait_ge(sems[pe], val)
                    if isinstance(extra, str):
                        fn(engine, sems[extra])
                    else:
                        last = fn(engine)
                        if last is not None:
                            last.then_inc(sems[e], extra)
            return body

        block.tensor(run_stream("T"))
        block.vector(run_stream("V"))
        block.scalar(run_stream("S"))
        block.gpsimd(run_stream("G"))
        block.sync(run_stream("SY"))
        return sems


class Builder:
    def __init__(self, taps=(), nbatches=NB, stages=(1, 1, 1, 1), substop=None):
        self.substop = substop
        self.nbatches = nbatches
        self.stages = stages
        self.taps = set(taps)
        self.nc = bass.Bass("TRN2", target_bir_lowering=False)
        self.sc = Sched()
        self.tap_names = []
        self.war_keys = [("WARinit",)]
        self.ars_keys = [("ARSinit",)]

    def op(self, eng, reads, writes, fn, n_inc=1):
        self.sc.op(eng, reads, writes, fn, n_inc)

    def dma(self, q, reads, writes, pairs, slow=False):
        def fn(e, sem):
            for (o, i) in pairs:
                if slow:
                    e.dma_start(out=o, in_=i,
                                allow_slow_non_contiguous=True).then_inc(sem, 16)
                else:
                    e.dma_start(out=o, in_=i).then_inc(sem, 16)
        self.sc.dma_op(q, reads, writes, fn, len(pairs))

    def mm(self, reads, writes, groups, accum=False, stop=True):
        def fn(e):
            last = None
            for (ps, mms) in groups:
                n = len(mms)
                for i, (l, r) in enumerate(mms):
                    last = e.matmul(ps, l, r,
                                    start=(i == 0 and not accum),
                                    stop=(i == n - 1 and stop))
            return last
        self.sc.op("T", reads, writes, fn)

    def use_war(self, new_keys):
        self.sc.alias(new_keys, self.war_keys)
        self.war_keys = list(new_keys)

    def use_ars(self, new_keys):
        self.sc.alias(new_keys, self.ars_keys)
        self.ars_keys = list(new_keys)

    def tap(self, label, ap, reads, shape, dtype=F32):
        if label not in self.taps:
            return
        out = self.nc.dram_tensor("tap_" + label, list(shape), dtype,
                                  kind="ExternalOutput").ap()
        self.tap_names.append("tap_" + label)
        self.dma("SY", reads, [], [(out, ap)])

    def build(self):
        nc = self.nc
        d = {}

        def din(name, shape, dt_=F32):
            d[name] = nc.dram_tensor(name, list(shape), dt_,
                                     kind="ExternalInput").ap()
        din("x", (NB, T, D))
        for nm, sh in [
            ("WQ", (D, D)), ("WK", (D, D)), ("WV", (D, D)),
            ("WO", (D, D)), ("Wg", (D, D)),
            ("bQT", (128, 8)), ("bKT", (128, 8)),
            ("bV", (1, D)), ("bO", (1, D)), ("bg", (1, D)),
            ("A", (64, 1)), ("Bw", (D, 64)), ("Cw", (64, D)),
            ("W1", (D, 4 * D)), ("b1", (1, 4 * D)),
            ("W2", (4 * D, D)), ("b2", (1, D)),
            ("W_qkv", (D, 3 * D)), ("W_out", (D, D)),
            ("agate_w", (D, 8)), ("q_w", (8, D, 128)), ("qbT", (128, 8)),
            ("k_w", (D, 128)), ("k_b", (128, 1)),
            ("v_w", (D, 128)), ("v_b", (1, 128)),
            ("o_w", (8, 128, D)), ("o_b", (8, D)),
            ("relT", (128, 129)),
            ("fgate_w", (D, 16)), ("ff_w1", (16, D, 512)), ("ff_b1", (16, 512)),
            ("ff_w2", (16, 512, D)), ("ff_b2", (16, D)),
            ("LNPa", (128, 64)),
        ]:
            din(nm, sh)
        for nm, sh, dt_ in [
            ("IDN", (128, 128), BF16), ("ONESR", (1, 128), BF16),
            ("MASKA", (128, 256), F32), ("MASKB", (128, 512), F32),
            ("IOT8", (128, 128), F32), ("IOT16", (128, 256), F32),
            ("SEGC", (128, 512), BF16), ("LTX", (32, 32), BF16),
            ("ONH", (1, 256), BF16),
        ]:
            din(nm, sh, dt_)

        self.y = nc.dram_tensor("y", [NB, T, D], F32, kind="ExternalOutput").ap()
        self.rext = nc.dram_tensor("rext", [NT * 2, 128, 384], F32,
                                   kind="Internal")
        self.cvd = nc.dram_tensor("cvd", [1, 4096], BF16, kind="Internal")
        self.crsd = nc.dram_tensor("crsd", [1, 4096], BF16, kind="Internal")
        self.d = d

        sb = lambda nm, shape, dt_=F32: nc.alloc_sbuf_tensor(nm, list(shape), dt_)
        self.X = sb("X", (128, NT, D))
        self.XT = sb("XT", (128, NDC, T), BF16)
        self.Bb = sb("Bb", (128, NT, D), BF16)
        self.WARt = sb("WAR", (128, 16384), BF16)
        self.WSL = sb("WSL", (128, 2, 4096), BF16)
        self.SCR2 = sb("SCR2", (128, 1024))
        self.SCB = sb("SCB", (128, 1024), BF16)
        self.ST = sb("ST", (128, 64))
        self.MSQ = sb("MSQ", (128, 16))
        self.SUMS = sb("SUMS", (128, 32))
        self.LNP = sb("LNP", (128, 64))
        self.BQS = sb("BQS", (128, 32))
        self.ACOL = sb("ACOL", (64, 1))
        self.BROW = sb("BROW", (1, 1024), BF16)
        self.IDN = sb("IDNs", (128, 128), BF16)
        self.ONESR = sb("ONESRs", (1, 128), BF16)
        self.ARS = sb("ARS", (128, 5760))

        pp = lambda nm, shape, dt_=F32: nc.alloc_psum_tensor(nm, list(shape), dt_)
        self.pf = [pp(f"pf{i}", (128, 512)) for i in range(6)]
        self.pb = [pp(f"pb{i}", (128, 1024), BF16) for i in range(2)]

        A = self.ARS
        self.U = A[0:64, 0:2048]
        self.USB = A[0:64, 2048:3072].bitcast(BF16)
        self.QTg = A[:, 3072:3136].bitcast(BF16)
        self.KTr = A[:, 3136:3264].bitcast(BF16)
        self.Vr = A[:, 3264:3392].bitcast(BF16).rearrange("p (s n) -> p s n", s=2)
        self.Pa = A[:, 3392:3520].bitcast(BF16)
        self.PTa = A[:, 3520:3648].bitcast(BF16)
        self.SML = A[:, 3648:4160]
        self.SMM = A[:, 4160:4672]
        self.SCB2 = A[:, 3648:4160].bitcast(BF16)
        self.CW = A[0:64, 4672:5184].bitcast(BF16)
        self.BWw = A[:, 5184:5440].bitcast(BF16).rearrange("p (c n) -> p c n", c=8)
        self.MASKA = A[:, 5440:5696]
        self.SKW = A[:, 0:256]
        self.RELS = A[:, 256:388]
        self.REXT = A[:, 388:772]
        self.BCL = [A[:, 388:515], A[:, 772:899]]
        self.BCR = [A[:, 516:644], A[:, 900:1028]]
        self.GL = A[:, 772:1028]
        self.GT = A[:, 1028:1156]
        self.G2T = A[:, 1156:1412]
        self.GTB = A[0:8, 1412:2436].bitcast(BF16)
        self.G2TB = A[0:16, 2436:3460].bitcast(BF16)
        self.RTT = A[:, 3460:3526].bitcast(BF16)
        self.OB = A[0:8, 3526:4038].bitcast(BF16)
        self.FB2 = A[0:16, 4038:4550].bitcast(BF16)
        self.IOT8 = A[:, 4550:4678]
        self.IOT16 = A[:, 4678:4934]
        self.AGW = A[:, 4934:4966].bitcast(BF16).rearrange("p (c n) -> p c n", c=8)
        self.CTXE = A[:, 5094:5158].bitcast(BF16)
        self.CTXE2 = A[:, 5158:5222].bitcast(BF16)
        self.SKW2 = A[:, 5222:5478]
        self.RELS2 = A[:, 5478:5610]
        self.FGW = A[:, 4966:5030].bitcast(BF16).rearrange("p (c n) -> p c n", c=8)
        self.SLb = A[:, 0:2048]
        self.MASKB = A[:, 2048:2560]
        self.SEGC = A[:, 2560:2816].bitcast(BF16)
        self.LTX = A[0:32, 2816:2832].bitcast(BF16)
        self.CSKT = A[:, 2832:2864]
        self.CUMK = A[:, 2864:2900]
        self.CUMKB = A[:, 2900:2918].bitcast(BF16)
        self.CSVB = A[0:32, 2918:2982].bitcast(BF16)
        self.CUMVB = A[0:32, 2982:3046].bitcast(BF16)
        self.CV2 = A[0:2, 3046:4070].bitcast(BF16)
        self.CRS2 = A[0:2, 4070:5094].bitcast(BF16)
        self.SMS = A[:, 5100:5132]
        self.PTT = A[:, 5132:5196].bitcast(BF16)
        self.PTT2 = A[:, 5610:5674].bitcast(BF16)

        W = self.WARt[:]
        self.CTX = W.rearrange("p (h t) -> p h t", h=8)
        self.MIDT = W[:, 0:2048].rearrange("p (j q t) -> p j q t", j=4, q=4)
        self.MIDTf = W[:, 0:2048]
        self.QP = W[:, 0:4096].rearrange("p (s t) -> p s t", s=2)
        self.KT1 = W[:, 4096:6144]
        self.VTok = W[:, 6144:8192].rearrange("p (i n) -> p i n", i=16)
        self.PB = W[:, 8192:10240]
        self.PT = W[:, 10240:12288]
        self.OWp = W[:, 12288:14336].rearrange("p (s n) -> p s n", s=2)
        self.MIDT2 = W[:, 14336:14848].rearrange("p (q t) -> p q t", q=4)
        self.PB2 = W[:, 14336:16384]
        self.MIDT2f = W[:, 14336:14848]
        self.QTb = W[:, 0:2048]
        self.KTb = W[:, 2048:4096]
        self.VGb = W[:, 4096:6144].rearrange("p (i n) -> p i n", i=16)
        self.PLb = W[:, 6144:8192]
        self.OUTBg = W[:, 8192:10240]
        self.PTb = W[:, 10240:11264]
        self.CRSB = self.WARt[0:1, 8192:12288]
        self.CRB = W[:, 15360:16384].bitcast(F32)

        self.prologue()
        for b in range(self.nbatches):
            self.batch(b)

        from contextlib import ExitStack
        with ExitStack() as stack:
            block = stack.enter_context(nc.Block())
            self.sc.emit(nc, block)
        return nc

    def prologue(self):
        d = self.d
        self.dma("SY", [], [("IDN",), ("ONESR",), ("LNP",), ("BQSr",),
                            ("ACOL",)], [
            (self.IDN[:], d["IDN"]), (self.ONESR[:], d["ONESR"]),
            (self.LNP[:], d["LNPa"]),
            (self.BQS[:, 0:8], d["bQT"]), (self.BQS[:, 8:16], d["qbT"]),
            (self.BQS[:, 16:24], d["bKT"]), (self.BQS[:, 24:25], d["k_b"]),
            (self.ACOL[:], d["A"]),
        ])
        self.op("V", [("BQSr",)], [("BQS",)],
                lambda e: e.tensor_scalar_mul(self.BQS[:, 0:8],
                                              self.BQS[:, 0:8], 0.125))
        self.op("V", [("BQSr",)], [("BQS2",)],
                lambda e: e.tensor_scalar_mul(self.BQS[:, 8:16],
                                              self.BQS[:, 8:16],
                                              float(128.0 ** -0.5)))
        self.op("V", [], [("ONEC",), ("SUMS",)],
                lambda e: e.memset(self.SUMS[:, 30:31], 1.0))
        self.op("V", [], [("CONSTC",), ("SUMS",)],
                lambda e: e.memset(self.SUMS[:, 8:9], float(EPS)))
        self.op("V", [], [("CONSTC",), ("SUMS",)],
                lambda e: e.memset(self.SUMS[:, 9:10], 0.125))
        self.op("V", [], [("CONSTC",), ("SUMS",)],
                lambda e: e.memset(self.SUMS[:, 10:11], float(128.0 ** -0.5)))
        self.op("V", [], [("CONSTC",), ("SUMS",)],
                lambda e: e.memset(self.SUMS[:, 11:12], 1.0 / 64.0))

    def batch(self, b):
        d = self.d
        X = self.X
        self.dma("SY", [], [("X", ti) for ti in range(NT)],
                 [(X[:], d["x"][b].rearrange("(t p) d -> p t d", p=128))])
        if self.substop == "load":
            pass
        elif self.stages[0]:
            self.attn_a(b)
        self.tap(f"y1_{b}", X[:], [("X", ti) for ti in range(NT)], (128, NT, D))
        if self.stages[1]:
            self.ffn(b, 0)
        self.tap(f"y2_{b}", X[:], [("X", ti) for ti in range(NT)], (128, NT, D))
        if self.stages[2]:
            self.attn_b(b)
        self.tap(f"y3_{b}", X[:], [("X", ti) for ti in range(NT)], (128, NT, D))
        if self.stages[3]:
            self.ffn(b, 1)
        self.dma("SY", [("X", ti) for ti in range(NT)], [("Yout", b)],
                 [(self.y[b].rearrange("(t p) d -> p t d", p=128), X[:])])

    def ln_stats(self, src_key, src_ap_fn, c0, tag):
        stv = self.ST
        for ti in range(NT):
            ap = src_ap_fn(ti)
            self.op("S", [(src_key, ti)], [("SCR2",), (tag, "s2", ti),
                                           ("STm", c0 + 16 + ti)],
                    lambda e, ap=ap, ti=ti: e.activation(
                        self.SCR2[:], ap, AF.Square,
                        accum_out=stv[:, c0 + 16 + ti:c0 + 17 + ti]))
            self.op("V", [(src_key, ti)], [(tag, "s1", ti), ("STm", c0 + ti)],
                    lambda e, ap=ap, ti=ti: e.reduce_sum(
                        stv[:, c0 + ti:c0 + 1 + ti], ap, axis=AX.X))
        rs1 = [(tag, "s1", ti) for ti in range(NT)]
        rs2 = [(tag, "s2", ti) for ti in range(NT)]
        stm = [("STm", c0 + i) for i in range(32)]
        self.op("V", rs1, [(tag, "mu")] + stm[:16],
                lambda e: e.tensor_scalar_mul(stv[:, c0:c0 + 16],
                                              stv[:, c0:c0 + 16], 1.0 / D))
        self.op("V", [(tag, "mu")], [(tag, "msq"), ("MSQ",)],
                lambda e: e.tensor_tensor(self.MSQ[:], stv[:, c0:c0 + 16],
                                          stv[:, c0:c0 + 16], ALU.mult))
        self.op("V", rs2 + [(tag, "msq")], [(tag, "var")] + stm[16:],
                lambda e: e.tensor_scalar(stv[:, c0 + 16:c0 + 32],
                                          stv[:, c0 + 16:c0 + 32],
                                          1.0 / D, None, ALU.mult))
        self.op("V", [(tag, "var"), (tag, "msq")], [(tag, "var2")] + stm[16:],
                lambda e: e.tensor_tensor(stv[:, c0 + 16:c0 + 32],
                                          stv[:, c0 + 16:c0 + 32],
                                          self.MSQ[:], ALU.subtract))
        self.op("S", [(tag, "var2")], [(tag, "sd")] + stm[16:],
                lambda e: e.activation(stv[:, c0 + 16:c0 + 32],
                                       stv[:, c0 + 16:c0 + 32], AF.Sqrt,
                                       bias=self.SUMS[:, 8:9]))
        self.op("V", [(tag, "sd")], [(tag, "rstd")] + stm[16:],
                lambda e: e.reciprocal(stv[:, c0 + 16:c0 + 32],
                                       stv[:, c0 + 16:c0 + 32]))
        return (stv[:, c0:c0 + 16], stv[:, c0 + 16:c0 + 32], tag)

    def normalize_tile(self, src_ap, stats, ti, out_ap, src_keys, out_keys):
        mu, rstd, tag = stats
        self.op("V", src_keys + [(tag, "mu"), (tag, "rstd")], out_keys,
                lambda e: e.tensor_scalar(out_ap, src_ap,
                                          mu[:, ti:ti + 1], rstd[:, ti:ti + 1],
                                          ALU.subtract, ALU.mult))

    def transpose_to_xt(self, ti, gcol, bcol, src_key, buf=None):
        sbuf = self.SCB if buf is None else buf
        for dc in range(NDC):
            self.op("T", [src_key, ("IDN",)], [("pb", 0)],
                    lambda e, dc=dc: e.transpose(
                        self.pb[0][:, dc * 128:dc * 128 + 128],
                        sbuf[:, dc * 128:dc * 128 + 128], self.IDN[:]))
        for dc in range(NDC):
            if gcol is not None:
                self.op("V", [("pb", 0), ("LNP",)],
                        [("XT", ti, dc)],
                        lambda e, dc=dc, ti=ti: e.tensor_scalar(
                            self.XT[:, dc, ti * 128:ti * 128 + 128],
                            self.pb[0][:, dc * 128:dc * 128 + 128],
                            self.LNP[:, gcol * 8 + dc:gcol * 8 + dc + 1],
                            self.LNP[:, bcol * 8 + dc:bcol * 8 + dc + 1],
                            ALU.mult, ALU.add))
            else:
                self.op("V", [("pb", 0)], [("XT", ti, dc)],
                        lambda e, dc=dc, ti=ti: e.tensor_copy(
                            self.XT[:, dc, ti * 128:ti * 128 + 128],
                            self.pb[0][:, dc * 128:dc * 128 + 128]))

    def ln_to_xt(self, src_key, src_ap_fn, c0, tag, gcol, bcol, rot=False):
        st = self.ln_stats(src_key, src_ap_fn, c0, tag)
        for ti in range(NT):
            use2 = rot and ti % 2 == 1
            buf = self.SCB2 if use2 else self.SCB
            bk = ("SCB2",) if use2 else ("SCB",)
            self.normalize_tile(src_ap_fn(ti), st, ti, buf[:],
                                [(src_key, ti)], [bk])
            self.transpose_to_xt(ti, gcol, bcol, bk, buf)
        return st

    def attn_a(self, b):
        d = self.d
        X, XT, Bb = self.X, self.XT, self.Bb
        pf, pbp = self.pf, self.pb
        WS = self.WSL
        CTX = self.CTX

        ars_keys = [("U", tc) for tc in range(4)] + [
            ("USB",), ("QTg",), ("KTr", 0), ("KTr", 1), ("Vr", 0), ("Vr", 1),
            ("Pa",), ("PTa",), ("gch",), ("smch",), ("CW",), ("BWw",),
            ("MASKA",), ("SCB2",)]
        self.use_ars(ars_keys)
        self.dma("SY", [], [("MASKA",)], [(self.MASKA, d["MASKA"])])
        self.dma("G", [], [("CW",), ("BWw",)], [
            (self.CW, d["Cw"]),
            (self.BWw, d["Bw"].rearrange("(c p) n -> p c n", p=128)),
        ])
        self.dma("G", [], [("BROW",)], [(self.BROW[:], d["bV"])])

        st0 = self.ln_stats("X", lambda ti: X[:, ti, :], 0, ("aL0", b))
        if self.substop == "stats":
            return
        for ti in range(NT):
            self.normalize_tile(X[:, ti, :], st0, ti, Bb[:, ti, :],
                                [("X", ti)], [("B", ti)])
        if self.substop == "norm":
            return
        st1 = self.ln_stats("B", lambda ti: Bb[:, ti, :], 32, ("aL1", b))
        for ti in range(NT):
            use2 = ti % 2 == 1
            buf = self.SCB2 if use2 else self.SCB
            bk = ("SCB2",) if use2 else ("SCB",)
            self.normalize_tile(Bb[:, ti, :], st1, ti, buf[:],
                                [("B", ti)], [bk])
            self.transpose_to_xt(ti, 0, 1, bk, buf)

        if self.substop == "ln":
            return
        ctx_keys = [("CTX", hg) for hg in range(8)]
        self.use_war(ctx_keys)

        for hg in range(8):
            slot = hg % 2
            wk = ("WSL", slot)
            self.dma("G", [], [wk], [
                (WS[:, slot, 0:1024].rearrange("p (c n) -> p c n", n=128),
                 d["WQ"].rearrange("(c p) n -> p c n", p=128)[:, :, hg * 128:hg * 128 + 128]),
                (WS[:, slot, 1024:2048].rearrange("p (c n) -> p c n", n=128),
                 d["WK"].rearrange("(c p) n -> p c n", p=128)[:, :, hg * 128:hg * 128 + 128]),
                (WS[:, slot, 2048:3072].rearrange("p (c n) -> p c n", n=128),
                 d["WV"].rearrange("(c p) n -> p c n", p=128)[:, :, hg * 128:hg * 128 + 128]),
            ])
            wq = WS[:, slot, 0:1024].rearrange("p (c n) -> p c n", n=128)
            wkv = WS[:, slot, 1024:2048].rearrange("p (c n) -> p c n", n=128)
            wv = WS[:, slot, 2048:3072].rearrange("p (c n) -> p c n", n=128)
            for ti in range(NT):
                xtr = [("XT", ti, dc) for dc in range(NDC)]
                xts = lambda dc, ti=ti: XT[:, dc, ti * 128:ti * 128 + 128]
                self.mm(xtr + [wk], [("pf", 0)],
                        [(pf[0][:, 0:128],
                          [(wq[:, dc, :], xts(dc)) for dc in range(NDC)])])
                self.op("V", [("pf", 0), ("BQS",)], [("QTg",)],
                        lambda e, hg=hg: e.tensor_scalar(
                            self.QTg, pf[0][:, 0:128], 0.125,
                            self.BQS[:, hg:hg + 1], ALU.mult, ALU.add))
                self.mm(xtr + [wk], [("pf", 1)],
                        [(pf[1][:, 0:128],
                          [(wkv[:, dc, :], xts(dc)) for dc in range(NDC)])])
                self.op("V", [("pf", 1), ("BQSr",)], [("KTr", ti % 2)],
                        lambda e, ti=ti, hg=hg: e.tensor_scalar_add(
                            self.KTr[:, (ti % 2) * 128:(ti % 2) * 128 + 128],
                            pf[1][:, 0:128],
                            self.BQS[:, 16 + hg:17 + hg]))
                self.mm(xtr + [wk, ("BROW",)], [("pf", 2)],
                        [(pf[2][:, 0:128],
                          [(xts(dc), wv[:, dc, :]) for dc in range(NDC)] +
                          [(self.ONESR[:], self.BROW[:, hg * 128:hg * 128 + 128])])])
                self.op("V", [("pf", 2)], [("Vr", ti % 2)],
                        lambda e, ti=ti: e.tensor_copy(
                            self.Vr[:, ti % 2, :], pf[2][:, 0:128]))
                for h2 in range(2):
                    r0 = 64 * h2
                    grp = []
                    if ti > 0:
                        grp.append((pf[3][:, 0:128],
                                    [(self.QTg[r0:r0 + 64, :],
                                      self.KTr[r0:r0 + 64,
                                               ((ti - 1) % 2) * 128:((ti - 1) % 2) * 128 + 128])]))
                    grp.append((pf[3][:, 128:256],
                                [(self.QTg[r0:r0 + 64, :],
                                  self.KTr[r0:r0 + 64,
                                           (ti % 2) * 128:(ti % 2) * 128 + 128])]))
                    self.mm([("QTg",), ("KTr", ti % 2), ("KTr", (ti - 1) % 2)],
                            [("pf", 3)], grp)
                    if ti == 0:
                        self.op("V", [], [("SA0",), ("SCR2",)],
                                lambda e: e.memset(self.SCR2[:, 0:128], -1e9))
                        self.op("V", [("pf", 3), ("MASKA",), ("SA0",)], [("SA",), ("SCR2",)],
                                lambda e: e.tensor_tensor(
                                    self.SCR2[:, 128:256], pf[3][:, 128:256],
                                    self.MASKA[:, 128:256], ALU.add))
                    else:
                        self.op("V", [("pf", 3), ("MASKA",)], [("SA",), ("SA0",), ("SCR2",)],
                                lambda e: e.tensor_tensor(
                                    self.SCR2[:, 0:256], pf[3][:, 0:256],
                                    self.MASKA, ALU.add))
                    self.op("S", [("SA",), ("SA0",), ("SCR2",)], [("Pa",), ("sumA",), ("SUMS",)],
                            lambda e: e.activation(self.Pa, self.SCR2[:, 0:256],
                                                   AF.Exp,
                                                   accum_out=self.SUMS[:, 0:1]))
                    self.op("V", [("sumA",)], [("rA",), ("SUMS",)],
                            lambda e: e.reciprocal(self.SUMS[:, 1:2],
                                                   self.SUMS[:, 0:1]))
                    self.op("V", [("rA",), ("Pa",)], [("Pa",)],
                            lambda e: e.tensor_scalar_mul(self.Pa, self.Pa,
                                                          self.SUMS[:, 1:2]))
                    for k in ([1] if ti == 0 else [0, 1]):
                        self.op("T", [("Pa",), ("IDN",)], [("pb", 1)],
                                lambda e, k=k: e.transpose(
                                    pbp[1][:, k * 128:k * 128 + 128],
                                    self.Pa[:, k * 128:k * 128 + 128],
                                    self.IDN[:]))
                    if ti == 0:
                        self.op("V", [("pb", 1)], [("PTa",)],
                                lambda e: e.tensor_copy(self.PTa[:, 128:256],
                                                        pbp[1][:, 128:256]))
                    else:
                        self.op("V", [("pb", 1)], [("PTa",)],
                                lambda e: e.tensor_copy(self.PTa,
                                                        pbp[1][:, 0:256]))
                    mmc = []
                    if ti > 0:
                        mmc.append((self.Vr[:, (ti - 1) % 2, r0:r0 + 64],
                                    self.PTa[:, 0:128]))
                    mmc.append((self.Vr[:, ti % 2, r0:r0 + 64],
                                self.PTa[:, 128:256]))
                    self.mm([("PTa",), ("Vr", ti % 2), ("Vr", (ti - 1) % 2)],
                            [("pf", 4)],
                            [(pf[4][r0:r0 + 64, 0:128], mmc)])
                self.op("V", [("pf", 4)], [("CTX", hg)],
                        lambda e, hg=hg, ti=ti: e.tensor_copy(
                            CTX[:, hg, ti * 128:ti * 128 + 128],
                            pf[4][:, 0:128]))

        if self.substop == "qkv":
            return
        for tc in range(4):
            xtr = [("XT", 4 * tc + i, dc) for i in range(4) for dc in range(NDC)]
            self.mm(xtr + [("BWw",)], [("pf", 5)],
                    [(pf[5][0:64, :],
                      [(self.BWw[:, dc, :], XT[:, dc, tc * 512:tc * 512 + 512])
                       for dc in range(NDC)])])
            self.op("V", [("pf", 5)], [("U", tc)],
                    lambda e, tc=tc: e.tensor_copy(
                        self.U[:, tc * 512:tc * 512 + 512], pf[5][0:64, :]))
        self.op("V", [("U", 0), ("U", 1), ("U", 2), ("U", 3), ("ACOL",)],
                [("USB",)],
                lambda e: e.tensor_tensor_scan(
                    self.USB, self.ACOL[:].broadcast_to((64, T)), self.U,
                    0.0, ALU.mult, ALU.add))
        if self.substop == "pass1":
            return

        for dch in range(2):
            cs = slice(dch * 512, dch * 512 + 512)
            self.dma("G", [], [("WSL", 0), ("WSL", 1)], [
                (WS[:, 0, :].rearrange("p (c n) -> p c n", n=512),
                 d["WO"].rearrange("(c p) n -> p c n", p=128)[:, :, cs]),
                (WS[:, 1, :].rearrange("p (c n) -> p c n", n=512),
                 d["Wg"].rearrange("(c p) n -> p c n", p=128)[:, :, cs]),
            ])
            self.dma("G", [], [("BROW",)], [
                (self.BROW[:, 0:512], d["bO"][:, cs]),
                (self.BROW[:, 512:1024], d["bg"][:, cs]),
            ])
            wo = WS[:, 0, :].rearrange("p (c n) -> p c n", n=512)
            wg = WS[:, 1, :].rearrange("p (c n) -> p c n", n=512)
            for ti in range(NT):
                self.mm(ctx_keys + [("WSL", 0), ("BROW",)], [("pf", 0)],
                        [(pf[0][:, :],
                          [(CTX[:, hgx, ti * 128:ti * 128 + 128], wo[:, hgx, :])
                           for hgx in range(8)] +
                          [(self.ONESR[:], self.BROW[:, 0:512])])])
                if self.substop == "pass2" and dch == 0 and ti == 0:
                    dbg = self.nc.dram_tensor("dbg_ao", [128, 512], F32,
                                              kind="ExternalOutput").ap()
                    self.op("V", [("pf", 0)], [("dbgao",), ("SCR2",)],
                            lambda e: e.tensor_copy(self.SCR2[:, 0:512],
                                                    pf[0][:, :]))
                    self.dma("SY", [("dbgao",), ("SCR2",)], [], [(dbg, self.SCR2[:, 0:512])])
                self.mm([("XT", ti, dc) for dc in range(NDC)] +
                        [("WSL", 1), ("BROW",)], [("pf", 2)],
                        [(pf[2][:, :],
                          [(XT[:, dc, ti * 128:ti * 128 + 128], wg[:, dc, :])
                           for dc in range(NDC)] +
                          [(self.ONESR[:], self.BROW[:, 512:1024])])])
                self.op("S", [("pf", 2)], [("gch",), ("SCB2",)],
                        lambda e: e.activation(self.SML, pf[2][:, :], AF.Sigmoid))
                self.mm([("USB",), ("CW",)], [("pf", 3)],
                        [(pf[3][:, :],
                          [(self.USB[:, ti * 128:ti * 128 + 128],
                            self.CW[:, dch * 512:dch * 512 + 512])])])
                self.op("V", [("pf", 3)], [("smch",), ("SCB2",)],
                        lambda e: e.tensor_copy(self.SMM, pf[3][:, :]))
                self.op("V", [("pf", 0), ("smch",)], [("d1",), ("SCR2",)],
                        lambda e: e.tensor_tensor(self.SCR2[:, 0:512],
                                                  pf[0][:, :], self.SMM,
                                                  ALU.subtract))
                self.op("V", [("d1",), ("gch",)], [("d2",), ("SCR2",)],
                        lambda e: e.tensor_tensor(self.SCR2[:, 0:512],
                                                  self.SCR2[:, 0:512],
                                                  self.SML, ALU.mult))
                self.op("V", [("d2",), ("smch",)], [("d3",), ("SCR2",)],
                        lambda e: e.tensor_tensor(self.SCR2[:, 0:512],
                                                  self.SCR2[:, 0:512],
                                                  self.SMM, ALU.add))
                self.op("V", [("d3",), ("B", ti)], [("x1f",), ("SCR2",)],
                        lambda e, ti=ti, cs=cs: e.tensor_tensor(
                            self.SCR2[:, 0:512], self.SCR2[:, 0:512],
                            Bb[:, ti, cs], ALU.add))
                self.op("V", [("x1f",), ("SCR2",), ("X", ti)], [("X", ti)],
                        lambda e, ti=ti, cs=cs: e.tensor_tensor(
                            X[:, ti, cs], X[:, ti, cs], self.SCR2[:, 0:512],
                            ALU.add))
                self.op("V", [("x1f",), ("SCR2",)], [("B", ti)],
                        lambda e, ti=ti, cs=cs: e.tensor_copy(
                            Bb[:, ti, cs], self.SCR2[:, 0:512]))

        if self.substop == "pass2":
            return
        self.ln_to_xt("B", lambda ti: Bb[:, ti, :], 0, ("aL2", b), 2, 3, rot=True)

        self.use_war([("MIDT", j) for j in range(4)])
        MIDT = self.MIDT
        for tb in range(4):
            tis = list(range(4 * tb, 4 * tb + 4))
            for nc_ in range(NDC):
                self.dma("G", [], [("WSL", 0), ("WSL", 1)], [
                    (WS[:, 0, :].rearrange("p (c n) -> p c n", n=512),
                     d["W1"].rearrange("(c p) n -> p c n", p=128)[:, :, nc_ * 512:nc_ * 512 + 512]),
                    (WS[:, 1, :].rearrange("p (c n) -> p c n", n=1024),
                     d["W2"][nc_ * 512:nc_ * 512 + 512, :]
                     .rearrange("(c p) n -> p c n", p=128)),
                ])
                self.dma("G", [], [("BROW",)], [
                    (self.BROW[:, 0:512], d["b1"][:, nc_ * 512:nc_ * 512 + 512]),
                ])
                w1 = WS[:, 0, :].rearrange("p (c n) -> p c n", n=512)
                w2 = WS[:, 1, :].rearrange("p (c n) -> p c n", n=1024)
                for j, ti in enumerate(tis):
                    buf = self.SCB if j % 2 == 0 else self.SCB2
                    bk = ("SCB",) if j % 2 == 0 else ("SCB2",)
                    mmg = [(XT[:, dc, ti * 128:ti * 128 + 128], w1[:, dc, :])
                           for dc in range(NDC)]
                    mmg.append((self.ONESR[:], self.BROW[:, 0:512]))
                    self.mm([("XT", ti, dc) for dc in range(NDC)] +
                            [("WSL", 0), ("BROW",)], [("pf", 4)],
                            [(pf[4][:, :], mmg)])
                    self.op("S", [("pf", 4)], [bk],
                            lambda e, buf=buf: e.activation(buf[:, 0:512],
                                                            pf[4][:, :],
                                                            AF.Gelu))
                    for q in range(4):
                        self.op("T", [bk, ("IDN",)], [("pb", 0)],
                                lambda e, q=q, buf=buf: e.transpose(
                                    pbp[0][:, q * 128:q * 128 + 128],
                                    buf[:, q * 128:q * 128 + 128],
                                    self.IDN[:]))
                    self.op("V", [("pb", 0)],
                            [("MIDT", j)],
                            lambda e, j=j: e.tensor_copy(
                                self.MIDTf[:, j * 512:j * 512 + 512],
                                pbp[0][:, 0:512]))
                for dch in range(2):
                    if nc_ == 0:
                        self.dma("G", [], [("BROW",)],
                                 [(self.BROW[:, 512:1024],
                                   d["b2"][:, dch * 512:dch * 512 + 512])])
                    for j, ti in enumerate(tis):
                        mmg = [(MIDT[:, j, q, :],
                                w2[:, q, dch * 512:dch * 512 + 512])
                               for q in range(4)]
                        if nc_ == 0:
                            mmg.append((self.ONESR[:], self.BROW[:, 512:1024]))
                        self.mm([("MIDT", j), ("WSL", 1)] +
                                ([("BROW",)] if nc_ == 0 else []),
                                [("pf", j % 2)],
                                [(pf[j % 2][:, :], mmg)])
                        cs = slice(dch * 512, dch * 512 + 512)
                        self.op("V", [("pf", j % 2), ("X", ti)], [("X", ti)],
                                lambda e, ti=ti, cs=cs, j=j: e.tensor_tensor(
                                    X[:, ti, cs], X[:, ti, cs],
                                    pf[j % 2][:, :], ALU.add))

    def ffn(self, b, app):
        d = self.d
        X, XT, Bb = self.X, self.XT, self.Bb
        pf, pbp = self.pf, self.pb
        WS = self.WSL

        ars_keys = ([("SKW", 0), ("SKW", 1), ("RELS", 0), ("RELS", 1),
                     ("GT",), ("G2T",),
                     ("GTB",), ("G2TB",), ("RTT",), ("OBs",), ("FB2",),
                     ("IOT8",), ("IOT16",), ("AGW",), ("FGW",),
                     ("CTXE", 0), ("CTXE", 1),
                     ("REXT", 0), ("REXT", 1), ("REXT", 2), ("SCB2",)] +
                    [("GL", ti) for ti in range(NT)])
        self.use_ars(ars_keys)
        self.dma("SY", [], [("IOT8",), ("IOT16",)],
                 [(self.IOT8, d["IOT8"]), (self.IOT16, d["IOT16"])])
        self.dma("G", [], [("RTT",), ("OBs",), ("FB2",), ("AGW",), ("FGW",)], [
            (self.RTT[:, 0:129], d["relT"]),
            (self.OB, d["o_b"]),
            (self.FB2, d["ff_b2"]),
            (self.AGW, d["agate_w"].rearrange("(c p) n -> p c n", p=128)),
            (self.FGW, d["fgate_w"].rearrange("(c p) n -> p c n", p=128)),
        ])

        st0 = self.ln_stats("X", lambda ti: X[:, ti, :], 0, ("fL0", b, app))
        for ti in range(NT):
            self.normalize_tile(X[:, ti, :], st0, ti, Bb[:, ti, :],
                                [("X", ti)], [("B", ti)])
        self.ln_to_xt("B", lambda ti: Bb[:, ti, :], 32, ("fL1", b, app), 4, 5)

        war_keys = ([("QP", s) for s in range(2)] + [("KT1", tc) for tc in range(4)] +
                    [("VTok", ti) for ti in range(NT)] +
                    [("PB", r_, c4) for r_ in range(2) for c4 in range(4)] +
                    [("PBs", 0), ("PBs", 1)] +
                    [("PT", 0), ("PT", 1), ("OWp", 0), ("OWp", 1)])
        self.use_war(war_keys)

        for ti in range(NT):
            self.mm([("XT", ti, dc) for dc in range(NDC)] + [("AGW",)],
                    [("pf", 5)],
                    [(pf[5][:, 0:8],
                      [(XT[:, dc, ti * 128:ti * 128 + 128], self.AGW[:, dc, :])
                       for dc in range(NDC)])])
            self.op("V", [("pf", 5)], [("GL", ti)],
                    lambda e, ti=ti: e.tensor_copy(self.GL[:, ti * 8:ti * 8 + 8],
                                                   pf[5][:, 0:8]))
        self.gates_topk(self.GL[:, 0:128], 8, self.IOT8, self.GT, ("GT",),
                        [("GL", ti) for ti in range(NT)], b, app)
        self.op("V", [("GT",)], [("GTc",), ("SCB",)],
                lambda e: e.tensor_copy(self.SCB[:, 0:128], self.GT))
        for half in range(2):
            for j in range(8):
                ti = half * 8 + j
                self.op("T", [("GTc",), ("SCB",), ("IDN",)], [("pb", 0)],
                        lambda e, ti=ti, j=j: e.transpose(
                            pbp[0][0:8, j * 128:j * 128 + 128],
                            self.SCB[:, ti * 8:ti * 8 + 8], self.IDN[:]))
            self.op("V", [("pb", 0)], [("GTB",)],
                    lambda e, half=half: e.tensor_copy(
                        self.GTB[:, half * 1024:half * 1024 + 1024],
                        pbp[0][0:8, :]))

        self.dma("G", [], [("WSL", 0)], [
            (WS[:, 0, 0:1024].rearrange("p (c n) -> p c n", n=128),
             d["k_w"].rearrange("(c p) n -> p c n", p=128)),
            (WS[:, 0, 1024:2048].rearrange("p (c n) -> p c n", n=128),
             d["v_w"].rearrange("(c p) n -> p c n", p=128)),
        ])
        self.dma("G", [], [("BROW",)], [(self.BROW[:, 0:128], d["v_b"])])
        kw = WS[:, 0, 0:1024].rearrange("p (c n) -> p c n", n=128)
        vw = WS[:, 0, 1024:2048].rearrange("p (c n) -> p c n", n=128)
        for tc in range(4):
            xtr = [("XT", 4 * tc + i, dc) for i in range(4) for dc in range(NDC)]
            self.mm(xtr + [("WSL", 0)], [("pf", 0)],
                    [(pf[0][:, :],
                      [(kw[:, dc, :], XT[:, dc, tc * 512:tc * 512 + 512])
                       for dc in range(NDC)])])
            self.op("V", [("pf", 0), ("BQSr",)], [("KT1", tc)],
                    lambda e, tc=tc: e.tensor_scalar_add(
                        self.KT1[:, tc * 512:tc * 512 + 512], pf[0][:, :],
                        self.BQS[:, 24:25]))
        for ti in range(NT):
            self.mm([("XT", ti, dc) for dc in range(NDC)] +
                    [("WSL", 0), ("BROW",)], [("pf", 1)],
                    [(pf[1][:, 0:128],
                      [(XT[:, dc, ti * 128:ti * 128 + 128], vw[:, dc, :])
                       for dc in range(NDC)] +
                      [(self.ONESR[:], self.BROW[:, 0:128])])])
            self.op("V", [("pf", 1)], [("VTok", ti)],
                    lambda e, ti=ti: e.tensor_copy(self.VTok[:, ti, :],
                                                   pf[1][:, 0:128]))

        for ep in range(4):
            self.dma("G", [], [("WSL", 1)], [
                (WS[:, 1, e2 * 1024:e2 * 1024 + 1024]
                 .rearrange("p (c n) -> p c n", n=128),
                 d["q_w"][2 * ep + e2].rearrange("(c p) n -> p c n", p=128))
                for e2 in range(2)])
            self.dma("G", [], [("OWp", 0), ("OWp", 1)], [
                (self.OWp[:, e2, :], d["o_w"][2 * ep + e2]) for e2 in range(2)])
            for e2 in range(2):
                e_ = 2 * ep + e2
                qw = WS[:, 1, e2 * 1024:e2 * 1024 + 1024].rearrange(
                    "p (c n) -> p c n", n=128)
                for tc in range(4):
                    xtr = [("XT", 4 * tc + i, dc) for i in range(4)
                           for dc in range(NDC)]
                    self.mm(xtr + [("WSL", 1)], [("pf", 0)],
                            [(pf[0][:, :],
                              [(qw[:, dc, :], XT[:, dc, tc * 512:tc * 512 + 512])
                               for dc in range(NDC)])])
                    self.op("V", [("pf", 0), ("BQS2",)], [("QP", e2)],
                            lambda e, e2=e2, tc=tc, e_=e_: e.tensor_scalar(
                                self.QP[:, e2, tc * 512:tc * 512 + 512],
                                pf[0][:, :], float(128.0 ** -0.5),
                                self.BQS[:, 8 + e_:9 + e_],
                                ALU.mult, ALU.add))
            for ti in range(NT):
                lo = max(0, 128 * ti - 64)
                hi = min(T, 128 * ti + 192)
                jjoff = lo - (128 * ti - 64)
                for e2 in range(2):
                    e_ = 2 * ep + e2
                    rr = (ti * 2 + e2) % 2
                    RELS = self.RELS if rr == 0 else self.RELS2
                    SKW = self.SKW if rr == 0 else self.SKW2
                    PB = self.PB if rr == 0 else self.PB2
                    CTXE = self.CTXE if rr == 0 else self.CTXE2
                    sb_ = 16 + 8 * rr
                    st_ = 12 + 2 * rr
                    qtile = self.QP[:, e2, ti * 128:ti * 128 + 128]
                    self.mm([("QP", e2), ("RTT",)], [("pf", 5)],
                            [(pf[5][:, 0:129], [(qtile, self.RTT[:, 0:129])])])
                    self.op("V", [("pf", 5)], [("RELS", rr)],
                            lambda e, RELS=RELS: e.tensor_copy(
                                RELS[:, 0:129], pf[5][:, 0:129]))
                    self.op("V", [("RELS", rr), ("IOT16",)], [("REXT", 0)],
                            lambda e, RELS=RELS: e.tensor_scalar(
                                self.REXT[:, 0:127], self.IOT16[:, 0:127],
                                0.0, RELS[:, 0:1], ALU.mult, ALU.add))
                    self.op("V", [("RELS", rr)], [("REXT", 1)],
                            lambda e, RELS=RELS: e.tensor_copy(
                                self.REXT[:, 127:256], RELS[:, 0:129]))
                    self.op("V", [("RELS", rr), ("IOT16",)], [("REXT", 2)],
                            lambda e, RELS=RELS: e.tensor_scalar(
                                self.REXT[:, 256:384], self.IOT16[:, 0:128],
                                0.0, RELS[:, 128:129], ALU.mult, ALU.add))
                    self.dma("SY", [("REXT", 0), ("REXT", 1), ("REXT", 2)],
                             [("rexd", ti, e2)],
                             [(self.rext.ap()[ti * 2 + e2], self.REXT)])
                    skw_src = AP(self.rext, (ti * 2 + e2) * 49152 + 127 + jjoff,
                                 [[383, 128], [1, hi - lo]])
                    self.dma("SY", [("rexd", ti, e2)], [("SKW", rr)],
                             [(SKW[:, 0:hi - lo], skw_src)])
                    nsum = 0
                    sum_keys = []
                    for c4 in range(4):
                        s0, s1 = c4 * 512, c4 * 512 + 512
                        bank = c4 % 2
                        self.mm([("QP", e2), ("KT1", c4)], [("pf", bank)],
                                [(pf[bank][:, :],
                                  [(qtile, self.KT1[:, s0:s1])])])
                        segs = []
                        a0 = s0
                        for (rs, re_, kind) in [(s0, min(s1, lo), "L"),
                                                (max(s0, lo), min(s1, hi), "B"),
                                                (max(s0, hi), s1, "R")]:
                            if re_ > rs:
                                segs.append((rs, re_, kind))
                        for (rs, re_, kind) in segs:
                            pslice = pf[bank][:, rs - s0:re_ - s0]
                            if kind == "B":
                                self.op("V", [("pf", bank), ("SKW", rr)],
                                        [("pf", bank)],
                                        lambda e, pslice=pslice, rs=rs, re_=re_,
                                        lo=lo, SKW=SKW:
                                        e.tensor_tensor(
                                            pslice, pslice,
                                            SKW[:, rs - lo:re_ - lo],
                                            ALU.add))
                                bias = 0.0
                            elif kind == "L":
                                bias = RELS[:, 0:1]
                            else:
                                bias = RELS[:, 128:129]
                            sk = ("psum", rr, nsum)
                            self.op("S", [("pf", bank), ("RELS", rr)],
                                    [("PB", rr, c4), sk, ("SUMS", rr)],
                                    lambda e, pslice=pslice, rs=rs, re_=re_,
                                    bias=bias, nsum=nsum, PB=PB, sb_=sb_:
                                    e.activation(PB[:, rs:re_], pslice,
                                                 AF.Exp, bias=bias,
                                                 accum_out=self.SUMS[:, sb_ + nsum:sb_ + nsum + 1]))
                            sum_keys.append(sk)
                            nsum += 1
                    self.op("V", sum_keys, [("tsum", rr), ("SUMS", rr)],
                            lambda e, nsum=nsum, sb_=sb_, st_=st_: e.reduce_sum(
                                self.SUMS[:, st_:st_ + 1],
                                self.SUMS[:, sb_:sb_ + nsum],
                                axis=AX.X))
                    self.op("V", [("tsum", rr)], [("rsum", rr), ("SUMS", rr)],
                            lambda e, st_=st_: e.reciprocal(
                                self.SUMS[:, st_ + 1:st_ + 2],
                                self.SUMS[:, st_:st_ + 1]))
                    self.op("V", [("rsum", rr), ("GT",)],
                            [("pscale", rr), ("SUMS", rr)],
                            lambda e, ti=ti, e_=e_, st_=st_: e.tensor_tensor(
                                self.SUMS[:, st_ + 1:st_ + 2],
                                self.SUMS[:, st_ + 1:st_ + 2],
                                self.GT[:, ti * 8 + e_:ti * 8 + e_ + 1],
                                ALU.mult))
                    self.op("V", [("pscale", rr)] +
                            [("PB", rr, c4) for c4 in range(4)],
                            [("PBs", rr)],
                            lambda e, PB=PB, st_=st_: e.tensor_scalar_mul(
                                PB, PB, self.SUMS[:, st_ + 1:st_ + 2]))
                    for half in range(2):
                        for k in range(8):
                            kt = half * 8 + k
                            self.op("T", [("PBs", rr), ("IDN",)], [("pb", 1)],
                                    lambda e, kt=kt, k=k, PB=PB: e.transpose(
                                        pbp[1][:, k * 128:k * 128 + 128],
                                        PB[:, kt * 128:kt * 128 + 128],
                                        self.IDN[:]))
                        self.op("V", [("pb", 1)],
                                [("PT", half)],
                                lambda e, half=half: e.tensor_copy(
                                    self.PT[:, half * 1024:half * 1024 + 1024],
                                    pbp[1][:, :]))
                    self.mm([("PT", 0), ("PT", 1)] +
                            [("VTok", kt) for kt in range(NT)], [("pf", 2)],
                            [(pf[2][:, 0:128],
                              [(self.VTok[:, kt, :],
                                self.PT[:, kt * 128:kt * 128 + 128])
                               for kt in range(NT)])])
                    self.op("V", [("pf", 2)], [("CTXE", rr)],
                            lambda e, CTXE=CTXE: e.tensor_copy(
                                CTXE, pf[2][:, 0:128]))
                    mmo = [[(CTXE,
                             self.OWp[:, e2, dch * 512:dch * 512 + 512])]
                           for dch in range(2)]
                    if ep == 0 and e2 == 0:
                        for dch in range(2):
                            mmo[dch].append(
                                (self.GTB[:, ti * 128:ti * 128 + 128],
                                 self.OB[:, dch * 512:dch * 512 + 512]))
                    self.mm([("CTXE", rr), ("OWp", e2), ("GTB",), ("OBs",)],
                            [("pf", 3), ("pf", 4)],
                            [(pf[3][:, :], mmo[0]), (pf[4][:, :], mmo[1])],
                            accum=(e2 == 1), stop=(e2 == 1))
                for dch in range(2):
                    cs = slice(dch * 512, dch * 512 + 512)
                    self.op("V", [("pf", 3 + dch), ("B", ti)], [("B", ti)],
                            lambda e, ti=ti, cs=cs, dch=dch: e.tensor_tensor(
                                Bb[:, ti, cs], Bb[:, ti, cs],
                                pf[3 + dch][:, :], ALU.add))

        for ti in range(NT):
            self.op("V", [("B", ti), ("X", ti)], [("X", ti)],
                    lambda e, ti=ti: e.tensor_tensor(
                        X[:, ti, :], X[:, ti, :], Bb[:, ti, :], ALU.add))

        self.ln_to_xt("B", lambda ti: Bb[:, ti, :], 0, ("fL2", b, app), 6, 7)

        for ti in range(NT):
            self.mm([("XT", ti, dc) for dc in range(NDC)] + [("FGW",)],
                    [("pf", 5)],
                    [(pf[5][:, 0:16],
                      [(XT[:, dc, ti * 128:ti * 128 + 128], self.FGW[:, dc, :])
                       for dc in range(NDC)])])
            self.op("V", [("pf", 5)], [("GL", ti)],
                    lambda e, ti=ti: e.tensor_copy(
                        self.GL[:, ti * 16:ti * 16 + 16], pf[5][:, 0:16]))
        self.gates_topk(self.GL[:, 0:256], 16, self.IOT16, self.G2T, ("G2T",),
                        [("GL", ti) for ti in range(NT)], b, app,
                        iot_key=("IOT16",))
        self.op("V", [("G2T",)], [("G2c",), ("SCB",)],
                lambda e: e.tensor_copy(self.SCB[:, 0:256], self.G2T))
        for half in range(2):
            for j in range(8):
                ti = half * 8 + j
                self.op("T", [("G2c",), ("SCB",), ("IDN",)], [("pb", 0)],
                        lambda e, ti=ti, j=j: e.transpose(
                            pbp[0][0:16, j * 128:j * 128 + 128],
                            self.SCB[:, ti * 16:ti * 16 + 16], self.IDN[:]))
            self.op("V", [("pb", 0)], [("G2TB",)],
                    lambda e, half=half: e.tensor_copy(
                        self.G2TB[:, half * 1024:half * 1024 + 1024],
                        pbp[0][0:16, :]))

        self.use_war([("MIDT2",)])
        for e_ in range(16):
            self.dma("G", [], [("WSL", 0), ("WSL", 1)], [
                (WS[:, 0, :].rearrange("p (c n) -> p c n", n=512),
                 d["ff_w1"][e_].rearrange("(c p) n -> p c n", p=128)),
                (WS[:, 1, :].rearrange("p (c n) -> p c n", n=1024),
                 d["ff_w2"][e_].rearrange("(c p) n -> p c n", p=128)),
            ])
            self.dma("G", [], [("BROW",)],
                     [(self.BROW[:, 0:512], d["ff_b1"][e_:e_ + 1, :])])
            w1 = WS[:, 0, :].rearrange("p (c n) -> p c n", n=512)
            w2 = WS[:, 1, :].rearrange("p (c n) -> p c n", n=1024)
            for ti in range(NT):
                mmg = [(XT[:, dc, ti * 128:ti * 128 + 128], w1[:, dc, :])
                       for dc in range(NDC)]
                mmg.append((self.ONESR[:], self.BROW[:, 0:512]))
                self.mm([("XT", ti, dc) for dc in range(NDC)] +
                        [("WSL", 0), ("BROW",)], [("pf", 4)],
                        [(pf[4][:, :], mmg)])
                self.op("S", [("pf", 4), ("G2T",)], [("SCB",)],
                        lambda e, ti=ti, e_=e_: e.activation(
                            self.SCB[:, 0:512], pf[4][:, :], AF.Relu,
                            scale=self.G2T[:, ti * 16 + e_:ti * 16 + e_ + 1]))
                for q in range(4):
                    self.op("T", [("SCB",), ("IDN",)], [("pb", 0)],
                            lambda e, q=q: e.transpose(
                                pbp[0][:, q * 128:q * 128 + 128],
                                self.SCB[:, q * 128:q * 128 + 128],
                                self.IDN[:]))
                self.op("V", [("pb", 0)],
                        [("MIDT2",)],
                        lambda e: e.tensor_copy(
                            self.MIDT2f[:], pbp[0][:, 0:512]))
                for dch in range(2):
                    mmg = [(self.MIDT2[:, q, :],
                            w2[:, q, dch * 512:dch * 512 + 512])
                           for q in range(4)]
                    if e_ == 0:
                        mmg.append((self.G2TB[:, ti * 128:ti * 128 + 128],
                                    self.FB2[:, dch * 512:dch * 512 + 512]))
                    self.mm([("MIDT2",), ("WSL", 1), ("G2TB",), ("FB2",)],
                            [("pf", dch)],
                            [(pf[dch][:, :], mmg)])
                    cs = slice(dch * 512, dch * 512 + 512)
                    self.op("V", [("pf", dch), ("X", ti)], [("X", ti)],
                            lambda e, ti=ti, cs=cs, dch=dch: e.tensor_tensor(
                                X[:, ti, cs], X[:, ti, cs], pf[dch][:, :],
                                ALU.add))

    def gates_topk(self, logits_ap, ne, iot_ap, out_ap, out_key, logit_keys,
                   b, app, iot_key=("IOT8",)):
        S2 = self.SCR2
        nt = NT
        w = nt * ne
        l3 = logits_ap.rearrange("p (t e) -> p t e", e=ne)
        rk = logit_keys

        def v(eng, reads, writes, fn):
            self.op(eng, reads + [("SCR2",), iot_key],
                    writes + [("SCR2",)], fn)
        v("V", rk, [("g_m1",)],
          lambda e: e.reduce_max(S2[:, 0:16], l3, axis=AX.X))
        v("V", rk + [("g_m1",)], [("g_eq",)],
          lambda e: e.tensor_tensor(
              S2[:, 128:128 + w].rearrange("p (t e) -> p t e", e=ne), l3,
              S2[:, 0:16].unsqueeze(2).broadcast_to((128, nt, ne)),
              ALU.is_equal))
        v("V", [("g_eq",)], [("g_t1",)],
          lambda e: e.tensor_tensor(S2[:, 384:384 + w], S2[:, 128:128 + w],
                                    iot_ap[:, 0:w], ALU.mult))
        v("V", [("g_eq",), ("g_t1",)], [("g_t2",)],
          lambda e: e.scalar_tensor_tensor(S2[:, 384:384 + w],
                                           S2[:, 128:128 + w], -float(ne),
                                           S2[:, 384:384 + w],
                                           ALU.mult, ALU.add))
        v("V", [("g_t2",)], [("g_i1",)],
          lambda e: e.tensor_reduce(S2[:, 16:32],
                                    S2[:, 384:384 + w].rearrange(
                                        "p (t e) -> p t e", e=ne),
                                    axis=AX.X, op=ALU.min))
        v("V", [("g_i1",)], [("g_i1b",)],
          lambda e: e.tensor_scalar_add(S2[:, 16:32], S2[:, 16:32], float(ne)))
        v("V", [("g_i1b",)], [("g_m1k",)],
          lambda e: e.tensor_tensor(
              S2[:, 128:128 + w].rearrange("p (t e) -> p t e", e=ne),
              iot_ap[:, 0:w].rearrange("p (t e) -> p t e", e=ne),
              S2[:, 16:32].unsqueeze(2).broadcast_to((128, nt, ne)),
              ALU.is_equal))
        v("V", rk + [("g_m1k",)], [("g_l2",)],
          lambda e: e.scalar_tensor_tensor(S2[:, 384:384 + w],
                                           S2[:, 128:128 + w], -1e30,
                                           logits_ap, ALU.mult, ALU.add))
        v("V", [("g_l2",)], [("g_m2",)],
          lambda e: e.reduce_max(S2[:, 32:48],
                                 S2[:, 384:384 + w].rearrange(
                                     "p (t e) -> p t e", e=ne), axis=AX.X))
        v("V", [("g_l2",), ("g_m2",)], [("g_eq2",)],
          lambda e: e.tensor_tensor(
              S2[:, 640:640 + w].rearrange("p (t e) -> p t e", e=ne),
              S2[:, 384:384 + w].rearrange("p (t e) -> p t e", e=ne),
              S2[:, 32:48].unsqueeze(2).broadcast_to((128, nt, ne)),
              ALU.is_equal))
        v("V", [("g_eq2",)], [("g_t1b",)],
          lambda e: e.tensor_tensor(S2[:, 384:384 + w], S2[:, 640:640 + w],
                                    iot_ap[:, 0:w], ALU.mult))
        v("V", [("g_eq2",), ("g_t1b",)], [("g_t2b",)],
          lambda e: e.scalar_tensor_tensor(S2[:, 384:384 + w],
                                           S2[:, 640:640 + w], -float(ne),
                                           S2[:, 384:384 + w],
                                           ALU.mult, ALU.add))
        v("V", [("g_t2b",)], [("g_i2",)],
          lambda e: e.tensor_reduce(S2[:, 48:64],
                                    S2[:, 384:384 + w].rearrange(
                                        "p (t e) -> p t e", e=ne),
                                    axis=AX.X, op=ALU.min))
        v("V", [("g_i2",)], [("g_i2b",)],
          lambda e: e.tensor_scalar_add(S2[:, 48:64], S2[:, 48:64], float(ne)))
        v("V", [("g_i2b",)], [("g_m2k",)],
          lambda e: e.tensor_tensor(
              S2[:, 640:640 + w].rearrange("p (t e) -> p t e", e=ne),
              iot_ap[:, 0:w].rearrange("p (t e) -> p t e", e=ne),
              S2[:, 48:64].unsqueeze(2).broadcast_to((128, nt, ne)),
              ALU.is_equal))
        v("V", [("g_m1",), ("g_m2",)], [("g_dm",)],
          lambda e: e.tensor_tensor(S2[:, 64:80], S2[:, 0:16], S2[:, 32:48],
                                    ALU.subtract))
        v("S", [("g_dm",)], [("g_g1",)],
          lambda e: e.activation(S2[:, 64:80], S2[:, 64:80], AF.Sigmoid))
        v("V", [("g_g1",)], [("g_g2",)],
          lambda e: e.tensor_scalar(S2[:, 80:96], S2[:, 64:80], -1.0, 1.0,
                                    ALU.mult, ALU.add))
        v("V", [("g_m1k",), ("g_g1",)], [("g_p1",)],
          lambda e: e.tensor_tensor(
              out_ap.rearrange("p (t e) -> p t e", e=ne),
              S2[:, 128:128 + w].rearrange("p (t e) -> p t e", e=ne),
              S2[:, 64:80].unsqueeze(2).broadcast_to((128, nt, ne)),
              ALU.mult))
        v("V", [("g_m2k",), ("g_g2",)], [("g_p2",)],
          lambda e: e.tensor_tensor(
              S2[:, 640:640 + w].rearrange("p (t e) -> p t e", e=ne),
              S2[:, 640:640 + w].rearrange("p (t e) -> p t e", e=ne),
              S2[:, 80:96].unsqueeze(2).broadcast_to((128, nt, ne)),
              ALU.mult))
        v("V", [("g_p1",), ("g_p2",)], [out_key],
          lambda e: e.tensor_tensor(out_ap, out_ap, S2[:, 640:640 + w],
                                    ALU.add))

    def attn_b(self, b):
        d = self.d
        X, XT = self.X, self.XT
        pf, pbp = self.pf, self.pb
        WS = self.WSL

        ars_keys = [("SLb", cg) for cg in range(4)] + [
            ("MASKB",), ("SEGC",), ("LTX",), ("CSKT",), ("CUMK",), ("CUMKB",),
            ("CSVB",), ("CUMVB",), ("CV2",), ("CRS2",), ("PTT", 0), ("PTT", 1),
            ("PTTa", 0), ("PTTa", 1), ("PTTb", 0), ("PTTb", 1), ("SMS",),
            ("SCB2",)]
        self.use_ars(ars_keys)
        self.dma("SY", [], [("MASKB",)], [(self.MASKB, d["MASKB"])])
        self.dma("G", [], [("SEGC",), ("LTX",)], [
            (self.SEGC, d["SEGC"]), (self.LTX, d["LTX"]),
        ])
        self.op("V", [], [("PTT", 0)], lambda e: e.memset(self.PTT, 0.0))
        self.op("V", [], [("PTT", 1)], lambda e: e.memset(self.PTT2, 0.0))
        self.op("V", [], [("CRS2",), ("SCB2",)],
                lambda e: e.memset(self.CRS2, 0.0))

        st0 = self.ln_stats("X", lambda ti: X[:, ti, :], 0, ("bL0", b))
        for ti in range(NT):
            self.normalize_tile(X[:, ti, :], st0, ti, self.SCB[:],
                                [("X", ti)], [("SCB",)])
            self.transpose_to_xt(ti, None, None, ("SCB",))

        war_keys = ([("QTb", tc) for tc in range(4)] +
                    [("KTb", tc) for tc in range(4)] +
                    [("VGb", ti) for ti in range(NT)] +
                    [("PLb", cg) for cg in range(4)] +
                    [("CRSB", h2_, half_, i_) for h2_ in range(2)
                     for half_ in range(2) for i_ in range(2)])
        self.use_war(war_keys)
        OUTB = self.Bb[:]

        for hg in range(8):
            self.dma("G", [], [("WSL", 0)], [
                (WS[:, 0, i * 1024:i * 1024 + 1024]
                 .rearrange("p (c n) -> p c n", n=128),
                 d["W_qkv"].rearrange("(c p) n -> p c n", p=128)
                 [:, :, i * 1024 + hg * 128: i * 1024 + hg * 128 + 128])
                for i in range(3)])
            wq = WS[:, 0, 0:1024].rearrange("p (c n) -> p c n", n=128)
            wk = WS[:, 0, 1024:2048].rearrange("p (c n) -> p c n", n=128)
            wv = WS[:, 0, 2048:3072].rearrange("p (c n) -> p c n", n=128)
            for tc in range(4):
                xtr = [("XT", 4 * tc + i, dc) for i in range(4)
                       for dc in range(NDC)]
                self.mm(xtr + [("WSL", 0)], [("pf", 0)],
                        [(pf[0][:, :],
                          [(wq[:, dc, :], XT[:, dc, tc * 512:tc * 512 + 512])
                           for dc in range(NDC)])])
                self.op("V", [("pf", 0)], [("QTb", tc)],
                        lambda e, tc=tc: e.tensor_scalar_mul(
                            self.QTb[:, tc * 512:tc * 512 + 512], pf[0][:, :],
                            0.125))
                self.mm(xtr + [("WSL", 0)], [("pf", 1)],
                        [(pf[1][:, :],
                          [(wk[:, dc, :], XT[:, dc, tc * 512:tc * 512 + 512])
                           for dc in range(NDC)])])
                self.op("V", [("pf", 1)], [("KTb", tc)],
                        lambda e, tc=tc: e.tensor_copy(
                            self.KTb[:, tc * 512:tc * 512 + 512], pf[1][:, :]))
            for ti in range(NT):
                self.mm([("XT", ti, dc) for dc in range(NDC)] + [("WSL", 0)],
                        [("pf", 2)],
                        [(pf[2][:, 0:128],
                          [(XT[:, dc, ti * 128:ti * 128 + 128], wv[:, dc, :])
                           for dc in range(NDC)])])
                self.op("V", [("pf", 2)], [("VGb", ti)],
                        lambda e, ti=ti: e.tensor_copy(
                            self.VGb[:, ti, :], pf[2][:, 0:128]))
            self.op("V", [("KTb", tc) for tc in range(4)], [("CSKT",)],
                    lambda e: e.reduce_sum(
                        self.CSKT[:, 0:32],
                        self.KTb.rearrange("p (c k) -> p c k", c=32),
                        axis=AX.X))
            self.mm([("VGb", ti) for ti in range(NT)] + [("SEGC",)],
                    [("pf", 5)],
                    [(pf[5][0:32, 0:128],
                      [(self.SEGC[:, ti * 32:ti * 32 + 32], self.VGb[:, ti, :])
                       for ti in range(NT)])])
            self.op("V", [("pf", 5)], [("CSVB",)],
                    lambda e: e.tensor_copy(self.CSVB, pf[5][0:32, 0:128]))
            self.op("V", [("CUMK",)], [("CUMKz",)],
                    lambda e: e.memset(self.CUMK[:, 0:1], 0.0))
            self.op("V", [("CSKT",), ("CUMKz",), ("ONEC",)], [("CUMK",)],
                    lambda e: e.tensor_tensor_scan(
                        self.CUMK[:, 1:33],
                        self.SUMS[:, 30:31].broadcast_to((128, 32)),
                        self.CSKT[:, 0:32], 0.0, ALU.mult, ALU.add))
            self.op("V", [("CUMK",)], [("CUMKB",)],
                    lambda e: e.tensor_copy(self.CUMKB[:, 0:33],
                                            self.CUMK[:, 0:33]))
            self.mm([("CSVB",), ("LTX",)], [("pf", 5)],
                    [(pf[5][0:32, 0:128], [(self.LTX, self.CSVB)])])
            self.op("V", [("pf", 5)], [("CUMVB",)],
                    lambda e: e.tensor_copy(self.CUMVB, pf[5][0:32, 0:128]))
            self.dma("SY", [("CUMVB",)], [("cvd",)],
                     [(self.cvd.ap().rearrange("o (c f) -> (o c) f", c=32),
                       self.CUMVB)])
            cv2_src = AP(self.cvd, 0, [[128, 2], [256, 16], [1, 128]])
            self.dma("SY", [("cvd",)], [("CV2",), ("SCB2",)],
                     [(self.CV2.rearrange("p (t f) -> p t f", t=16), cv2_src)])
            for h2 in range(2):
                r0 = 64 * h2
                for half in range(2):
                    grp = []
                    for cc in range(8):
                        c = half * 16 + cc
                        grp.append((pf[h2][0:1, cc * 64:cc * 64 + 64],
                                    [(self.CUMKB[r0:r0 + 64, c:c + 1],
                                      self.QTb[r0:r0 + 64, c * 64:c * 64 + 64])]))
                    for cc in range(8):
                        c = half * 16 + 8 + cc
                        grp.append((pf[2 + h2][0:1, cc * 64:cc * 64 + 64],
                                    [(self.CUMKB[r0:r0 + 64, c:c + 1],
                                      self.QTb[r0:r0 + 64, c * 64:c * 64 + 64])]))
                    self.mm([("CUMKB",), ("QTb", half * 2), ("QTb", half * 2 + 1)],
                            [("pf", h2), ("pf", 2 + h2)], grp)
                    self.op("S", [("pf", h2), ("CONSTC",)],
                            [("CRSB", h2, half, 0)],
                            lambda e, h2=h2, half=half: e.activation(
                                self.CRSB[:, h2 * 2048 + half * 1024:
                                          h2 * 2048 + half * 1024 + 512],
                                pf[h2][0:1, :], AF.Sigmoid,
                                scale=self.SUMS[0:1, 11:12]))
                    self.op("S", [("pf", 2 + h2), ("CONSTC",)],
                            [("CRSB", h2, half, 1)],
                            lambda e, h2=h2, half=half: e.activation(
                                self.CRSB[:, h2 * 2048 + half * 1024 + 512:
                                          h2 * 2048 + half * 1024 + 1024],
                                pf[2 + h2][0:1, :], AF.Sigmoid,
                                scale=self.SUMS[0:1, 11:12]))
            for cg in range(4):
                grp = []
                for cc in range(8):
                    c = cg * 8 + cc
                    for h2 in range(2):
                        r0 = 64 * h2
                        grp.append((pf[2][r0:r0 + 64, cc * 64:cc * 64 + 64],
                                    [(self.QTb[r0:r0 + 64, c * 64:c * 64 + 64],
                                      self.KTb[r0:r0 + 64, c * 64:c * 64 + 64])]))
                self.mm([("QTb", cg), ("KTb", cg)], [("pf", 2)], grp)
                self.op("V", [("pf", 2), ("MASKB",)], [("SLb", cg)],
                        lambda e, cg=cg: e.tensor_tensor(
                            self.SLb[:, cg * 512:cg * 512 + 512],
                            pf[2][:, :], self.MASKB, ALU.add))
            self.op("S", [("SLb", cg) for cg in range(4)], [("PLball",)],
                    lambda e: e.activation(self.PLb, self.SLb, AF.Exp))
            self.op("V", [("PLball",)], [("SMS",)],
                    lambda e: e.reduce_sum(
                        self.SMS[:, 0:32],
                        self.PLb.rearrange("p (c k) -> p c k", c=32),
                        axis=AX.X))
            self.op("V", [("SMS",)], [("SMSr",)],
                    lambda e: e.reciprocal(self.SMS[:, 0:32], self.SMS[:, 0:32]))
            self.op("V", [("SMSr",), ("PLball",)], [("PLball",)],
                    lambda e: e.tensor_tensor(
                        self.PLb.rearrange("p (c k) -> p c k", c=32),
                        self.PLb.rearrange("p (c k) -> p c k", c=32),
                        self.SMS[:, 0:32].unsqueeze(2)
                        .broadcast_to((128, 32, 64)), ALU.mult))
            self.dma("SY", [("CRSB", h2_, half_, i_) for h2_ in range(2)
                            for half_ in range(2) for i_ in range(2)],
                     [("crsd",)],
                     [(self.crsd.ap(), self.WARt[0:1, 8192:12288])])
            for h2 in range(2):
                r0 = 64 * h2
                crs_src = AP(self.crsd, h2 * 2048,
                             [[128, 16], [1, 64]])
                crs_src2 = AP(self.crsd, h2 * 2048 + 64,
                              [[128, 16], [1, 64]])
                self.dma("SY", [("crsd",)], [("CRS2",), ("SCB2",)], [
                    (self.CRS2[0:1, :].rearrange("p (t f) -> p t f", t=16)
                     [:, :, 0:64], crs_src),
                    (self.CRS2[1:2, :].rearrange("p (t f) -> p t f", t=16)
                     [:, :, 64:128], crs_src2),
                ])
                for ti in range(NT):
                    ca, cb = 2 * ti, 2 * ti + 1
                    ptt = self.PTT if ti % 2 == 0 else self.PTT2
                    pk = ("PTT", ti % 2)
                    cpo = (ti % 2) * 256
                    self.op("T", [("PLball",), ("IDN",)], [("pb", 1)],
                            lambda e, ca=ca, r0=r0, cpo=cpo: e.transpose(
                                pbp[1][0:64, cpo:cpo + 64],
                                self.PLb[r0:r0 + 64, ca * 64:ca * 64 + 64],
                                self.IDN[r0:r0 + 64, r0:r0 + 64]))
                    self.op("T", [("PLball",), ("IDN",)], [("pb", 1)],
                            lambda e, cb=cb, r0=r0, cpo=cpo: e.transpose(
                                pbp[1][64:128, cpo + 64:cpo + 128],
                                self.PLb[r0:r0 + 64, cb * 64:cb * 64 + 64],
                                self.IDN[r0:r0 + 64, r0:r0 + 64]))
                    self.op("V", [("pb", 1)], [pk, ("PTTa", ti % 2)],
                            lambda e, ptt=ptt, cpo=cpo: e.tensor_copy(
                                ptt[0:64, 0:64], pbp[1][0:64, cpo:cpo + 64]))
                    self.op("V", [("pb", 1)], [pk, ("PTTb", ti % 2)],
                            lambda e, ptt=ptt, cpo=cpo: e.tensor_copy(
                                ptt[64:128, 64:128],
                                pbp[1][64:128, cpo + 64:cpo + 128]))
                    po = (ti % 8) * 64
                    self.mm([pk, ("VGb", ti), ("CRS2",),
                             ("CV2",)],
                            [("pf", 3)],
                            [(pf[3][:, po:po + 64],
                              [(ptt, self.VGb[:, ti, r0:r0 + 64]),
                               (self.CRS2[:, ti * 128:ti * 128 + 128],
                                self.CV2[:, ti * 128 + r0:ti * 128 + r0 + 64]
                                if False else
                                self.CV2.rearrange("p (t f) -> p t f", t=16)
                                [:, ti, r0:r0 + 64])])])
                    if ti % 8 == 7:
                        for tj in range(ti - 7, ti + 1):
                            self.op("V", [("pf", 3)], [("B", tj)],
                                    lambda e, tj=tj, hg=hg, r0=r0: e.tensor_copy(
                                        OUTB[:, tj, hg * 128 + r0:
                                             hg * 128 + r0 + 64],
                                        pf[3][:, (tj % 8) * 64:(tj % 8) * 64 + 64]))

        self.dma("G", [], [("WSL", 0), ("WSL", 1)], [
            (WS[:, 0, :].rearrange("p (c n) -> p c n", n=1024),
             d["W_out"].rearrange("(c p) n -> p c n", p=128)[:, 0:4, :]),
            (WS[:, 1, :].rearrange("p (c n) -> p c n", n=1024),
             d["W_out"].rearrange("(c p) n -> p c n", p=128)[:, 4:8, :]),
        ])
        wouts = [WS[:, 0, :].rearrange("p (c n) -> p c n", n=1024),
                 WS[:, 1, :].rearrange("p (c n) -> p c n", n=1024)]
        for ti in range(NT):
            self.op("V", [("B", ti)], [("SCB",)],
                    lambda e, ti=ti: e.tensor_copy(self.SCB[:],
                                                   OUTB[:, ti, :]))
            for dc in range(NDC):
                self.op("T", [("SCB",), ("IDN",)], [("pb", 0)],
                        lambda e, dc=dc: e.transpose(
                            self.pb[0][:, dc * 128:dc * 128 + 128],
                            self.SCB[:, dc * 128:dc * 128 + 128], self.IDN[:]))
            self.op("V", [("pb", 0)], [("OBT",), ("SCB",)],
                    lambda e: e.tensor_copy(self.SCB[:], self.pb[0][:, :]))
            obt = self.SCB[:].rearrange("p (c n) -> p c n", c=8)
            for dch in range(2):
                self.mm([("OBT",), ("SCB",), ("WSL", 0), ("WSL", 1)],
                        [("pf", dch)],
                        [(pf[dch][:, :],
                          [(obt[:, dc, :],
                            wouts[dc // 4][:, dc % 4, dch * 512:dch * 512 + 512])
                           for dc in range(NDC)])])
                cs = slice(dch * 512, dch * 512 + 512)
                self.op("V", [("pf", dch), ("X", ti)], [("X", ti)],
                        lambda e, ti=ti, cs=cs, dch=dch: e.tensor_tensor(
                            X[:, ti, cs], X[:, ti, cs], pf[dch][:, :],
                            ALU.add))


def build_constants():
    eye = np.eye(128, dtype=np.float32).astype(BF)
    onesr = np.ones((1, 128), np.float32).astype(BF)
    i = np.arange(128)
    m_cur = np.where(i[:, None] >= i[None, :], 0.0, -1e9).astype(np.float32)
    m_prev = np.where(i[None, :] > i[:, None], 0.0, -1e9).astype(np.float32)
    maska = np.concatenate([m_prev, m_cur], axis=1)
    q = np.arange(64)
    mb = np.where(q[None, :] > q[:, None], -65000.0, 0.0).astype(np.float32)
    mb2 = np.concatenate([mb, mb], axis=0)
    maskb = np.tile(mb2, (1, 8))
    iot8 = np.tile(np.arange(8, dtype=np.float32)[None, :], (128, 16))
    iot16 = np.tile(np.arange(16, dtype=np.float32)[None, :], (128, 16))
    segc = np.zeros((128, 512), np.float32)
    for ti in range(16):
        for p in range(128):
            segc[p, ti * 32 + 2 * ti + p // 64] = 1.0 / 64.0
    ltx = np.tril(np.ones((32, 32), np.float32), k=-1).T.copy()
    ltx = np.triu(np.ones((32, 32), np.float32), k=1) * 0.5
    onh = np.concatenate([
        np.concatenate([np.ones((1, 64)), np.zeros((1, 64))], axis=1),
        np.concatenate([np.zeros((1, 64)), np.ones((1, 64))], axis=1),
    ], axis=1).astype(np.float32)
    return {
        "IDN": eye, "ONESR": onesr, "MASKA": maska, "MASKB": maskb,
        "IOT8": iot8, "IOT16": iot16,
        "SEGC": segc.astype(BF), "LTX": ltx.astype(BF), "ONH": onh.astype(BF),
    }


def prep_inputs(inputs):
    g = lambda k: np.asarray(inputs[k])
    p = "L0_"
    cm = {}
    for nm in ["WQ", "WK", "WV", "WO", "Wg", "W1", "W2", "Bw", "Cw"]:
        cm[nm] = g(p + "attn_a_" + nm)
    cm["A"] = g(p + "attn_a_A").reshape(64, 1)
    lnp = np.zeros((128, 64), np.float32)
    for i, nm in enumerate(["attn_a_ln1_g", "attn_a_ln1_b", "attn_a_ln2_g",
                            "attn_a_ln2_b", "ffn_ln1_w", "ffn_ln1_b",
                            "ffn_ln2_w", "ffn_ln2_b"]):
        lnp[:, 8 * i:8 * i + 8] = g(p + nm).reshape(8, 128).T
    cm["LNPa"] = lnp
    cm["bQT"] = g(p + "attn_a_bQ").reshape(8, 128).T.copy()
    cm["bKT"] = g(p + "attn_a_bK").reshape(8, 128).T.copy()
    cm["bV"] = g(p + "attn_a_bV").reshape(1, D)
    cm["bO"] = g(p + "attn_a_bO").reshape(1, D)
    cm["bg"] = g(p + "attn_a_bg").reshape(1, D)
    cm["b1"] = g(p + "attn_a_b1").reshape(1, 4 * D)
    cm["b2"] = g(p + "attn_a_b2").reshape(1, D)
    cm["W_qkv"] = g(p + "attn_b_W_qkv")
    cm["W_out"] = g(p + "attn_b_W_out")
    cm["agate_w"] = g(p + "ffn_attn_gate_w")
    cm["q_w"] = g(p + "ffn_q_w")
    cm["qbT"] = g(p + "ffn_q_b").T.copy()
    cm["k_w"] = g(p + "ffn_k_w")
    cm["k_b"] = g(p + "ffn_k_b").reshape(128, 1)
    cm["v_w"] = g(p + "ffn_v_w")
    cm["v_b"] = g(p + "ffn_v_b").reshape(1, 128)
    cm["o_w"] = g(p + "ffn_o_w")
    cm["o_b"] = g(p + "ffn_o_b")
    cm["relT"] = g(p + "ffn_rel_table").T.copy()
    cm["fgate_w"] = g(p + "ffn_ff_gate_w")
    cm["ff_w1"] = g(p + "ffn_ff_w1")
    cm["ff_b1"] = g(p + "ffn_ff_b1")
    cm["ff_w2"] = g(p + "ffn_ff_w2")
    cm["ff_b2"] = g(p + "ffn_ff_b2")
    cm = {k: np.ascontiguousarray(v, dtype=np.float32) for k, v in cm.items()}
    cm.update(build_constants())
    x = np.asarray(inputs["L0_attn_a_x"])
    in_maps = []
    for c in range(NCORES):
        m = dict(cm)
        m["x"] = np.ascontiguousarray(x[c * NB:(c + 1) * NB])
        in_maps.append(m)
    return in_maps


def kernel(**inputs):
    in_maps = prep_inputs(inputs)
    bld = Builder()
    nc = bld.build()
    res = run_bass_kernel_spmd(nc, in_maps, core_ids=list(range(NCORES)))
    out = np.empty((32, T, D), np.float32)
    for c in range(NCORES):
        out[c * NB:(c + 1) * NB] = res.results[c]["y"]
    return out


# version 04
# speedup vs baseline: 1.1023x; 1.0826x over previous
import numpy as np
import ml_dtypes
import concourse.bass as bass
import concourse.mybir as mybir
from concourse.ap import AP
from concourse.bass_utils import run_bass_kernel_spmd

F32 = mybir.dt.float32
BF16 = mybir.dt.bfloat16
AF = mybir.ActivationFunctionType
ALU = mybir.AluOpType
AX = mybir.AxisListType

NCORES = 8
NB = 4
T = 2048
D = 1024
NT = 16
NDC = 8
EPS = 1e-5
BF = ml_dtypes.bfloat16


class Sched:
    ENGINES = ("T", "V", "S", "G", "SY")
    QUEUES = ("SY", "G")
    DMA_DEPTH = 8

    def __init__(self):
        self.count = {e: 0 for e in self.ENGINES}
        self.ops = {e: [] for e in self.ENGINES}
        self.last_write = {}
        self.readers = {}
        self.sem_names = list(self.ENGINES) + [
            f"{q}{i}" for q in self.QUEUES for i in range(self.DMA_DEPTH)]
        self.waited = {e: {} for e in self.ENGINES}
        self.dma_idx = {q: 0 for q in self.QUEUES}
        self.dma_total = {s: 0 for s in self.sem_names}
        self._sems = None

    def _deps(self, eng, reads, writes):
        need = {}
        for k in reads:
            w = self.last_write.get(k)
            if w is not None:
                need[w[0]] = max(need.get(w[0], 0), w[1])
        for k in writes:
            w = self.last_write.get(k)
            if w is not None:
                need[w[0]] = max(need.get(w[0], 0), w[1])
            for (re, rv) in self.readers.get(k, ()):
                need[re] = max(need.get(re, 0), rv)
        waits = []
        for pe, val in sorted(need.items()):
            if pe == eng and pe == "T":
                continue
            if self.waited[eng].get(pe, 0) >= val:
                continue
            waits.append((pe, val))
            self.waited[eng][pe] = val
        return waits

    def op(self, eng, reads, writes, fn, n_inc=1):
        waits = self._deps(eng, reads, writes)
        after = self.count[eng] + n_inc
        self.ops[eng].append((waits, fn, n_inc, eng))
        self.count[eng] = after
        for k in writes:
            self.last_write[k] = (eng, after)
            self.readers[k] = []
        for k in reads:
            self.readers.setdefault(k, []).append((eng, after))

    def dma_op(self, q, reads, writes, fn, ndma):
        waits = self._deps(q, reads, writes)
        s = f"{q}{self.dma_idx[q] % self.DMA_DEPTH}"
        self.dma_idx[q] += 1
        if self.dma_total[s] > 0 and self.waited[q].get(s, 0) < self.dma_total[s]:
            waits.append((s, self.dma_total[s]))
            self.waited[q][s] = self.dma_total[s]
        self.dma_total[s] += 16 * ndma
        tok = (s, self.dma_total[s])
        self.ops[q].append((waits, fn, s, q))
        for k in writes:
            self.last_write[k] = tok
            self.readers[k] = []
        for k in reads:
            self.readers.setdefault(k, []).append(tok)

    def alias(self, new_keys, old_keys):
        deps = []
        for k in old_keys:
            w = self.last_write.pop(k, None)
            if w is not None:
                deps.append(w)
            deps.extend(self.readers.pop(k, ()))
        byeng = {}
        for (e, v) in deps:
            byeng[e] = max(byeng.get(e, 0), v)
        for k in new_keys:
            self.readers[k] = [(e, v) for e, v in byeng.items()]
            self.last_write.pop(k, None)

    def emit(self, nc, block):
        sems = {s: nc.alloc_semaphore("sem_" + s) for s in self.sem_names}
        self._sems = sems

        def run_stream(e):
            def body(engine):
                for (waits, fn, extra, eng) in self.ops[e]:
                    for (pe, val) in waits:
                        engine.wait_ge(sems[pe], val)
                    if isinstance(extra, str):
                        fn(engine, sems[extra])
                    else:
                        last = fn(engine)
                        if last is not None:
                            last.then_inc(sems[e], extra)
            return body

        block.tensor(run_stream("T"))
        block.vector(run_stream("V"))
        block.scalar(run_stream("S"))
        block.gpsimd(run_stream("G"))
        block.sync(run_stream("SY"))
        return sems


class Builder:
    def __init__(self, taps=(), nbatches=NB, stages=(1, 1, 1, 1), substop=None):
        self.substop = substop
        self.nbatches = nbatches
        self.stages = stages
        self.taps = set(taps)
        self.nc = bass.Bass("TRN2", target_bir_lowering=False)
        self.sc = Sched()
        self.tap_names = []
        self.war_keys = [("WARinit",)]
        self.ars_keys = [("ARSinit",)]
        self.brow_keys = [("BROW",)]

    def op(self, eng, reads, writes, fn, n_inc=1):
        self.sc.op(eng, reads, writes, fn, n_inc)

    def dma(self, q, reads, writes, pairs, slow=False):
        def fn(e, sem):
            for (o, i) in pairs:
                if slow:
                    e.dma_start(out=o, in_=i,
                                allow_slow_non_contiguous=True).then_inc(sem, 16)
                else:
                    e.dma_start(out=o, in_=i).then_inc(sem, 16)
        self.sc.dma_op(q, reads, writes, fn, len(pairs))

    def mm(self, reads, writes, groups, accum=False, stop=True):
        def fn(e):
            last = None
            for (ps, mms) in groups:
                n = len(mms)
                for i, (l, r) in enumerate(mms):
                    last = e.matmul(ps, l, r,
                                    start=(i == 0 and not accum),
                                    stop=(i == n - 1 and stop))
            return last
        self.sc.op("T", reads, writes, fn)

    def use_war(self, new_keys):
        self.sc.alias(new_keys, self.war_keys)
        self.war_keys = list(new_keys)

    def use_brow(self, new_keys):
        self.sc.alias(new_keys, self.brow_keys)
        self.brow_keys = list(new_keys)

    def use_ars(self, new_keys):
        self.sc.alias(new_keys, self.ars_keys)
        self.ars_keys = list(new_keys)

    def tap(self, label, ap, reads, shape, dtype=F32):
        if label not in self.taps:
            return
        out = self.nc.dram_tensor("tap_" + label, list(shape), dtype,
                                  kind="ExternalOutput").ap()
        self.tap_names.append("tap_" + label)
        self.dma("SY", reads, [], [(out, ap)])

    def build(self):
        nc = self.nc
        d = {}

        def din(name, shape, dt_=F32):
            d[name] = nc.dram_tensor(name, list(shape), dt_,
                                     kind="ExternalInput").ap()
        din("x", (NB, T, D))
        for nm, sh in [
            ("WQ", (D, D)), ("WK", (D, D)), ("WV", (D, D)),
            ("WO", (D, D)), ("Wg", (D, D)),
            ("bQT", (128, 8)), ("bKT", (128, 8)),
            ("bV", (1, D)), ("bO", (1, D)), ("bg", (1, D)),
            ("A", (64, 1)), ("Bw", (D, 64)), ("Cw", (64, D)),
            ("W1", (D, 4 * D)), ("b1", (1, 4 * D)),
            ("W2", (4 * D, D)), ("b2", (1, D)),
            ("W_qkv", (D, 3 * D)), ("W_out", (D, D)),
            ("agate_w", (D, 8)), ("q_w", (8, D, 128)), ("qbT", (128, 8)),
            ("k_w", (D, 128)), ("k_b", (128, 1)),
            ("v_w", (D, 128)), ("v_b", (1, 128)),
            ("o_w", (8, 128, D)), ("o_b", (8, D)),
            ("relT", (128, 129)),
            ("fgate_w", (D, 16)), ("ff_w1", (16, D, 512)), ("ff_b1", (16, 512)),
            ("ff_w2", (16, 512, D)), ("ff_b2", (16, D)),
            ("LNPa", (128, 64)),
        ]:
            din(nm, sh)
        for nm, sh, dt_ in [
            ("IDN", (128, 128), BF16), ("ONESR", (1, 128), BF16),
            ("MASKA", (128, 256), F32), ("MASKB", (128, 512), F32),
            ("IOT8", (128, 128), F32), ("IOT16", (128, 256), F32),
            ("SEGC", (128, 512), BF16), ("LTX", (32, 32), BF16),
            ("ONH", (1, 256), BF16),
        ]:
            din(nm, sh, dt_)

        self.y = nc.dram_tensor("y", [NB, T, D], F32, kind="ExternalOutput").ap()
        self.rext = nc.dram_tensor("rext", [NT * 2, 128, 384], F32,
                                   kind="Internal")
        self.cvd = nc.dram_tensor("cvd", [1, 4096], BF16, kind="Internal")
        self.crsd = nc.dram_tensor("crsd", [1, 4096], BF16, kind="Internal")
        self.d = d

        sb = lambda nm, shape, dt_=F32: nc.alloc_sbuf_tensor(nm, list(shape), dt_)
        self.X = sb("X", (128, NT, D))
        self.XT = sb("XT", (128, NDC, T), BF16)
        self.Bb = sb("Bb", (128, NT, D), BF16)
        self.WARt = sb("WAR", (128, 16384), BF16)
        self.WSL = sb("WSL", (128, 2, 4096), BF16)
        self.SCR2 = sb("SCR2", (128, 1024))
        self.SCB = sb("SCB", (128, 1024), BF16)
        self.ST = sb("ST", (128, 64))
        self.MSQ = sb("MSQ", (128, 16))
        self.SUMS = sb("SUMS", (128, 32))
        self.LNP = sb("LNP", (128, 64))
        self.BQS = sb("BQS", (128, 32))
        self.ACOL = sb("ACOL", (64, 1))
        self.BROW = sb("BROW", (1, 1024), BF16)
        self.IDN = sb("IDNs", (128, 128), BF16)
        self.ONESR = sb("ONESRs", (1, 128), BF16)
        self.ARS = sb("ARS", (128, 5760))

        pp = lambda nm, shape, dt_=F32: nc.alloc_psum_tensor(nm, list(shape), dt_)
        self.pf = [pp(f"pf{i}", (128, 512)) for i in range(6)]
        self.pb = [pp(f"pb{i}", (128, 1024), BF16) for i in range(2)]

        A = self.ARS
        self.U = A[0:64, 0:2048]
        self.USB = A[0:64, 2048:3072].bitcast(BF16)
        self.QTg = A[:, 3072:3136].bitcast(BF16)
        self.KTr = A[:, 3136:3264].bitcast(BF16)
        self.Vr = A[:, 3264:3392].bitcast(BF16).rearrange("p (s n) -> p s n", s=2)
        self.Pa = A[:, 3392:3520].bitcast(BF16)
        self.PTa = A[:, 3520:3648].bitcast(BF16)
        self.SML = A[:, 3648:4160]
        self.SMM = A[:, 4160:4672]
        self.SCB2 = A[:, 3648:4160].bitcast(BF16)
        self.CW = A[0:64, 4672:5184].bitcast(BF16)
        self.BWw = A[:, 5184:5440].bitcast(BF16).rearrange("p (c n) -> p c n", c=8)
        self.MASKA = A[:, 5440:5696]
        self.SKW = A[:, 0:256]
        self.RELS = A[:, 256:388]
        self.REXT = A[:, 388:772]
        self.BCL = [A[:, 388:515], A[:, 772:899]]
        self.BCR = [A[:, 516:644], A[:, 900:1028]]
        self.GL = A[:, 772:1028]
        self.GT = A[:, 1028:1156]
        self.G2T = A[:, 1156:1412]
        self.GTB = A[0:8, 1412:2436].bitcast(BF16)
        self.G2TB = A[0:16, 2436:3460].bitcast(BF16)
        self.RTT = A[:, 3460:3526].bitcast(BF16)
        self.OB = A[0:8, 3526:4038].bitcast(BF16)
        self.FB2 = A[0:16, 4038:4550].bitcast(BF16)
        self.IOT8 = A[:, 4550:4678]
        self.IOT16 = A[:, 4678:4934]
        self.AGW = A[:, 4934:4966].bitcast(BF16).rearrange("p (c n) -> p c n", c=8)
        self.CTXE = A[:, 5094:5158].bitcast(BF16)
        self.CTXE2 = A[:, 5158:5222].bitcast(BF16)
        self.SKW2 = A[:, 5222:5478]
        self.RELS2 = A[:, 5478:5610]
        self.FGW = A[:, 4966:5030].bitcast(BF16).rearrange("p (c n) -> p c n", c=8)
        self.SLb = A[:, 0:2048]
        self.MASKB = A[:, 2048:2560]
        self.SEGC = A[:, 2560:2816].bitcast(BF16)
        self.LTX = A[0:32, 2816:2832].bitcast(BF16)
        self.CSKT = A[:, 2832:2864]
        self.CUMK = A[:, 2864:2900]
        self.CUMKB = A[:, 2900:2918].bitcast(BF16)
        self.CSVB = A[0:32, 2918:2982].bitcast(BF16)
        self.CUMVB = A[0:32, 2982:3046].bitcast(BF16)
        self.CV2 = A[0:2, 3046:4070].bitcast(BF16)
        self.CRS2 = A[0:2, 4070:5094].bitcast(BF16)
        self.SMS = A[:, 5100:5132]
        self.PTT = A[:, 5132:5196].bitcast(BF16)
        self.PTT2 = A[:, 5610:5674].bitcast(BF16)

        W = self.WARt[:]
        self.CTX = W.rearrange("p (h t) -> p h t", h=8)
        self.MIDT = W[:, 0:2048].rearrange("p (j q t) -> p j q t", j=4, q=4)
        self.MIDTf = W[:, 0:2048]
        self.QP = W[:, 0:4096].rearrange("p (s t) -> p s t", s=2)
        self.KT1 = W[:, 4096:6144]
        self.VTok = W[:, 6144:8192].rearrange("p (i n) -> p i n", i=16)
        self.PB = W[:, 8192:10240]
        self.PT = W[:, 10240:12288]
        self.OWp = W[:, 12288:14336].rearrange("p (s n) -> p s n", s=2)
        self.MIDT2 = W[:, 14336:14848].rearrange("p (q t) -> p q t", q=4)
        self.PB2 = W[:, 14336:16384]
        self.MIDT2f = W[:, 14336:14848]
        self.QTb = W[:, 0:2048]
        self.KTb = W[:, 2048:4096]
        self.VGb = W[:, 4096:6144].rearrange("p (i n) -> p i n", i=16)
        self.PLb = W[:, 6144:8192]
        self.OUTBg = W[:, 8192:10240]
        self.PTb = W[:, 10240:11264]
        self.CRSB = self.WARt[0:1, 8192:12288]
        self.CRB = W[:, 15360:16384].bitcast(F32)

        self.prologue()
        for b in range(self.nbatches):
            self.batch(b)

        from contextlib import ExitStack
        with ExitStack() as stack:
            block = stack.enter_context(nc.Block())
            self.sc.emit(nc, block)
        return nc

    def prologue(self):
        d = self.d
        self.dma("SY", [], [("IDN",), ("ONESR",), ("LNP",), ("BQSr",),
                            ("ACOL",)], [
            (self.IDN[:], d["IDN"]), (self.ONESR[:], d["ONESR"]),
            (self.LNP[:], d["LNPa"]),
            (self.BQS[:, 0:8], d["bQT"]), (self.BQS[:, 8:16], d["qbT"]),
            (self.BQS[:, 16:24], d["bKT"]), (self.BQS[:, 24:25], d["k_b"]),
            (self.ACOL[:], d["A"]),
        ])
        self.op("V", [("BQSr",)], [("BQS",)],
                lambda e: e.tensor_scalar_mul(self.BQS[:, 0:8],
                                              self.BQS[:, 0:8], 0.125))
        self.op("V", [("BQSr",)], [("BQS2",)],
                lambda e: e.tensor_scalar_mul(self.BQS[:, 8:16],
                                              self.BQS[:, 8:16],
                                              float(128.0 ** -0.5)))
        self.op("V", [], [("ONEC",), ("SUMS",)],
                lambda e: e.memset(self.SUMS[:, 30:31], 1.0))
        self.op("V", [], [("CONSTC",), ("SUMS",)],
                lambda e: e.memset(self.SUMS[:, 8:9], float(EPS)))
        self.op("V", [], [("CONSTC",), ("SUMS",)],
                lambda e: e.memset(self.SUMS[:, 9:10], 0.125))
        self.op("V", [], [("CONSTC",), ("SUMS",)],
                lambda e: e.memset(self.SUMS[:, 10:11], float(128.0 ** -0.5)))
        self.op("V", [], [("CONSTC",), ("SUMS",)],
                lambda e: e.memset(self.SUMS[:, 11:12], 1.0 / 64.0))

    def batch(self, b):
        d = self.d
        X = self.X
        self.dma("SY", [], [("X", ti) for ti in range(NT)],
                 [(X[:], d["x"][b].rearrange("(t p) d -> p t d", p=128))])
        if self.substop == "load":
            pass
        elif self.stages[0]:
            self.attn_a(b)
        self.tap(f"y1_{b}", X[:], [("X", ti) for ti in range(NT)], (128, NT, D))
        if self.stages[1]:
            self.ffn(b, 0)
        self.tap(f"y2_{b}", X[:], [("X", ti) for ti in range(NT)], (128, NT, D))
        if self.stages[2]:
            self.attn_b(b)
        self.tap(f"y3_{b}", X[:], [("X", ti) for ti in range(NT)], (128, NT, D))
        if self.stages[3]:
            self.ffn(b, 1)
        self.dma("SY", [("X", ti) for ti in range(NT)], [("Yout", b)],
                 [(self.y[b].rearrange("(t p) d -> p t d", p=128), X[:])])

    def ln_stats(self, src_key, src_ap_fn, c0, tag):
        stv = self.ST
        for ti in range(NT):
            ap = src_ap_fn(ti)
            self.op("S", [(src_key, ti)], [("SCR2",), (tag, "s2", ti),
                                           ("STm", c0 + 16 + ti)],
                    lambda e, ap=ap, ti=ti: e.activation(
                        self.SCR2[:], ap, AF.Square,
                        accum_out=stv[:, c0 + 16 + ti:c0 + 17 + ti]))
            self.op("V", [(src_key, ti)], [(tag, "s1", ti), ("STm", c0 + ti)],
                    lambda e, ap=ap, ti=ti: e.reduce_sum(
                        stv[:, c0 + ti:c0 + 1 + ti], ap, axis=AX.X))
        rs1 = [(tag, "s1", ti) for ti in range(NT)]
        rs2 = [(tag, "s2", ti) for ti in range(NT)]
        stm = [("STm", c0 + i) for i in range(32)]
        self.op("V", rs1, [(tag, "mu")] + stm[:16],
                lambda e: e.tensor_scalar_mul(stv[:, c0:c0 + 16],
                                              stv[:, c0:c0 + 16], 1.0 / D))
        self.op("V", [(tag, "mu")], [(tag, "msq"), ("MSQ",)],
                lambda e: e.tensor_tensor(self.MSQ[:], stv[:, c0:c0 + 16],
                                          stv[:, c0:c0 + 16], ALU.mult))
        self.op("V", rs2 + [(tag, "msq")], [(tag, "var")] + stm[16:],
                lambda e: e.tensor_scalar(stv[:, c0 + 16:c0 + 32],
                                          stv[:, c0 + 16:c0 + 32],
                                          1.0 / D, None, ALU.mult))
        self.op("V", [(tag, "var"), (tag, "msq")], [(tag, "var2")] + stm[16:],
                lambda e: e.tensor_tensor(stv[:, c0 + 16:c0 + 32],
                                          stv[:, c0 + 16:c0 + 32],
                                          self.MSQ[:], ALU.subtract))
        self.op("S", [(tag, "var2")], [(tag, "sd")] + stm[16:],
                lambda e: e.activation(stv[:, c0 + 16:c0 + 32],
                                       stv[:, c0 + 16:c0 + 32], AF.Sqrt,
                                       bias=self.SUMS[:, 8:9]))
        self.op("V", [(tag, "sd")], [(tag, "rstd")] + stm[16:],
                lambda e: e.reciprocal(stv[:, c0 + 16:c0 + 32],
                                       stv[:, c0 + 16:c0 + 32]))
        return (stv[:, c0:c0 + 16], stv[:, c0 + 16:c0 + 32], tag)

    def normalize_tile(self, src_ap, stats, ti, out_ap, src_keys, out_keys):
        mu, rstd, tag = stats
        self.op("V", src_keys + [(tag, "mu"), (tag, "rstd")], out_keys,
                lambda e: e.tensor_scalar(out_ap, src_ap,
                                          mu[:, ti:ti + 1], rstd[:, ti:ti + 1],
                                          ALU.subtract, ALU.mult))

    def transpose_to_xt(self, ti, gcol, bcol, src_key, buf=None):
        sbuf = self.SCB if buf is None else buf
        for dc in range(NDC):
            self.op("T", [src_key, ("IDN",)], [("pb", 0)],
                    lambda e, dc=dc: e.transpose(
                        self.pb[0][:, dc * 128:dc * 128 + 128],
                        sbuf[:, dc * 128:dc * 128 + 128], self.IDN[:]))
        for dc in range(NDC):
            if gcol is not None:
                self.op("V", [("pb", 0), ("LNP",)],
                        [("XT", ti, dc)],
                        lambda e, dc=dc, ti=ti: e.tensor_scalar(
                            self.XT[:, dc, ti * 128:ti * 128 + 128],
                            self.pb[0][:, dc * 128:dc * 128 + 128],
                            self.LNP[:, gcol * 8 + dc:gcol * 8 + dc + 1],
                            self.LNP[:, bcol * 8 + dc:bcol * 8 + dc + 1],
                            ALU.mult, ALU.add))
            else:
                self.op("V", [("pb", 0)], [("XT", ti, dc)],
                        lambda e, dc=dc, ti=ti: e.tensor_copy(
                            self.XT[:, dc, ti * 128:ti * 128 + 128],
                            self.pb[0][:, dc * 128:dc * 128 + 128]))

    def ln_to_xt(self, src_key, src_ap_fn, c0, tag, gcol, bcol, rot=False):
        st = self.ln_stats(src_key, src_ap_fn, c0, tag)
        for ti in range(NT):
            use2 = rot and ti % 2 == 1
            buf = self.SCB2 if use2 else self.SCB
            bk = ("SCB2",) if use2 else ("SCB",)
            self.normalize_tile(src_ap_fn(ti), st, ti, buf[:],
                                [(src_key, ti)], [bk])
            self.transpose_to_xt(ti, gcol, bcol, bk, buf)
        return st

    def attn_a(self, b):
        d = self.d
        X, XT, Bb = self.X, self.XT, self.Bb
        pf, pbp = self.pf, self.pb
        WS = self.WSL
        CTX = self.CTX

        ars_keys = [("U", tc) for tc in range(4)] + [
            ("USB",), ("QTg",), ("KTr", 0), ("KTr", 1), ("Vr", 0), ("Vr", 1),
            ("Pa",), ("PTa",), ("gch",), ("smch",), ("CW",), ("BWw",),
            ("MASKA",), ("SCB2",)]
        self.use_ars(ars_keys)
        self.dma("SY", [], [("MASKA",)], [(self.MASKA, d["MASKA"])])
        self.dma("G", [], [("CW",), ("BWw",)], [
            (self.CW, d["Cw"]),
            (self.BWw, d["Bw"].rearrange("(c p) n -> p c n", p=128)),
        ])
        self.use_brow([("BROW",)])
        self.dma("G", [], [("BROW",)], [(self.BROW[:], d["bV"])])

        st0 = self.ln_stats("X", lambda ti: X[:, ti, :], 0, ("aL0", b))
        if self.substop == "stats":
            return
        for ti in range(NT):
            self.normalize_tile(X[:, ti, :], st0, ti, Bb[:, ti, :],
                                [("X", ti)], [("B", ti)])
        if self.substop == "norm":
            return
        st1 = self.ln_stats("B", lambda ti: Bb[:, ti, :], 32, ("aL1", b))
        for ti in range(NT):
            use2 = ti % 2 == 1
            buf = self.SCB2 if use2 else self.SCB
            bk = ("SCB2",) if use2 else ("SCB",)
            self.normalize_tile(Bb[:, ti, :], st1, ti, buf[:],
                                [("B", ti)], [bk])
            self.transpose_to_xt(ti, 0, 1, bk, buf)

        if self.substop == "ln":
            return
        ctx_keys = [("CTX", hg) for hg in range(8)]
        self.use_war(ctx_keys)

        for hg in range(8):
            slot = hg % 2
            wk = ("WSL", slot)
            self.dma("G", [], [wk], [
                (WS[:, slot, 0:1024].rearrange("p (c n) -> p c n", n=128),
                 d["WQ"].rearrange("(c p) n -> p c n", p=128)[:, :, hg * 128:hg * 128 + 128]),
                (WS[:, slot, 1024:2048].rearrange("p (c n) -> p c n", n=128),
                 d["WK"].rearrange("(c p) n -> p c n", p=128)[:, :, hg * 128:hg * 128 + 128]),
                (WS[:, slot, 2048:3072].rearrange("p (c n) -> p c n", n=128),
                 d["WV"].rearrange("(c p) n -> p c n", p=128)[:, :, hg * 128:hg * 128 + 128]),
            ])
            wq = WS[:, slot, 0:1024].rearrange("p (c n) -> p c n", n=128)
            wkv = WS[:, slot, 1024:2048].rearrange("p (c n) -> p c n", n=128)
            wv = WS[:, slot, 2048:3072].rearrange("p (c n) -> p c n", n=128)
            for ti in range(NT):
                xtr = [("XT", ti, dc) for dc in range(NDC)]
                xts = lambda dc, ti=ti: XT[:, dc, ti * 128:ti * 128 + 128]
                self.mm(xtr + [wk], [("pf", 0)],
                        [(pf[0][:, 0:128],
                          [(wq[:, dc, :], xts(dc)) for dc in range(NDC)])])
                self.op("V", [("pf", 0), ("BQS",)], [("QTg",)],
                        lambda e, hg=hg: e.tensor_scalar(
                            self.QTg, pf[0][:, 0:128], 0.125,
                            self.BQS[:, hg:hg + 1], ALU.mult, ALU.add))
                self.mm(xtr + [wk], [("pf", 1)],
                        [(pf[1][:, 0:128],
                          [(wkv[:, dc, :], xts(dc)) for dc in range(NDC)])])
                self.op("V", [("pf", 1), ("BQSr",)], [("KTr", ti % 2)],
                        lambda e, ti=ti, hg=hg: e.tensor_scalar_add(
                            self.KTr[:, (ti % 2) * 128:(ti % 2) * 128 + 128],
                            pf[1][:, 0:128],
                            self.BQS[:, 16 + hg:17 + hg]))
                self.mm(xtr + [wk, ("BROW",)], [("pf", 2)],
                        [(pf[2][:, 0:128],
                          [(xts(dc), wv[:, dc, :]) for dc in range(NDC)] +
                          [(self.ONESR[:], self.BROW[:, hg * 128:hg * 128 + 128])])])
                self.op("V", [("pf", 2)], [("Vr", ti % 2)],
                        lambda e, ti=ti: e.tensor_copy(
                            self.Vr[:, ti % 2, :], pf[2][:, 0:128]))
                for h2 in range(2):
                    r0 = 64 * h2
                    grp = []
                    if ti > 0:
                        grp.append((pf[3][:, 0:128],
                                    [(self.QTg[r0:r0 + 64, :],
                                      self.KTr[r0:r0 + 64,
                                               ((ti - 1) % 2) * 128:((ti - 1) % 2) * 128 + 128])]))
                    grp.append((pf[3][:, 128:256],
                                [(self.QTg[r0:r0 + 64, :],
                                  self.KTr[r0:r0 + 64,
                                           (ti % 2) * 128:(ti % 2) * 128 + 128])]))
                    self.mm([("QTg",), ("KTr", ti % 2), ("KTr", (ti - 1) % 2)],
                            [("pf", 3)], grp)
                    if ti == 0:
                        self.op("V", [], [("SA0",), ("SCR2",)],
                                lambda e: e.memset(self.SCR2[:, 0:128], -1e9))
                        self.op("V", [("pf", 3), ("MASKA",), ("SA0",)], [("SA",), ("SCR2",)],
                                lambda e: e.tensor_tensor(
                                    self.SCR2[:, 128:256], pf[3][:, 128:256],
                                    self.MASKA[:, 128:256], ALU.add))
                    else:
                        self.op("V", [("pf", 3), ("MASKA",)], [("SA",), ("SA0",), ("SCR2",)],
                                lambda e: e.tensor_tensor(
                                    self.SCR2[:, 0:256], pf[3][:, 0:256],
                                    self.MASKA, ALU.add))
                    self.op("S", [("SA",), ("SA0",), ("SCR2",)], [("Pa",), ("sumA",), ("SUMS",)],
                            lambda e: e.activation(self.Pa, self.SCR2[:, 0:256],
                                                   AF.Exp,
                                                   accum_out=self.SUMS[:, 0:1]))
                    self.op("V", [("sumA",)], [("rA",), ("SUMS",)],
                            lambda e: e.reciprocal(self.SUMS[:, 1:2],
                                                   self.SUMS[:, 0:1]))
                    self.op("V", [("rA",), ("Pa",)], [("Pa",)],
                            lambda e: e.tensor_scalar_mul(self.Pa, self.Pa,
                                                          self.SUMS[:, 1:2]))
                    for k in ([1] if ti == 0 else [0, 1]):
                        self.op("T", [("Pa",), ("IDN",)], [("pb", 1)],
                                lambda e, k=k: e.transpose(
                                    pbp[1][:, k * 128:k * 128 + 128],
                                    self.Pa[:, k * 128:k * 128 + 128],
                                    self.IDN[:]))
                    if ti == 0:
                        self.op("V", [("pb", 1)], [("PTa",)],
                                lambda e: e.tensor_copy(self.PTa[:, 128:256],
                                                        pbp[1][:, 128:256]))
                    else:
                        self.op("V", [("pb", 1)], [("PTa",)],
                                lambda e: e.tensor_copy(self.PTa,
                                                        pbp[1][:, 0:256]))
                    mmc = []
                    if ti > 0:
                        mmc.append((self.Vr[:, (ti - 1) % 2, r0:r0 + 64],
                                    self.PTa[:, 0:128]))
                    mmc.append((self.Vr[:, ti % 2, r0:r0 + 64],
                                self.PTa[:, 128:256]))
                    self.mm([("PTa",), ("Vr", ti % 2), ("Vr", (ti - 1) % 2)],
                            [("pf", 4)],
                            [(pf[4][r0:r0 + 64, 0:128], mmc)])
                self.op("V", [("pf", 4)], [("CTX", hg)],
                        lambda e, hg=hg, ti=ti: e.tensor_copy(
                            CTX[:, hg, ti * 128:ti * 128 + 128],
                            pf[4][:, 0:128]))

        if self.substop == "qkv":
            return
        for tc in range(4):
            xtr = [("XT", 4 * tc + i, dc) for i in range(4) for dc in range(NDC)]
            self.mm(xtr + [("BWw",)], [("pf", 5)],
                    [(pf[5][0:64, :],
                      [(self.BWw[:, dc, :], XT[:, dc, tc * 512:tc * 512 + 512])
                       for dc in range(NDC)])])
            self.op("V", [("pf", 5)], [("U", tc)],
                    lambda e, tc=tc: e.tensor_copy(
                        self.U[:, tc * 512:tc * 512 + 512], pf[5][0:64, :]))
        self.op("V", [("U", 0), ("U", 1), ("U", 2), ("U", 3), ("ACOL",)],
                [("USB",)],
                lambda e: e.tensor_tensor_scan(
                    self.USB, self.ACOL[:].broadcast_to((64, T)), self.U,
                    0.0, ALU.mult, ALU.add))
        if self.substop == "pass1":
            return

        for dch in range(2):
            cs = slice(dch * 512, dch * 512 + 512)
            self.dma("G", [], [("WSL", 0), ("WSL", 1)], [
                (WS[:, 0, :].rearrange("p (c n) -> p c n", n=512),
                 d["WO"].rearrange("(c p) n -> p c n", p=128)[:, :, cs]),
                (WS[:, 1, :].rearrange("p (c n) -> p c n", n=512),
                 d["Wg"].rearrange("(c p) n -> p c n", p=128)[:, :, cs]),
            ])
            self.dma("G", [], [("BROW",)], [
                (self.BROW[:, 0:512], d["bO"][:, cs]),
                (self.BROW[:, 512:1024], d["bg"][:, cs]),
            ])
            wo = WS[:, 0, :].rearrange("p (c n) -> p c n", n=512)
            wg = WS[:, 1, :].rearrange("p (c n) -> p c n", n=512)
            for ti in range(NT):
                self.mm(ctx_keys + [("WSL", 0), ("BROW",)], [("pf", 0)],
                        [(pf[0][:, :],
                          [(CTX[:, hgx, ti * 128:ti * 128 + 128], wo[:, hgx, :])
                           for hgx in range(8)] +
                          [(self.ONESR[:], self.BROW[:, 0:512])])])
                if self.substop == "pass2" and dch == 0 and ti == 0:
                    dbg = self.nc.dram_tensor("dbg_ao", [128, 512], F32,
                                              kind="ExternalOutput").ap()
                    self.op("V", [("pf", 0)], [("dbgao",), ("SCR2",)],
                            lambda e: e.tensor_copy(self.SCR2[:, 0:512],
                                                    pf[0][:, :]))
                    self.dma("SY", [("dbgao",), ("SCR2",)], [], [(dbg, self.SCR2[:, 0:512])])
                self.mm([("XT", ti, dc) for dc in range(NDC)] +
                        [("WSL", 1), ("BROW",)], [("pf", 2)],
                        [(pf[2][:, :],
                          [(XT[:, dc, ti * 128:ti * 128 + 128], wg[:, dc, :])
                           for dc in range(NDC)] +
                          [(self.ONESR[:], self.BROW[:, 512:1024])])])
                self.op("S", [("pf", 2)], [("gch",), ("SCB2",)],
                        lambda e: e.activation(self.SML, pf[2][:, :], AF.Sigmoid))
                self.mm([("USB",), ("CW",)], [("pf", 3)],
                        [(pf[3][:, :],
                          [(self.USB[:, ti * 128:ti * 128 + 128],
                            self.CW[:, dch * 512:dch * 512 + 512])])])
                self.op("V", [("pf", 3)], [("smch",), ("SCB2",)],
                        lambda e: e.tensor_copy(self.SMM, pf[3][:, :]))
                self.op("V", [("pf", 0), ("smch",)], [("d1",), ("SCR2",)],
                        lambda e: e.tensor_tensor(self.SCR2[:, 0:512],
                                                  pf[0][:, :], self.SMM,
                                                  ALU.subtract))
                self.op("V", [("d1",), ("gch",)], [("d2",), ("SCR2",)],
                        lambda e: e.tensor_tensor(self.SCR2[:, 0:512],
                                                  self.SCR2[:, 0:512],
                                                  self.SML, ALU.mult))
                self.op("V", [("d2",), ("smch",)], [("d3",), ("SCR2",)],
                        lambda e: e.tensor_tensor(self.SCR2[:, 0:512],
                                                  self.SCR2[:, 0:512],
                                                  self.SMM, ALU.add))
                self.op("V", [("d3",), ("B", ti)], [("x1f",), ("SCR2",)],
                        lambda e, ti=ti, cs=cs: e.tensor_tensor(
                            self.SCR2[:, 0:512], self.SCR2[:, 0:512],
                            Bb[:, ti, cs], ALU.add))
                self.op("V", [("x1f",), ("SCR2",), ("X", ti)], [("X", ti)],
                        lambda e, ti=ti, cs=cs: e.tensor_tensor(
                            X[:, ti, cs], X[:, ti, cs], self.SCR2[:, 0:512],
                            ALU.add))
                self.op("V", [("x1f",), ("SCR2",)], [("B", ti)],
                        lambda e, ti=ti, cs=cs: e.tensor_copy(
                            Bb[:, ti, cs], self.SCR2[:, 0:512]))

        if self.substop == "pass2":
            return
        self.ln_to_xt("B", lambda ti: Bb[:, ti, :], 0, ("aL2", b), 2, 3, rot=True)

        self.use_war([("MIDT", j) for j in range(4)] +
                     [("W2A", 0), ("W2A", 1), ("B2R",)])
        self.use_brow([("BROW", 0), ("BROW", 1)])
        B2R = self.WARt[0:1, 10240:10752]
        MIDT = self.MIDT
        for tb in range(4):
            tis = list(range(4 * tb, 4 * tb + 4))
            for nc_ in range(NDC):
                sl = nc_ % 2
                w2ap = self.WARt[:, 2048 + sl * 4096:2048 + sl * 4096 + 4096]
                self.dma("G", [], [("WSL", sl), ("W2A", sl)], [
                    (WS[:, sl, :].rearrange("p (c n) -> p c n", n=512),
                     d["W1"].rearrange("(c p) n -> p c n", p=128)[:, :, nc_ * 512:nc_ * 512 + 512]),
                    (w2ap.rearrange("p (c n) -> p c n", n=1024),
                     d["W2"][nc_ * 512:nc_ * 512 + 512, :]
                     .rearrange("(c p) n -> p c n", p=128)),
                ])
                self.dma("G", [], [("BROW", sl)], [
                    (self.BROW[:, sl * 512:sl * 512 + 512],
                     d["b1"][:, nc_ * 512:nc_ * 512 + 512]),
                ])
                w1 = WS[:, sl, :].rearrange("p (c n) -> p c n", n=512)
                w2 = w2ap.rearrange("p (c n) -> p c n", n=1024)
                for j, ti in enumerate(tis):
                    buf = self.SCB if j % 2 == 0 else self.SCB2
                    bk = ("SCB",) if j % 2 == 0 else ("SCB2",)
                    mmg = [(XT[:, dc, ti * 128:ti * 128 + 128], w1[:, dc, :])
                           for dc in range(NDC)]
                    mmg.append((self.ONESR[:],
                                self.BROW[:, sl * 512:sl * 512 + 512]))
                    self.mm([("XT", ti, dc) for dc in range(NDC)] +
                            [("WSL", sl), ("BROW", sl)], [("pf", 4)],
                            [(pf[4][:, :], mmg)])
                    self.op("S", [("pf", 4)], [bk],
                            lambda e, buf=buf: e.activation(buf[:, 0:512],
                                                            pf[4][:, :],
                                                            AF.Gelu))
                    for q in range(4):
                        self.op("T", [bk, ("IDN",)], [("pb", 0)],
                                lambda e, q=q, buf=buf: e.transpose(
                                    pbp[0][:, q * 128:q * 128 + 128],
                                    buf[:, q * 128:q * 128 + 128],
                                    self.IDN[:]))
                    self.op("V", [("pb", 0)],
                            [("MIDT", j)],
                            lambda e, j=j: e.tensor_copy(
                                self.MIDTf[:, j * 512:j * 512 + 512],
                                pbp[0][:, 0:512]))
                for dch in range(2):
                    if nc_ == 0:
                        self.dma("G", [], [("B2R",)],
                                 [(B2R,
                                   d["b2"][:, dch * 512:dch * 512 + 512])])
                    for j, ti in enumerate(tis):
                        mmg = [(MIDT[:, j, q, :],
                                w2[:, q, dch * 512:dch * 512 + 512])
                               for q in range(4)]
                        if nc_ == 0:
                            mmg.append((self.ONESR[:], B2R))
                        self.mm([("MIDT", j), ("W2A", sl)] +
                                ([("B2R",)] if nc_ == 0 else []),
                                [("pf", j % 2)],
                                [(pf[j % 2][:, :], mmg)])
                        cs = slice(dch * 512, dch * 512 + 512)
                        self.op("V", [("pf", j % 2), ("X", ti)], [("X", ti)],
                                lambda e, ti=ti, cs=cs, j=j: e.tensor_tensor(
                                    X[:, ti, cs], X[:, ti, cs],
                                    pf[j % 2][:, :], ALU.add))

    def ffn(self, b, app):
        d = self.d
        X, XT, Bb = self.X, self.XT, self.Bb
        pf, pbp = self.pf, self.pb
        WS = self.WSL

        ars_keys = ([("SKW", 0), ("SKW", 1), ("RELS", 0), ("RELS", 1),
                     ("GT",), ("G2T",),
                     ("GTB",), ("G2TB",), ("RTT",), ("OBs",), ("FB2",),
                     ("IOT8",), ("IOT16",), ("AGW",), ("FGW",),
                     ("CTXE", 0), ("CTXE", 1),
                     ("REXT", 0), ("REXT", 1), ("REXT", 2), ("SCB2",)] +
                    [("GL", ti) for ti in range(NT)])
        self.use_ars(ars_keys)
        self.dma("SY", [], [("IOT8",), ("IOT16",)],
                 [(self.IOT8, d["IOT8"]), (self.IOT16, d["IOT16"])])
        self.dma("G", [], [("RTT",), ("OBs",), ("FB2",), ("AGW",), ("FGW",)], [
            (self.RTT[:, 0:129], d["relT"]),
            (self.OB, d["o_b"]),
            (self.FB2, d["ff_b2"]),
            (self.AGW, d["agate_w"].rearrange("(c p) n -> p c n", p=128)),
            (self.FGW, d["fgate_w"].rearrange("(c p) n -> p c n", p=128)),
        ])

        st0 = self.ln_stats("X", lambda ti: X[:, ti, :], 0, ("fL0", b, app))
        for ti in range(NT):
            self.normalize_tile(X[:, ti, :], st0, ti, Bb[:, ti, :],
                                [("X", ti)], [("B", ti)])
        self.ln_to_xt("B", lambda ti: Bb[:, ti, :], 32, ("fL1", b, app), 4, 5)

        war_keys = ([("QP", s) for s in range(2)] + [("KT1", tc) for tc in range(4)] +
                    [("VTok", ti) for ti in range(NT)] +
                    [("PB", r_, c4) for r_ in range(2) for c4 in range(4)] +
                    [("PBs", 0), ("PBs", 1)] +
                    [("PT", 0), ("PT", 1), ("OWp", 0), ("OWp", 1)])
        self.use_war(war_keys)

        for ti in range(NT):
            self.mm([("XT", ti, dc) for dc in range(NDC)] + [("AGW",)],
                    [("pf", 5)],
                    [(pf[5][:, 0:8],
                      [(XT[:, dc, ti * 128:ti * 128 + 128], self.AGW[:, dc, :])
                       for dc in range(NDC)])])
            self.op("V", [("pf", 5)], [("GL", ti)],
                    lambda e, ti=ti: e.tensor_copy(self.GL[:, ti * 8:ti * 8 + 8],
                                                   pf[5][:, 0:8]))
        self.gates_topk(self.GL[:, 0:128], 8, self.IOT8, self.GT, ("GT",),
                        [("GL", ti) for ti in range(NT)], b, app)
        self.op("V", [("GT",)], [("GTc",), ("SCB",)],
                lambda e: e.tensor_copy(self.SCB[:, 0:128], self.GT))
        for half in range(2):
            for j in range(8):
                ti = half * 8 + j
                self.op("T", [("GTc",), ("SCB",), ("IDN",)], [("pb", 0)],
                        lambda e, ti=ti, j=j: e.transpose(
                            pbp[0][0:8, j * 128:j * 128 + 128],
                            self.SCB[:, ti * 8:ti * 8 + 8], self.IDN[:]))
            self.op("V", [("pb", 0)], [("GTB",)],
                    lambda e, half=half: e.tensor_copy(
                        self.GTB[:, half * 1024:half * 1024 + 1024],
                        pbp[0][0:8, :]))

        self.dma("G", [], [("WSL", 0)], [
            (WS[:, 0, 0:1024].rearrange("p (c n) -> p c n", n=128),
             d["k_w"].rearrange("(c p) n -> p c n", p=128)),
            (WS[:, 0, 1024:2048].rearrange("p (c n) -> p c n", n=128),
             d["v_w"].rearrange("(c p) n -> p c n", p=128)),
        ])
        self.use_brow([("BROW",)])
        self.dma("G", [], [("BROW",)], [(self.BROW[:, 0:128], d["v_b"])])
        kw = WS[:, 0, 0:1024].rearrange("p (c n) -> p c n", n=128)
        vw = WS[:, 0, 1024:2048].rearrange("p (c n) -> p c n", n=128)
        for tc in range(4):
            xtr = [("XT", 4 * tc + i, dc) for i in range(4) for dc in range(NDC)]
            self.mm(xtr + [("WSL", 0)], [("pf", 0)],
                    [(pf[0][:, :],
                      [(kw[:, dc, :], XT[:, dc, tc * 512:tc * 512 + 512])
                       for dc in range(NDC)])])
            self.op("V", [("pf", 0), ("BQSr",)], [("KT1", tc)],
                    lambda e, tc=tc: e.tensor_scalar_add(
                        self.KT1[:, tc * 512:tc * 512 + 512], pf[0][:, :],
                        self.BQS[:, 24:25]))
        for ti in range(NT):
            self.mm([("XT", ti, dc) for dc in range(NDC)] +
                    [("WSL", 0), ("BROW",)], [("pf", 1)],
                    [(pf[1][:, 0:128],
                      [(XT[:, dc, ti * 128:ti * 128 + 128], vw[:, dc, :])
                       for dc in range(NDC)] +
                      [(self.ONESR[:], self.BROW[:, 0:128])])])
            self.op("V", [("pf", 1)], [("VTok", ti)],
                    lambda e, ti=ti: e.tensor_copy(self.VTok[:, ti, :],
                                                   pf[1][:, 0:128]))

        for ep in range(4):
            self.dma("G", [], [("WSL", 1)], [
                (WS[:, 1, e2 * 1024:e2 * 1024 + 1024]
                 .rearrange("p (c n) -> p c n", n=128),
                 d["q_w"][2 * ep + e2].rearrange("(c p) n -> p c n", p=128))
                for e2 in range(2)])
            self.dma("G", [], [("OWp", 0), ("OWp", 1)], [
                (self.OWp[:, e2, :], d["o_w"][2 * ep + e2]) for e2 in range(2)])
            for e2 in range(2):
                e_ = 2 * ep + e2
                qw = WS[:, 1, e2 * 1024:e2 * 1024 + 1024].rearrange(
                    "p (c n) -> p c n", n=128)
                for tc in range(4):
                    xtr = [("XT", 4 * tc + i, dc) for i in range(4)
                           for dc in range(NDC)]
                    self.mm(xtr + [("WSL", 1)], [("pf", 0)],
                            [(pf[0][:, :],
                              [(qw[:, dc, :], XT[:, dc, tc * 512:tc * 512 + 512])
                               for dc in range(NDC)])])
                    self.op("V", [("pf", 0), ("BQS2",)], [("QP", e2)],
                            lambda e, e2=e2, tc=tc, e_=e_: e.tensor_scalar(
                                self.QP[:, e2, tc * 512:tc * 512 + 512],
                                pf[0][:, :], float(128.0 ** -0.5),
                                self.BQS[:, 8 + e_:9 + e_],
                                ALU.mult, ALU.add))
            for ti in range(NT):
                lo = max(0, 128 * ti - 64)
                hi = min(T, 128 * ti + 192)
                jjoff = lo - (128 * ti - 64)
                for e2 in range(2):
                    e_ = 2 * ep + e2
                    rr = (ti * 2 + e2) % 2
                    RELS = self.RELS if rr == 0 else self.RELS2
                    SKW = self.SKW if rr == 0 else self.SKW2
                    PB = self.PB if rr == 0 else self.PB2
                    CTXE = self.CTXE if rr == 0 else self.CTXE2
                    sb_ = 16 + 8 * rr
                    st_ = 12 + 2 * rr
                    qtile = self.QP[:, e2, ti * 128:ti * 128 + 128]
                    self.mm([("QP", e2), ("RTT",)], [("pf", 5)],
                            [(pf[5][:, 0:129], [(qtile, self.RTT[:, 0:129])])])
                    self.op("V", [("pf", 5)], [("RELS", rr)],
                            lambda e, RELS=RELS: e.tensor_copy(
                                RELS[:, 0:129], pf[5][:, 0:129]))
                    self.op("V", [("RELS", rr), ("IOT16",)], [("REXT", 0)],
                            lambda e, RELS=RELS: e.tensor_scalar(
                                self.REXT[:, 0:127], self.IOT16[:, 0:127],
                                0.0, RELS[:, 0:1], ALU.mult, ALU.add))
                    self.op("V", [("RELS", rr)], [("REXT", 1)],
                            lambda e, RELS=RELS: e.tensor_copy(
                                self.REXT[:, 127:256], RELS[:, 0:129]))
                    self.op("V", [("RELS", rr), ("IOT16",)], [("REXT", 2)],
                            lambda e, RELS=RELS: e.tensor_scalar(
                                self.REXT[:, 256:384], self.IOT16[:, 0:128],
                                0.0, RELS[:, 128:129], ALU.mult, ALU.add))
                    self.dma("SY", [("REXT", 0), ("REXT", 1), ("REXT", 2)],
                             [("rexd", ti, e2)],
                             [(self.rext.ap()[ti * 2 + e2], self.REXT)])
                    skw_src = AP(self.rext, (ti * 2 + e2) * 49152 + 127 + jjoff,
                                 [[383, 128], [1, hi - lo]])
                    self.dma("SY", [("rexd", ti, e2)], [("SKW", rr)],
                             [(SKW[:, 0:hi - lo], skw_src)])
                    nsum = 0
                    sum_keys = []
                    for c4 in range(4):
                        s0, s1 = c4 * 512, c4 * 512 + 512
                        bank = c4 % 2
                        self.mm([("QP", e2), ("KT1", c4)], [("pf", bank)],
                                [(pf[bank][:, :],
                                  [(qtile, self.KT1[:, s0:s1])])])
                        segs = []
                        a0 = s0
                        for (rs, re_, kind) in [(s0, min(s1, lo), "L"),
                                                (max(s0, lo), min(s1, hi), "B"),
                                                (max(s0, hi), s1, "R")]:
                            if re_ > rs:
                                segs.append((rs, re_, kind))
                        for (rs, re_, kind) in segs:
                            pslice = pf[bank][:, rs - s0:re_ - s0]
                            if kind == "B":
                                self.op("V", [("pf", bank), ("SKW", rr)],
                                        [("pf", bank)],
                                        lambda e, pslice=pslice, rs=rs, re_=re_,
                                        lo=lo, SKW=SKW:
                                        e.tensor_tensor(
                                            pslice, pslice,
                                            SKW[:, rs - lo:re_ - lo],
                                            ALU.add))
                                bias = 0.0
                            elif kind == "L":
                                bias = RELS[:, 0:1]
                            else:
                                bias = RELS[:, 128:129]
                            sk = ("psum", rr, nsum)
                            self.op("S", [("pf", bank), ("RELS", rr)],
                                    [("PB", rr, c4), sk, ("SUMS", rr)],
                                    lambda e, pslice=pslice, rs=rs, re_=re_,
                                    bias=bias, nsum=nsum, PB=PB, sb_=sb_:
                                    e.activation(PB[:, rs:re_], pslice,
                                                 AF.Exp, bias=bias,
                                                 accum_out=self.SUMS[:, sb_ + nsum:sb_ + nsum + 1]))
                            sum_keys.append(sk)
                            nsum += 1
                    self.op("V", sum_keys, [("tsum", rr), ("SUMS", rr)],
                            lambda e, nsum=nsum, sb_=sb_, st_=st_: e.reduce_sum(
                                self.SUMS[:, st_:st_ + 1],
                                self.SUMS[:, sb_:sb_ + nsum],
                                axis=AX.X))
                    self.op("V", [("tsum", rr)], [("rsum", rr), ("SUMS", rr)],
                            lambda e, st_=st_: e.reciprocal(
                                self.SUMS[:, st_ + 1:st_ + 2],
                                self.SUMS[:, st_:st_ + 1]))
                    self.op("V", [("rsum", rr), ("GT",)],
                            [("pscale", rr), ("SUMS", rr)],
                            lambda e, ti=ti, e_=e_, st_=st_: e.tensor_tensor(
                                self.SUMS[:, st_ + 1:st_ + 2],
                                self.SUMS[:, st_ + 1:st_ + 2],
                                self.GT[:, ti * 8 + e_:ti * 8 + e_ + 1],
                                ALU.mult))
                    self.op("V", [("pscale", rr)] +
                            [("PB", rr, c4) for c4 in range(4)],
                            [("PBs", rr)],
                            lambda e, PB=PB, st_=st_: e.tensor_scalar_mul(
                                PB, PB, self.SUMS[:, st_ + 1:st_ + 2]))
                    for half in range(2):
                        for k in range(8):
                            kt = half * 8 + k
                            self.op("T", [("PBs", rr), ("IDN",)], [("pb", 1)],
                                    lambda e, kt=kt, k=k, PB=PB: e.transpose(
                                        pbp[1][:, k * 128:k * 128 + 128],
                                        PB[:, kt * 128:kt * 128 + 128],
                                        self.IDN[:]))
                        self.op("V", [("pb", 1)],
                                [("PT", half)],
                                lambda e, half=half: e.tensor_copy(
                                    self.PT[:, half * 1024:half * 1024 + 1024],
                                    pbp[1][:, :]))
                    self.mm([("PT", 0), ("PT", 1)] +
                            [("VTok", kt) for kt in range(NT)], [("pf", 2)],
                            [(pf[2][:, 0:128],
                              [(self.VTok[:, kt, :],
                                self.PT[:, kt * 128:kt * 128 + 128])
                               for kt in range(NT)])])
                    self.op("V", [("pf", 2)], [("CTXE", rr)],
                            lambda e, CTXE=CTXE: e.tensor_copy(
                                CTXE, pf[2][:, 0:128]))
                    mmo = [[(CTXE,
                             self.OWp[:, e2, dch * 512:dch * 512 + 512])]
                           for dch in range(2)]
                    if ep == 0 and e2 == 0:
                        for dch in range(2):
                            mmo[dch].append(
                                (self.GTB[:, ti * 128:ti * 128 + 128],
                                 self.OB[:, dch * 512:dch * 512 + 512]))
                    self.mm([("CTXE", rr), ("OWp", e2), ("GTB",), ("OBs",)],
                            [("pf", 3), ("pf", 4)],
                            [(pf[3][:, :], mmo[0]), (pf[4][:, :], mmo[1])],
                            accum=(e2 == 1), stop=(e2 == 1))
                for dch in range(2):
                    cs = slice(dch * 512, dch * 512 + 512)
                    self.op("V", [("pf", 3 + dch), ("B", ti)], [("B", ti)],
                            lambda e, ti=ti, cs=cs, dch=dch: e.tensor_tensor(
                                Bb[:, ti, cs], Bb[:, ti, cs],
                                pf[3 + dch][:, :], ALU.add))

        for ti in range(NT):
            self.op("V", [("B", ti), ("X", ti)], [("X", ti)],
                    lambda e, ti=ti: e.tensor_tensor(
                        X[:, ti, :], X[:, ti, :], Bb[:, ti, :], ALU.add))

        self.ln_to_xt("B", lambda ti: Bb[:, ti, :], 0, ("fL2", b, app), 6, 7)

        for ti in range(NT):
            self.mm([("XT", ti, dc) for dc in range(NDC)] + [("FGW",)],
                    [("pf", 5)],
                    [(pf[5][:, 0:16],
                      [(XT[:, dc, ti * 128:ti * 128 + 128], self.FGW[:, dc, :])
                       for dc in range(NDC)])])
            self.op("V", [("pf", 5)], [("GL", ti)],
                    lambda e, ti=ti: e.tensor_copy(
                        self.GL[:, ti * 16:ti * 16 + 16], pf[5][:, 0:16]))
        self.gates_topk(self.GL[:, 0:256], 16, self.IOT16, self.G2T, ("G2T",),
                        [("GL", ti) for ti in range(NT)], b, app,
                        iot_key=("IOT16",))
        self.op("V", [("G2T",)], [("G2c",), ("SCB",)],
                lambda e: e.tensor_copy(self.SCB[:, 0:256], self.G2T))
        for half in range(2):
            for j in range(8):
                ti = half * 8 + j
                self.op("T", [("G2c",), ("SCB",), ("IDN",)], [("pb", 0)],
                        lambda e, ti=ti, j=j: e.transpose(
                            pbp[0][0:16, j * 128:j * 128 + 128],
                            self.SCB[:, ti * 16:ti * 16 + 16], self.IDN[:]))
            self.op("V", [("pb", 0)], [("G2TB",)],
                    lambda e, half=half: e.tensor_copy(
                        self.G2TB[:, half * 1024:half * 1024 + 1024],
                        pbp[0][0:16, :]))

        self.use_war([("MIDT2",), ("W2A", 0), ("W2A", 1)])
        self.use_brow([("BROW", 0), ("BROW", 1)])
        for e_ in range(16):
            sl = e_ % 2
            w2ap = self.WARt[:, sl * 4096:sl * 4096 + 4096]
            self.dma("G", [], [("WSL", sl), ("W2A", sl)], [
                (WS[:, sl, :].rearrange("p (c n) -> p c n", n=512),
                 d["ff_w1"][e_].rearrange("(c p) n -> p c n", p=128)),
                (w2ap.rearrange("p (c n) -> p c n", n=1024),
                 d["ff_w2"][e_].rearrange("(c p) n -> p c n", p=128)),
            ])
            self.dma("G", [], [("BROW", sl)],
                     [(self.BROW[:, sl * 512:sl * 512 + 512],
                       d["ff_b1"][e_:e_ + 1, :])])
            w1 = WS[:, sl, :].rearrange("p (c n) -> p c n", n=512)
            w2 = w2ap.rearrange("p (c n) -> p c n", n=1024)
            for ti in range(NT):
                mmg = [(XT[:, dc, ti * 128:ti * 128 + 128], w1[:, dc, :])
                       for dc in range(NDC)]
                mmg.append((self.ONESR[:],
                            self.BROW[:, sl * 512:sl * 512 + 512]))
                self.mm([("XT", ti, dc) for dc in range(NDC)] +
                        [("WSL", sl), ("BROW", sl)], [("pf", 4)],
                        [(pf[4][:, :], mmg)])
                self.op("S", [("pf", 4), ("G2T",)], [("SCB",)],
                        lambda e, ti=ti, e_=e_: e.activation(
                            self.SCB[:, 0:512], pf[4][:, :], AF.Relu,
                            scale=self.G2T[:, ti * 16 + e_:ti * 16 + e_ + 1]))
                for q in range(4):
                    self.op("T", [("SCB",), ("IDN",)], [("pb", 0)],
                            lambda e, q=q: e.transpose(
                                pbp[0][:, q * 128:q * 128 + 128],
                                self.SCB[:, q * 128:q * 128 + 128],
                                self.IDN[:]))
                self.op("V", [("pb", 0)],
                        [("MIDT2",)],
                        lambda e: e.tensor_copy(
                            self.MIDT2f[:], pbp[0][:, 0:512]))
                for dch in range(2):
                    mmg = [(self.MIDT2[:, q, :],
                            w2[:, q, dch * 512:dch * 512 + 512])
                           for q in range(4)]
                    if e_ == 0:
                        mmg.append((self.G2TB[:, ti * 128:ti * 128 + 128],
                                    self.FB2[:, dch * 512:dch * 512 + 512]))
                    self.mm([("MIDT2",), ("W2A", sl), ("G2TB",), ("FB2",)],
                            [("pf", dch)],
                            [(pf[dch][:, :], mmg)])
                    cs = slice(dch * 512, dch * 512 + 512)
                    self.op("V", [("pf", dch), ("X", ti)], [("X", ti)],
                            lambda e, ti=ti, cs=cs, dch=dch: e.tensor_tensor(
                                X[:, ti, cs], X[:, ti, cs], pf[dch][:, :],
                                ALU.add))

    def gates_topk(self, logits_ap, ne, iot_ap, out_ap, out_key, logit_keys,
                   b, app, iot_key=("IOT8",)):
        S2 = self.SCR2
        nt = NT
        w = nt * ne
        l3 = logits_ap.rearrange("p (t e) -> p t e", e=ne)
        rk = logit_keys

        def v(eng, reads, writes, fn):
            self.op(eng, reads + [("SCR2",), iot_key],
                    writes + [("SCR2",)], fn)
        v("V", rk, [("g_m1",)],
          lambda e: e.reduce_max(S2[:, 0:16], l3, axis=AX.X))
        v("V", rk + [("g_m1",)], [("g_eq",)],
          lambda e: e.tensor_tensor(
              S2[:, 128:128 + w].rearrange("p (t e) -> p t e", e=ne), l3,
              S2[:, 0:16].unsqueeze(2).broadcast_to((128, nt, ne)),
              ALU.is_equal))
        v("V", [("g_eq",)], [("g_t1",)],
          lambda e: e.tensor_tensor(S2[:, 384:384 + w], S2[:, 128:128 + w],
                                    iot_ap[:, 0:w], ALU.mult))
        v("V", [("g_eq",), ("g_t1",)], [("g_t2",)],
          lambda e: e.scalar_tensor_tensor(S2[:, 384:384 + w],
                                           S2[:, 128:128 + w], -float(ne),
                                           S2[:, 384:384 + w],
                                           ALU.mult, ALU.add))
        v("V", [("g_t2",)], [("g_i1",)],
          lambda e: e.tensor_reduce(S2[:, 16:32],
                                    S2[:, 384:384 + w].rearrange(
                                        "p (t e) -> p t e", e=ne),
                                    axis=AX.X, op=ALU.min))
        v("V", [("g_i1",)], [("g_i1b",)],
          lambda e: e.tensor_scalar_add(S2[:, 16:32], S2[:, 16:32], float(ne)))
        v("V", [("g_i1b",)], [("g_m1k",)],
          lambda e: e.tensor_tensor(
              S2[:, 128:128 + w].rearrange("p (t e) -> p t e", e=ne),
              iot_ap[:, 0:w].rearrange("p (t e) -> p t e", e=ne),
              S2[:, 16:32].unsqueeze(2).broadcast_to((128, nt, ne)),
              ALU.is_equal))
        v("V", rk + [("g_m1k",)], [("g_l2",)],
          lambda e: e.scalar_tensor_tensor(S2[:, 384:384 + w],
                                           S2[:, 128:128 + w], -1e30,
                                           logits_ap, ALU.mult, ALU.add))
        v("V", [("g_l2",)], [("g_m2",)],
          lambda e: e.reduce_max(S2[:, 32:48],
                                 S2[:, 384:384 + w].rearrange(
                                     "p (t e) -> p t e", e=ne), axis=AX.X))
        v("V", [("g_l2",), ("g_m2",)], [("g_eq2",)],
          lambda e: e.tensor_tensor(
              S2[:, 640:640 + w].rearrange("p (t e) -> p t e", e=ne),
              S2[:, 384:384 + w].rearrange("p (t e) -> p t e", e=ne),
              S2[:, 32:48].unsqueeze(2).broadcast_to((128, nt, ne)),
              ALU.is_equal))
        v("V", [("g_eq2",)], [("g_t1b",)],
          lambda e: e.tensor_tensor(S2[:, 384:384 + w], S2[:, 640:640 + w],
                                    iot_ap[:, 0:w], ALU.mult))
        v("V", [("g_eq2",), ("g_t1b",)], [("g_t2b",)],
          lambda e: e.scalar_tensor_tensor(S2[:, 384:384 + w],
                                           S2[:, 640:640 + w], -float(ne),
                                           S2[:, 384:384 + w],
                                           ALU.mult, ALU.add))
        v("V", [("g_t2b",)], [("g_i2",)],
          lambda e: e.tensor_reduce(S2[:, 48:64],
                                    S2[:, 384:384 + w].rearrange(
                                        "p (t e) -> p t e", e=ne),
                                    axis=AX.X, op=ALU.min))
        v("V", [("g_i2",)], [("g_i2b",)],
          lambda e: e.tensor_scalar_add(S2[:, 48:64], S2[:, 48:64], float(ne)))
        v("V", [("g_i2b",)], [("g_m2k",)],
          lambda e: e.tensor_tensor(
              S2[:, 640:640 + w].rearrange("p (t e) -> p t e", e=ne),
              iot_ap[:, 0:w].rearrange("p (t e) -> p t e", e=ne),
              S2[:, 48:64].unsqueeze(2).broadcast_to((128, nt, ne)),
              ALU.is_equal))
        v("V", [("g_m1",), ("g_m2",)], [("g_dm",)],
          lambda e: e.tensor_tensor(S2[:, 64:80], S2[:, 0:16], S2[:, 32:48],
                                    ALU.subtract))
        v("S", [("g_dm",)], [("g_g1",)],
          lambda e: e.activation(S2[:, 64:80], S2[:, 64:80], AF.Sigmoid))
        v("V", [("g_g1",)], [("g_g2",)],
          lambda e: e.tensor_scalar(S2[:, 80:96], S2[:, 64:80], -1.0, 1.0,
                                    ALU.mult, ALU.add))
        v("V", [("g_m1k",), ("g_g1",)], [("g_p1",)],
          lambda e: e.tensor_tensor(
              out_ap.rearrange("p (t e) -> p t e", e=ne),
              S2[:, 128:128 + w].rearrange("p (t e) -> p t e", e=ne),
              S2[:, 64:80].unsqueeze(2).broadcast_to((128, nt, ne)),
              ALU.mult))
        v("V", [("g_m2k",), ("g_g2",)], [("g_p2",)],
          lambda e: e.tensor_tensor(
              S2[:, 640:640 + w].rearrange("p (t e) -> p t e", e=ne),
              S2[:, 640:640 + w].rearrange("p (t e) -> p t e", e=ne),
              S2[:, 80:96].unsqueeze(2).broadcast_to((128, nt, ne)),
              ALU.mult))
        v("V", [("g_p1",), ("g_p2",)], [out_key],
          lambda e: e.tensor_tensor(out_ap, out_ap, S2[:, 640:640 + w],
                                    ALU.add))

    def attn_b(self, b):
        d = self.d
        X, XT = self.X, self.XT
        pf, pbp = self.pf, self.pb
        WS = self.WSL

        ars_keys = [("SLb", cg) for cg in range(4)] + [
            ("MASKB",), ("SEGC",), ("LTX",), ("CSKT",), ("CUMK",), ("CUMKB",),
            ("CSVB",), ("CUMVB",), ("CV2",), ("CRS2",), ("PTT", 0), ("PTT", 1),
            ("PTTa", 0), ("PTTa", 1), ("PTTb", 0), ("PTTb", 1), ("SMS",),
            ("SCB2",)]
        self.use_ars(ars_keys)
        self.dma("SY", [], [("MASKB",)], [(self.MASKB, d["MASKB"])])
        self.dma("G", [], [("SEGC",), ("LTX",)], [
            (self.SEGC, d["SEGC"]), (self.LTX, d["LTX"]),
        ])
        self.op("V", [], [("PTT", 0)], lambda e: e.memset(self.PTT, 0.0))
        self.op("V", [], [("PTT", 1)], lambda e: e.memset(self.PTT2, 0.0))
        self.op("V", [], [("CRS2",), ("SCB2",)],
                lambda e: e.memset(self.CRS2, 0.0))

        st0 = self.ln_stats("X", lambda ti: X[:, ti, :], 0, ("bL0", b))
        for ti in range(NT):
            self.normalize_tile(X[:, ti, :], st0, ti, self.SCB[:],
                                [("X", ti)], [("SCB",)])
            self.transpose_to_xt(ti, None, None, ("SCB",))

        war_keys = ([("QTb", tc) for tc in range(4)] +
                    [("KTb", tc) for tc in range(4)] +
                    [("VGb", ti) for ti in range(NT)] +
                    [("PLb", cg) for cg in range(4)] +
                    [("CRSB", h2_, half_, i_) for h2_ in range(2)
                     for half_ in range(2) for i_ in range(2)])
        self.use_war(war_keys)
        OUTB = self.Bb[:]

        for hg in range(8):
            sl = hg % 2
            self.dma("G", [], [("WSL", sl)], [
                (WS[:, sl, i * 1024:i * 1024 + 1024]
                 .rearrange("p (c n) -> p c n", n=128),
                 d["W_qkv"].rearrange("(c p) n -> p c n", p=128)
                 [:, :, i * 1024 + hg * 128: i * 1024 + hg * 128 + 128])
                for i in range(3)])
            wq = WS[:, sl, 0:1024].rearrange("p (c n) -> p c n", n=128)
            wk = WS[:, sl, 1024:2048].rearrange("p (c n) -> p c n", n=128)
            wv = WS[:, sl, 2048:3072].rearrange("p (c n) -> p c n", n=128)
            for tc in range(4):
                xtr = [("XT", 4 * tc + i, dc) for i in range(4)
                       for dc in range(NDC)]
                self.mm(xtr + [("WSL", sl)], [("pf", 0)],
                        [(pf[0][:, :],
                          [(wq[:, dc, :], XT[:, dc, tc * 512:tc * 512 + 512])
                           for dc in range(NDC)])])
                self.op("V", [("pf", 0)], [("QTb", tc)],
                        lambda e, tc=tc: e.tensor_scalar_mul(
                            self.QTb[:, tc * 512:tc * 512 + 512], pf[0][:, :],
                            0.125))
                self.mm(xtr + [("WSL", sl)], [("pf", 1)],
                        [(pf[1][:, :],
                          [(wk[:, dc, :], XT[:, dc, tc * 512:tc * 512 + 512])
                           for dc in range(NDC)])])
                self.op("V", [("pf", 1)], [("KTb", tc)],
                        lambda e, tc=tc: e.tensor_copy(
                            self.KTb[:, tc * 512:tc * 512 + 512], pf[1][:, :]))
            for ti in range(NT):
                self.mm([("XT", ti, dc) for dc in range(NDC)] + [("WSL", sl)],
                        [("pf", 2)],
                        [(pf[2][:, 0:128],
                          [(XT[:, dc, ti * 128:ti * 128 + 128], wv[:, dc, :])
                           for dc in range(NDC)])])
                self.op("V", [("pf", 2)], [("VGb", ti)],
                        lambda e, ti=ti: e.tensor_copy(
                            self.VGb[:, ti, :], pf[2][:, 0:128]))
            self.op("V", [("KTb", tc) for tc in range(4)], [("CSKT",)],
                    lambda e: e.reduce_sum(
                        self.CSKT[:, 0:32],
                        self.KTb.rearrange("p (c k) -> p c k", c=32),
                        axis=AX.X))
            self.mm([("VGb", ti) for ti in range(NT)] + [("SEGC",)],
                    [("pf", 5)],
                    [(pf[5][0:32, 0:128],
                      [(self.SEGC[:, ti * 32:ti * 32 + 32], self.VGb[:, ti, :])
                       for ti in range(NT)])])
            self.op("V", [("pf", 5)], [("CSVB",)],
                    lambda e: e.tensor_copy(self.CSVB, pf[5][0:32, 0:128]))
            self.op("V", [("CUMK",)], [("CUMKz",)],
                    lambda e: e.memset(self.CUMK[:, 0:1], 0.0))
            self.op("V", [("CSKT",), ("CUMKz",), ("ONEC",)], [("CUMK",)],
                    lambda e: e.tensor_tensor_scan(
                        self.CUMK[:, 1:33],
                        self.SUMS[:, 30:31].broadcast_to((128, 32)),
                        self.CSKT[:, 0:32], 0.0, ALU.mult, ALU.add))
            self.op("V", [("CUMK",)], [("CUMKB",)],
                    lambda e: e.tensor_copy(self.CUMKB[:, 0:33],
                                            self.CUMK[:, 0:33]))
            self.mm([("CSVB",), ("LTX",)], [("pf", 5)],
                    [(pf[5][0:32, 0:128], [(self.LTX, self.CSVB)])])
            self.op("V", [("pf", 5)], [("CUMVB",)],
                    lambda e: e.tensor_copy(self.CUMVB, pf[5][0:32, 0:128]))
            self.dma("SY", [("CUMVB",)], [("cvd",)],
                     [(self.cvd.ap().rearrange("o (c f) -> (o c) f", c=32),
                       self.CUMVB)])
            cv2_src = AP(self.cvd, 0, [[128, 2], [256, 16], [1, 128]])
            self.dma("SY", [("cvd",)], [("CV2",), ("SCB2",)],
                     [(self.CV2.rearrange("p (t f) -> p t f", t=16), cv2_src)])
            for h2 in range(2):
                r0 = 64 * h2
                for half in range(2):
                    grp = []
                    for cc in range(8):
                        c = half * 16 + cc
                        grp.append((pf[h2][0:1, cc * 64:cc * 64 + 64],
                                    [(self.CUMKB[r0:r0 + 64, c:c + 1],
                                      self.QTb[r0:r0 + 64, c * 64:c * 64 + 64])]))
                    for cc in range(8):
                        c = half * 16 + 8 + cc
                        grp.append((pf[2 + h2][0:1, cc * 64:cc * 64 + 64],
                                    [(self.CUMKB[r0:r0 + 64, c:c + 1],
                                      self.QTb[r0:r0 + 64, c * 64:c * 64 + 64])]))
                    self.mm([("CUMKB",), ("QTb", half * 2), ("QTb", half * 2 + 1)],
                            [("pf", h2), ("pf", 2 + h2)], grp)
                    self.op("S", [("pf", h2), ("CONSTC",)],
                            [("CRSB", h2, half, 0)],
                            lambda e, h2=h2, half=half: e.activation(
                                self.CRSB[:, h2 * 2048 + half * 1024:
                                          h2 * 2048 + half * 1024 + 512],
                                pf[h2][0:1, :], AF.Sigmoid,
                                scale=self.SUMS[0:1, 11:12]))
                    self.op("S", [("pf", 2 + h2), ("CONSTC",)],
                            [("CRSB", h2, half, 1)],
                            lambda e, h2=h2, half=half: e.activation(
                                self.CRSB[:, h2 * 2048 + half * 1024 + 512:
                                          h2 * 2048 + half * 1024 + 1024],
                                pf[2 + h2][0:1, :], AF.Sigmoid,
                                scale=self.SUMS[0:1, 11:12]))
            for cg in range(4):
                grp = []
                for cc in range(8):
                    c = cg * 8 + cc
                    for h2 in range(2):
                        r0 = 64 * h2
                        grp.append((pf[2][r0:r0 + 64, cc * 64:cc * 64 + 64],
                                    [(self.QTb[r0:r0 + 64, c * 64:c * 64 + 64],
                                      self.KTb[r0:r0 + 64, c * 64:c * 64 + 64])]))
                self.mm([("QTb", cg), ("KTb", cg)], [("pf", 2)], grp)
                self.op("V", [("pf", 2), ("MASKB",)], [("SLb", cg)],
                        lambda e, cg=cg: e.tensor_tensor(
                            self.SLb[:, cg * 512:cg * 512 + 512],
                            pf[2][:, :], self.MASKB, ALU.add))
            self.op("S", [("SLb", cg) for cg in range(4)], [("PLball",)],
                    lambda e: e.activation(self.PLb, self.SLb, AF.Exp))
            self.op("V", [("PLball",)], [("SMS",)],
                    lambda e: e.reduce_sum(
                        self.SMS[:, 0:32],
                        self.PLb.rearrange("p (c k) -> p c k", c=32),
                        axis=AX.X))
            self.op("V", [("SMS",)], [("SMSr",)],
                    lambda e: e.reciprocal(self.SMS[:, 0:32], self.SMS[:, 0:32]))
            self.op("V", [("SMSr",), ("PLball",)], [("PLball",)],
                    lambda e: e.tensor_tensor(
                        self.PLb.rearrange("p (c k) -> p c k", c=32),
                        self.PLb.rearrange("p (c k) -> p c k", c=32),
                        self.SMS[:, 0:32].unsqueeze(2)
                        .broadcast_to((128, 32, 64)), ALU.mult))
            self.dma("SY", [("CRSB", h2_, half_, i_) for h2_ in range(2)
                            for half_ in range(2) for i_ in range(2)],
                     [("crsd",)],
                     [(self.crsd.ap(), self.WARt[0:1, 8192:12288])])
            for h2 in range(2):
                r0 = 64 * h2
                crs_src = AP(self.crsd, h2 * 2048,
                             [[128, 16], [1, 64]])
                crs_src2 = AP(self.crsd, h2 * 2048 + 64,
                              [[128, 16], [1, 64]])
                self.dma("SY", [("crsd",)], [("CRS2",), ("SCB2",)], [
                    (self.CRS2[0:1, :].rearrange("p (t f) -> p t f", t=16)
                     [:, :, 0:64], crs_src),
                    (self.CRS2[1:2, :].rearrange("p (t f) -> p t f", t=16)
                     [:, :, 64:128], crs_src2),
                ])
                for ti in range(NT):
                    ca, cb = 2 * ti, 2 * ti + 1
                    ptt = self.PTT if ti % 2 == 0 else self.PTT2
                    pk = ("PTT", ti % 2)
                    cpo = (ti % 2) * 256
                    self.op("T", [("PLball",), ("IDN",)], [("pb", 1)],
                            lambda e, ca=ca, r0=r0, cpo=cpo: e.transpose(
                                pbp[1][0:64, cpo:cpo + 64],
                                self.PLb[r0:r0 + 64, ca * 64:ca * 64 + 64],
                                self.IDN[r0:r0 + 64, r0:r0 + 64]))
                    self.op("T", [("PLball",), ("IDN",)], [("pb", 1)],
                            lambda e, cb=cb, r0=r0, cpo=cpo: e.transpose(
                                pbp[1][64:128, cpo + 64:cpo + 128],
                                self.PLb[r0:r0 + 64, cb * 64:cb * 64 + 64],
                                self.IDN[r0:r0 + 64, r0:r0 + 64]))
                    self.op("V", [("pb", 1)], [pk, ("PTTa", ti % 2)],
                            lambda e, ptt=ptt, cpo=cpo: e.tensor_copy(
                                ptt[0:64, 0:64], pbp[1][0:64, cpo:cpo + 64]))
                    self.op("V", [("pb", 1)], [pk, ("PTTb", ti % 2)],
                            lambda e, ptt=ptt, cpo=cpo: e.tensor_copy(
                                ptt[64:128, 64:128],
                                pbp[1][64:128, cpo + 64:cpo + 128]))
                    po = (ti % 8) * 64
                    self.mm([pk, ("VGb", ti), ("CRS2",),
                             ("CV2",)],
                            [("pf", 3)],
                            [(pf[3][:, po:po + 64],
                              [(ptt, self.VGb[:, ti, r0:r0 + 64]),
                               (self.CRS2[:, ti * 128:ti * 128 + 128],
                                self.CV2[:, ti * 128 + r0:ti * 128 + r0 + 64]
                                if False else
                                self.CV2.rearrange("p (t f) -> p t f", t=16)
                                [:, ti, r0:r0 + 64])])])
                    if ti % 8 == 7:
                        for tj in range(ti - 7, ti + 1):
                            self.op("V", [("pf", 3)], [("B", tj)],
                                    lambda e, tj=tj, hg=hg, r0=r0: e.tensor_copy(
                                        OUTB[:, tj, hg * 128 + r0:
                                             hg * 128 + r0 + 64],
                                        pf[3][:, (tj % 8) * 64:(tj % 8) * 64 + 64]))

        self.dma("G", [], [("WSL", 0), ("WSL", 1)], [
            (WS[:, 0, :].rearrange("p (c n) -> p c n", n=1024),
             d["W_out"].rearrange("(c p) n -> p c n", p=128)[:, 0:4, :]),
            (WS[:, 1, :].rearrange("p (c n) -> p c n", n=1024),
             d["W_out"].rearrange("(c p) n -> p c n", p=128)[:, 4:8, :]),
        ])
        wouts = [WS[:, 0, :].rearrange("p (c n) -> p c n", n=1024),
                 WS[:, 1, :].rearrange("p (c n) -> p c n", n=1024)]
        for ti in range(NT):
            self.op("V", [("B", ti)], [("SCB",)],
                    lambda e, ti=ti: e.tensor_copy(self.SCB[:],
                                                   OUTB[:, ti, :]))
            for dc in range(NDC):
                self.op("T", [("SCB",), ("IDN",)], [("pb", 0)],
                        lambda e, dc=dc: e.transpose(
                            self.pb[0][:, dc * 128:dc * 128 + 128],
                            self.SCB[:, dc * 128:dc * 128 + 128], self.IDN[:]))
            self.op("V", [("pb", 0)], [("OBT",), ("SCB",)],
                    lambda e: e.tensor_copy(self.SCB[:], self.pb[0][:, :]))
            obt = self.SCB[:].rearrange("p (c n) -> p c n", c=8)
            for dch in range(2):
                self.mm([("OBT",), ("SCB",), ("WSL", 0), ("WSL", 1)],
                        [("pf", dch)],
                        [(pf[dch][:, :],
                          [(obt[:, dc, :],
                            wouts[dc // 4][:, dc % 4, dch * 512:dch * 512 + 512])
                           for dc in range(NDC)])])
                cs = slice(dch * 512, dch * 512 + 512)
                self.op("V", [("pf", dch), ("X", ti)], [("X", ti)],
                        lambda e, ti=ti, cs=cs, dch=dch: e.tensor_tensor(
                            X[:, ti, cs], X[:, ti, cs], pf[dch][:, :],
                            ALU.add))


def build_constants():
    eye = np.eye(128, dtype=np.float32).astype(BF)
    onesr = np.ones((1, 128), np.float32).astype(BF)
    i = np.arange(128)
    m_cur = np.where(i[:, None] >= i[None, :], 0.0, -1e9).astype(np.float32)
    m_prev = np.where(i[None, :] > i[:, None], 0.0, -1e9).astype(np.float32)
    maska = np.concatenate([m_prev, m_cur], axis=1)
    q = np.arange(64)
    mb = np.where(q[None, :] > q[:, None], -65000.0, 0.0).astype(np.float32)
    mb2 = np.concatenate([mb, mb], axis=0)
    maskb = np.tile(mb2, (1, 8))
    iot8 = np.tile(np.arange(8, dtype=np.float32)[None, :], (128, 16))
    iot16 = np.tile(np.arange(16, dtype=np.float32)[None, :], (128, 16))
    segc = np.zeros((128, 512), np.float32)
    for ti in range(16):
        for p in range(128):
            segc[p, ti * 32 + 2 * ti + p // 64] = 1.0 / 64.0
    ltx = np.tril(np.ones((32, 32), np.float32), k=-1).T.copy()
    ltx = np.triu(np.ones((32, 32), np.float32), k=1) * 0.5
    onh = np.concatenate([
        np.concatenate([np.ones((1, 64)), np.zeros((1, 64))], axis=1),
        np.concatenate([np.zeros((1, 64)), np.ones((1, 64))], axis=1),
    ], axis=1).astype(np.float32)
    return {
        "IDN": eye, "ONESR": onesr, "MASKA": maska, "MASKB": maskb,
        "IOT8": iot8, "IOT16": iot16,
        "SEGC": segc.astype(BF), "LTX": ltx.astype(BF), "ONH": onh.astype(BF),
    }


def prep_inputs(inputs):
    g = lambda k: np.asarray(inputs[k])
    p = "L0_"
    cm = {}
    for nm in ["WQ", "WK", "WV", "WO", "Wg", "W1", "W2", "Bw", "Cw"]:
        cm[nm] = g(p + "attn_a_" + nm)
    cm["A"] = g(p + "attn_a_A").reshape(64, 1)
    lnp = np.zeros((128, 64), np.float32)
    for i, nm in enumerate(["attn_a_ln1_g", "attn_a_ln1_b", "attn_a_ln2_g",
                            "attn_a_ln2_b", "ffn_ln1_w", "ffn_ln1_b",
                            "ffn_ln2_w", "ffn_ln2_b"]):
        lnp[:, 8 * i:8 * i + 8] = g(p + nm).reshape(8, 128).T
    cm["LNPa"] = lnp
    cm["bQT"] = g(p + "attn_a_bQ").reshape(8, 128).T.copy()
    cm["bKT"] = g(p + "attn_a_bK").reshape(8, 128).T.copy()
    cm["bV"] = g(p + "attn_a_bV").reshape(1, D)
    cm["bO"] = g(p + "attn_a_bO").reshape(1, D)
    cm["bg"] = g(p + "attn_a_bg").reshape(1, D)
    cm["b1"] = g(p + "attn_a_b1").reshape(1, 4 * D)
    cm["b2"] = g(p + "attn_a_b2").reshape(1, D)
    cm["W_qkv"] = g(p + "attn_b_W_qkv")
    cm["W_out"] = g(p + "attn_b_W_out")
    cm["agate_w"] = g(p + "ffn_attn_gate_w")
    cm["q_w"] = g(p + "ffn_q_w")
    cm["qbT"] = g(p + "ffn_q_b").T.copy()
    cm["k_w"] = g(p + "ffn_k_w")
    cm["k_b"] = g(p + "ffn_k_b").reshape(128, 1)
    cm["v_w"] = g(p + "ffn_v_w")
    cm["v_b"] = g(p + "ffn_v_b").reshape(1, 128)
    cm["o_w"] = g(p + "ffn_o_w")
    cm["o_b"] = g(p + "ffn_o_b")
    cm["relT"] = g(p + "ffn_rel_table").T.copy()
    cm["fgate_w"] = g(p + "ffn_ff_gate_w")
    cm["ff_w1"] = g(p + "ffn_ff_w1")
    cm["ff_b1"] = g(p + "ffn_ff_b1")
    cm["ff_w2"] = g(p + "ffn_ff_w2")
    cm["ff_b2"] = g(p + "ffn_ff_b2")
    cm = {k: np.ascontiguousarray(v, dtype=np.float32) for k, v in cm.items()}
    cm.update(build_constants())
    x = np.asarray(inputs["L0_attn_a_x"])
    in_maps = []
    for c in range(NCORES):
        m = dict(cm)
        m["x"] = np.ascontiguousarray(x[c * NB:(c + 1) * NB])
        in_maps.append(m)
    return in_maps


def kernel(**inputs):
    in_maps = prep_inputs(inputs)
    bld = Builder()
    nc = bld.build()
    res = run_bass_kernel_spmd(nc, in_maps, core_ids=list(range(NCORES)))
    out = np.empty((32, T, D), np.float32)
    for c in range(NCORES):
        out[c * NB:(c + 1) * NB] = res.results[c]["y"]
    return out
